# version 10
# baseline (speedup 1.0000x reference)
import numpy as np
from contextlib import ExitStack

import concourse.bass as bass
import concourse.mybir as mybir
from concourse.bass_utils import run_bass_kernel_spmd

F32 = mybir.dt.float32
F32R = mybir.dt.float32r
BF16 = mybir.dt.bfloat16
I32 = mybir.dt.int32
AF = mybir.ActivationFunctionType
OP = mybir.AluOpType
AX = mybir.AxisListType

LN1e4 = float(np.log(10000.0))
PI = float(np.pi)

NT = 2048
S = 4096
DM = 1024
FF = 4096
NPAIR = 8


class Ch:
    __slots__ = ("sem", "n", "name")

    def __init__(self, sem, name=""):
        self.sem = sem
        self.n = 0
        self.name = name


class Prog:
    ENGINES = ("sync", "gpsimd", "tensor", "scalar", "vector")
    SELF_WAIT = ("gpsimd", "scalar", "vector")

    def __init__(self, chans):
        self.steps = []
        self.chans = chans

    def add(self, engine, fn, waits=(), incs=()):
        waits = [(c, c.n if v is None else v) for (c, v) in waits]
        selfch = self.chans.get(engine)
        snap = selfch.n if selfch is not None else 0
        st = (engine, fn, waits, [], snap)
        self.steps.append(st)
        for (c, k) in incs:
            c.n += k
            st[3].append((c, k))
        return st

    def barrier(self, engines=("sync", "gpsimd", "tensor", "scalar", "vector")):
        snap = [(c, c.n) for c in set(self.chans.values()) if c is not None]
        for eng in engines:
            self.add(eng, lambda e: None, waits=list(snap))

    def emit(self, block):
        for engine in self.ENGINES:
            mysteps = [s for s in self.steps if s[0] == engine]
            if not mysteps:
                continue
            selfch = self.chans.get(engine)

            def section(e, mysteps=mysteps, engine=engine, selfch=selfch):
                done = {}
                for (_, fn, waits, incs, snap) in mysteps:
                    allw = list(waits)
                    if engine in self.SELF_WAIT and selfch is not None and snap > 0:
                        allw.append((selfch, snap))
                    for (c, v) in allw:
                        if v <= 0:
                            continue
                        if done.get(id(c), 0) >= v:
                            continue
                        e.wait_ge(c.sem, v)
                        done[id(c)] = v
                    inst = fn(e)
                    if inst is not None:
                        for (c, k) in incs:
                            inst.then_inc(c.sem, k)

            getattr(block, engine)(section)


def build_nc():
    nc = bass.Bass("TRN2", target_bir_lowering=False)
    st = ExitStack()

    d_xa = nc.dram_tensor("xa", [NT, DM], F32, kind="ExternalInput").ap()
    d_xb = nc.dram_tensor("xb", [NT, DM], F32, kind="ExternalInput").ap()
    d_wqkv = nc.dram_tensor("wqkv", [DM, 3 * DM], F32, kind="ExternalInput").ap()
    d_bqkv = nc.dram_tensor("bqkv", [24, 128], F32, kind="ExternalInput").ap()
    d_wout = nc.dram_tensor("wout", [DM, DM], F32, kind="ExternalInput").ap()
    d_bout = nc.dram_tensor("bout", [8, 128], F32, kind="ExternalInput").ap()
    d_w1 = nc.dram_tensor("w1", [DM, FF], F32, kind="ExternalInput").ap()
    d_b1 = nc.dram_tensor("b1", [32, 128], F32, kind="ExternalInput").ap()
    d_w2 = nc.dram_tensor("w2", [FF, DM], F32, kind="ExternalInput").ap()
    d_b2 = nc.dram_tensor("b2", [8, 128], F32, kind="ExternalInput").ap()
    d_poff = nc.dram_tensor("poff", [128, 2], F32, kind="ExternalInput").ap()
    d_out = nc.dram_tensor("out", [NT, DM], F32, kind="ExternalOutput").ap()
    d_x2 = nc.dram_tensor("x2scr", [NT, DM], F32, kind="ExternalOutput").ap()
    d_dafm = nc.dram_tensor("dbg_afm", [128, 16384], BF16, kind="ExternalOutput").ap()
    d_doat = nc.dram_tensor("dbg_oat", [128, 16384], BF16, kind="ExternalOutput").ap()
    d_dq = nc.dram_tensor("dbg_q", [128, 2048], BF16, kind="ExternalOutput").ap()
    d_dk = nc.dram_tensor("dbg_k", [128, 4096], BF16, kind="ExternalOutput").ap()
    d_dpt = nc.dram_tensor("dbg_pt", [128, 2048], BF16, kind="ExternalOutput").ap()
    d_dnum = nc.dram_tensor("dbg_num", [128, 2048], BF16, kind="ExternalOutput").ap()
    d_dden = nc.dram_tensor("dbg_den", [2, 2048], BF16, kind="ExternalOutput").ap()
    d_dc = nc.dram_tensor("dbg_c", [128, 1], F32, kind="ExternalOutput").ap()
    d_dv = nc.dram_tensor("dbg_v", [128, 2080], BF16, kind="ExternalOutput").ap()
    d_hq = nc.dram_tensor("hqscr", [32, 128, NT], BF16, kind="ExternalOutput").ap()

    chans = {}
    for name in ("dmw", "dmx", "dmb", "dmp", "dmm", "dmo", "pe", "act", "dve", "pool"):
        sem = st.enter_context(nc.semaphore(name))
        chans[name] = Ch(sem, name)
    (dmw, dmx, dmb, dmp, dmm, dmo, pe, act, dve, pool) = (
        chans[k] for k in ("dmw", "dmx", "dmb", "dmp", "dmm", "dmo", "pe", "act", "dve", "pool"))
    p = Prog({"vector": dve, "scalar": act, "gpsimd": pool, "tensor": pe, "sync": None})
    p.all_chans = list(chans.values())

    def sb(name, shape, dt):
        return st.enter_context(nc.sbuf_tensor(name, shape, dt)).ap()

    cst = sb("cst", [128, 8], F32)
    ones_bf = sb("ones_bf", [128, 1], BF16)
    onesr_bf = sb("onesr_bf", [1, 128], BF16)
    ones_f = sb("ones_f", [1, 128], F32)
    id_bf = sb("id_bf", [128, 128], BF16)
    id_f32 = sb("id_f32", [128, 128], F32)
    braw = sb("braw", [32, 128], F32)
    bq_t = sb("bq_t", [128, 24], F32)
    bo_t = sb("bo_t", [128, 8], F32)
    b1_t = sb("b1_t", [128, 32], F32)
    b2_t = sb("b2_t", [128, 8], F32)
    b1f = sb("b1f", [128, 32], F32)
    b1f4 = sb("b1f4", [128, 32], F32)
    b2f = sb("b2f", [128, 8], F32)
    poff = sb("poff_s", [128, 2], F32)
    posb = sb("posb", [128, S], BF16)
    stats = sb("stats", [128, 16], F32)
    csst = sb("csst", [1, 8], F32)
    negc_b = sb("negc_b", [128, 1], F32)
    sqm = sb("sqm", [128, 1024], BF16)

    stLN1 = ExitStack()
    stAtt = ExitStack()
    stOP = ExitStack()
    stFFN = ExitStack()

    def sbin(stk, name, shape, dt):
        return stk.enter_context(nc.sbuf_tensor(name, shape, dt)).ap()

    wsl = sb("wsl", [128, 6144], F32)
    wqkvb = sb("wqkvb", [128, 3072], BF16)
    pvars = sb("pvars", [128, 16], F32)
    pvi_t = sb("pvi_t", [128, 1], I32)

    attn_fmr = sbin(stAtt, "attn_fmr", [128, 8 * 2048], BF16)
    attn_fm = attn_fmr.rearrange("p (c t) -> p c t", c=8)
    q_pair = sbin(stAtt, "q_pair", [128, 2048], BF16)
    k_pair = sbin(stAtt, "k_pair", [128, 4096], BF16)
    v_fm = sbin(stAtt, "v_fm", [128, 4096], BF16)
    vtmA = sbin(stAtt, "vtmA", [128, 2080], BF16)
    vtmB = sbin(stAtt, "vtmB", [128, 2080], BF16)
    PTr = sbin(stAtt, "PTr", [128, 4096], BF16)
    PT0 = PTr[:, 0:2048]
    PT1 = PTr[:, 2048:4096]
    sqk_t = PTr
    attn_un = sbin(stAtt, "attn_un", [128, 2048], BF16)
    rc_t = sbin(stAtt, "rc_t", [2, 2048], BF16)
    sqq_alias = attn_un

    ln1r = sbin(stLN1, "ln1r", [128, 8 * 4096], BF16)
    ln1_fm = ln1r.rearrange("p (c t) -> p c t", c=8)
    xstage = sbin(stLN1, "xstage", [128, 4096], F32)
    ln1tmp_b = sbin(stLN1, "ln1tmp_b", [128, 1024], BF16)

    vvi_t = sbin(stLN1, "vvi_t", [128, 1024], I32)


    def lp(fn):
        def g(e):
            with nc.allow_low_precision(reason="bf16 softmax reciprocal/normalize"):
                return fn(e)
        return g

    ps_stack = ExitStack()

    def psopen(specs):
        out = []
        for (name, shape, dt) in specs:
            out.append(ps_stack.enter_context(nc.psum_tensor(name, shape, dt)).ap())
        return out

    def psclose():
        ps_stack.close()

    p.add("gpsimd", lambda e: e.memset(cst[:, 0:1], 16384.0), incs=[(pool, 1)])
    p.add("gpsimd", lambda e: e.memset(cst[:, 1:2], -64.0), incs=[(pool, 1)])
    p.add("gpsimd", lambda e: e.memset(cst[:, 2:3], -63.75), incs=[(pool, 1)])
    p.add("gpsimd", lambda e: e.memset(cst[:, 3:4], 1e-5), incs=[(pool, 1)])
    p.add("gpsimd", lambda e: e.memset(ones_bf, 1.0), incs=[(pool, 1)])
    p.add("gpsimd", lambda e: e.memset(onesr_bf, 1.0), incs=[(pool, 1)])
    p.add("gpsimd", lambda e: e.memset(ones_f, 1.0), incs=[(pool, 1)])
    p.add("gpsimd", lambda e: e.memset(id_bf, 1.0), incs=[(pool, 1)])
    p.add("gpsimd", lambda e: e.affine_select(id_bf, id_bf, [[1, 128]], OP.is_equal, 0.0,
                                              base=0, channel_multiplier=-1), incs=[(pool, 1)])
    p.add("gpsimd", lambda e: e.memset(id_f32, 1.0), incs=[(pool, 1)])
    p.add("gpsimd", lambda e: e.affine_select(id_f32, id_f32, [[1, 128]], OP.is_equal, 0.0,
                                              base=0, channel_multiplier=-1), incs=[(pool, 1)])
    p.add("gpsimd", lambda e: e.memset(vtmA, 1.0), incs=[(pool, 1)])
    p.add("gpsimd", lambda e: e.memset(vtmB, 1.0), incs=[(pool, 1)])
    g_cst = pool.n

    pbias, = psopen([("pbias", [128, 128], F32)])
    p.add("sync", lambda e: e.dma_start(out=poff, in_=d_poff), incs=[(dmp, 16)])
    w_po = dmp.n
    for (src, rows, dst) in ((d_bqkv, 24, bq_t), (d_bout, 8, bo_t), (d_b1, 32, b1_t),
                             (d_b2, 8, b2_t)):
        p.add("sync", lambda e, src=src, rows=rows: e.dma_start(out=braw[0:rows, :], in_=src),
              waits=[(pe, pe.n)], incs=[(dmb, 16)])
        w_l = dmb.n
        p.add("tensor", lambda e, rows=rows: e.transpose(pbias[:, 0:rows], braw[0:rows, :],
                                                         id_f32[0:rows, 0:rows]),
              waits=[(dmb, w_l), (pool, g_cst), (dve, None)], incs=[(pe, 1)])
        w_t = pe.n
        p.add("vector", lambda e, rows=rows, dst=dst: e.tensor_copy(dst, pbias[:, 0:rows]),
              waits=[(pe, w_t)], incs=[(dve, 1)])
    p.add("vector", lambda e: e.tensor_scalar(b1f, b1_t, 256.0, 16384.0, OP.mult, OP.add),
          incs=[(dve, 1)])
    p.add("vector", lambda e: e.tensor_scalar(b1f4, b1_t, 4.0, 256.0, OP.mult, OP.add),
          incs=[(dve, 1)])
    p.add("vector", lambda e: e.tensor_scalar(b2f, b2_t, 256.0, 16384.0, OP.mult, OP.add),
          incs=[(dve, 1)])
    d_bias = dve.n

    pv = pvars[:, 0:1]
    pvb = pvars[:, 1:2]
    m64 = pvars[:, 2:3]
    pvh = pvars[:, 3:4]
    pvi = pvi_t
    pvf = pvars[:, 5:6]
    pvd = pvars[:, 6:7]
    pvfl = pvars[:, 7:8]
    m2 = pvars[:, 8:9]
    fi = pvars[:, 9:10]
    freq = pvars[:, 10:11]
    off = pvars[:, 11:12]
    pos_i = xstage[:, 0:1024]
    argt = xstage[:, 1024:2048]
    vvt = xstage[:, 2048:3072]
    posw = xstage[:, 3072:4096]

    p.add("gpsimd", lambda e: e.iota(pv, [[0, 1]], base=0, channel_multiplier=1,
                                     allow_small_or_imprecise_dtypes=True), incs=[(pool, 1)])
    g_iota = pool.n
    p.add("vector", lambda e: e.tensor_scalar(pvb, pv, 64.0, -64.0, OP.is_ge, OP.mult),
          waits=[(pool, g_iota)], incs=[(dve, 1)])
    p.add("vector", lambda e: e.tensor_tensor(m64, pvb, pv, OP.add), incs=[(dve, 1)])
    p.add("vector", lambda e: e.tensor_scalar(pvh, m64, 0.5, None, OP.mult), incs=[(dve, 1)])
    p.add("vector", lambda e: e.tensor_copy(pvi, pvh), incs=[(dve, 1)])
    p.add("vector", lambda e: e.tensor_copy(pvf, pvi), incs=[(dve, 1)])
    p.add("vector", lambda e: e.tensor_tensor(pvd, pvh, pvf, OP.subtract), incs=[(dve, 1)])
    p.add("vector", lambda e: e.scalar_tensor_tensor(pvfl, pvd, 0.0, pvf, OP.is_lt, OP.subtract),
          incs=[(dve, 1)])
    p.add("vector", lambda e: e.scalar_tensor_tensor(m2, pvfl, 2.0, m64, OP.mult, OP.add),
          incs=[(dve, 1)])
    p.add("vector", lambda e: e.tensor_tensor(fi, m64, m2, OP.subtract), incs=[(dve, 1)])
    d_fi = dve.n
    p.add("scalar", lambda e: e.activation(freq, fi, AF.Exp, scale=-LN1e4 / 64.0),
          waits=[(dve, d_fi)], incs=[(act, 1)])
    a_fr = act.n
    p.add("vector", lambda e: e.tensor_scalar(off, m2, PI / 2.0, None, OP.mult), incs=[(dve, 1)])
    for seg in range(4):
        half, r = seg // 2, seg % 2
        p.add("gpsimd", lambda e, r=r: e.iota(pos_i, [[1, 1024]], base=r * 1024,
                                              channel_multiplier=0,
                                              allow_small_or_imprecise_dtypes=True),
              waits=[(dve, dve.n)], incs=[(pool, 1)])
        g_i = pool.n
        p.add("vector", lambda e, half=half: e.tensor_scalar(argt, pos_i, poff[:, half:half + 1],
                                                             freq, OP.add, OP.mult),
              waits=[(act, a_fr), (dmp, w_po), (pool, g_i)], incs=[(dve, 1)])
        p.add("vector", lambda e: e.tensor_scalar(argt, argt, off, None, OP.add), incs=[(dve, 1)])
        p.add("vector", lambda e: e.tensor_scalar(vvt, argt, 1.0 / (2 * PI), None, OP.mult),
              incs=[(dve, 1)])
        p.add("vector", lambda e: e.tensor_copy(vvi_t, vvt), incs=[(dve, 1)])
        p.add("vector", lambda e: e.tensor_copy(posw, vvi_t), incs=[(dve, 1)])
        p.add("vector", lambda e: e.tensor_tensor(vvt, vvt, posw, OP.subtract), incs=[(dve, 1)])
        p.add("vector", lambda e: e.scalar_tensor_tensor(posw, vvt, 0.5, posw, OP.is_ge, OP.add),
              incs=[(dve, 1)])
        p.add("vector", lambda e: e.scalar_tensor_tensor(posw, vvt, -0.5, posw, OP.is_lt, OP.subtract),
              incs=[(dve, 1)])
        p.add("vector", lambda e: e.scalar_tensor_tensor(argt, posw, 2 * PI, argt, OP.mult, OP.add),
              incs=[(dve, 1)])
        p.add("vector", lambda e: e.tensor_scalar(argt, argt, PI, -PI, OP.min, OP.max),
              incs=[(dve, 1)])
        d_th = dve.n
        p.add("scalar", lambda e, seg=seg: e.activation(posb[:, seg * 1024:(seg + 1) * 1024],
                                                        argt, AF.Sin),
              waits=[(dve, d_th)], incs=[(act, 1)])
    a_pos = act.n

    def ln_stats(src_tile, scol, waits=()):
        s_ = stats[:, scol + 2:scol + 3]
        q_ = stats[:, scol + 3:scol + 4]
        p.add("vector", lambda e: e.reduce_sum(s_, src_tile, AX.X), waits=list(waits),
              incs=[(dve, 1)])
        p.add("scalar", lambda e: e.activation(sqm, src_tile, AF.Square, accum_out=q_),
              waits=list(waits) + [(dve, dve.n)], incs=[(act, 1)])
        a_q = act.n
        p.add("vector", lambda e: e.tensor_scalar(stats[:, scol:scol + 1], s_, 1.0 / 1024.0,
                                                  None, OP.mult), incs=[(dve, 1)])
        p.add("vector", lambda e: e.tensor_scalar(q_, q_, 1.0 / 1024.0, None, OP.mult),
              waits=[(act, a_q)], incs=[(dve, 1)])
        p.add("vector", lambda e: e.tensor_tensor(s_, stats[:, scol:scol + 1],
                                                  stats[:, scol:scol + 1], OP.mult), incs=[(dve, 1)])
        p.add("vector", lambda e: e.tensor_tensor(q_, q_, s_, OP.subtract), incs=[(dve, 1)])
        d_v = dve.n
        p.add("scalar", lambda e: e.activation(s_, q_, AF.Sqrt, bias=cst[:, 3:4]),
              waits=[(dve, d_v), (pool, g_cst)], incs=[(act, 1)])
        a_s = act.n
        p.add("vector", lambda e: e.reciprocal(stats[:, scol + 1:scol + 2], s_),
              waits=[(act, a_s)], incs=[(dve, 1)])
        return dve.n

    ptr_bf, = psopen([("ptr_bf", [128, 2048], BF16)])
    for t in range(32):
        src = d_xa if t < 16 else d_xb
        tt = t % 16
        par = (t // 2) % 2
        if t % 2 == 0:
            p.add("sync", lambda e, src=src, tt=tt, par=par: e.dma_start(
                out=xstage[:, par * 2048:(par + 1) * 2048].rearrange("p (k d) -> p k d", k=2),
                in_=src.rearrange("(k p) d -> p k d", p=128)[:, tt:tt + 2, :]),
                waits=[(dve, dve.n), (act, act.n)], incs=[(dmx, 16)])
        w_g = dmx.n
        xt = xstage[:, (par * 2 + (t % 2)) * 1024:(par * 2 + (t % 2) + 1) * 1024]
        ln_stats(xt, 0, waits=[(dmx, w_g)])
        p.add("vector", lambda e, xt=xt: e.tensor_scalar(
            ln1tmp_b, xt, stats[:, 0:1], stats[:, 1:2], OP.subtract, OP.mult), incs=[(dve, 1)])
        d_ap = dve.n
        for c in range(8):
            p.add("tensor", lambda e, c=c: e.transpose(
                ptr_bf[:, c * 128:(c + 1) * 128], ln1tmp_b[:, c * 128:(c + 1) * 128], id_bf),
                waits=[(dve, d_ap), (pool, g_cst)], incs=[(pe, 1)])
        w_tr = pe.n
        p.add("vector", lambda e, t=t: e.tensor_copy(
            ln1_fm[:, :, t * 128:(t + 1) * 128],
            ptr_bf[:, 0:1024].rearrange("p (c j) -> p c j", c=8)),
            waits=[(pe, w_tr)], incs=[(dve, 1)])
    psclose()
    d_ln1 = dve.n

    PS_A, PS_B, PVT = psopen([("PS_A", [128, 1024], F32), ("PS_B", [128, 2048], F32),
                              ("PVT", [128, 2048], BF16)])
    wq_s = wsl[:, 0:1024]
    wk_s = wsl[:, 1024:2048]
    wv_s = wsl[:, 2048:3072]
    wq_s2 = wsl[:, 3072:4096]
    wk_s2 = wsl[:, 4096:5120]
    wv_s2 = wsl[:, 5120:6144]
    wq_b = wqkvb[:, 0:1024]
    wk_b = wqkvb[:, 1024:2048]
    wv_b = wqkvb[:, 2048:3072]
    sqq_t = attn_un

    def qkv_slab_dma(pair, par):
        bufs = (wq_s, wk_s, wv_s) if par == 0 else (wq_s2, wk_s2, wv_s2)
        for (j, buf) in enumerate(bufs):
            col0 = j * 1024 + pair * 128
            p.add("sync", lambda e, buf=buf, col0=col0: e.dma_start(
                out=buf.rearrange("p (k j) -> p k j", k=8),
                in_=d_wqkv[:, col0:col0 + 128].rearrange("(k p) j -> p k j", p=128)),
                waits=[(dve, dve.n)], incs=[(dmw, 16)])
        return dmw.n

    slab_ready = [0] * NPAIR
    slab_ready[0] = qkv_slab_dma(0, 0)

    for pair in range(NPAIR):
        p.barrier()
        par = pair % 2
        bufs = (wq_s, wk_s, wv_s) if par == 0 else (wq_s2, wk_s2, wv_s2)
        p.add("vector", lambda e, b=bufs: e.tensor_copy(wq_b, b[0]),
              waits=[(dmw, slab_ready[pair]), (pe, pe.n)], incs=[(dve, 1)])
        p.add("vector", lambda e, b=bufs: e.tensor_copy(wk_b, b[1]), incs=[(dve, 1)])
        p.add("vector", lambda e, b=bufs: e.tensor_copy(wv_b, b[2]), incs=[(dve, 1)])
        d_cast = dve.n
        if pair + 1 < NPAIR:
            slab_ready[pair + 1] = qkv_slab_dma(pair + 1, 1 - par)

        for qt in range(4):
            pa = qt % 2
            dst = PS_A[:, pa * 512:(pa + 1) * 512]
            for kc in range(8):
                p.add("tensor", lambda e, kc=kc, qt=qt, dst=dst: e.matmul(
                    dst, wq_b[:, kc * 128:(kc + 1) * 128],
                    ln1_fm[:, kc, qt * 512:(qt + 1) * 512],
                    start=(kc == 0), stop=(kc == 7)),
                    waits=[(dve, d_cast if kc == 0 else 0),
                           (act, act.n if kc == 0 else 0)],
                    incs=[(pe, 1)] if kc == 7 else ())
            w_mm = pe.n
            p.add("scalar", lambda e, qt=qt, dst=dst, pair=pair: e.activation(
                q_pair[:, qt * 512:(qt + 1) * 512], dst, AF.Identity,
                bias=bq_t[:, pair:pair + 1], scale=1.0),
                waits=[(pe, w_mm), (dve, d_bias)], incs=[(act, 1)])
        for which in ("k", "v"):
            wb = wk_b if which == "k" else wv_b
            bcol = 8 + pair if which == "k" else 16 + pair
            dstt = k_pair if which == "k" else v_fm
            for ktile in range(8):
                pa = ktile % 2
                dst = PS_A[:, pa * 512:(pa + 1) * 512]
                for kc in range(8):
                    p.add("tensor", lambda e, kc=kc, ktile=ktile, dst=dst, wb=wb,
                          which=which: e.matmul(
                        dst, wb[:, kc * 128:(kc + 1) * 128],
                        ln1_fm[:, kc, ktile * 512:(ktile + 1) * 512],
                        start=(kc == 0), stop=(kc == 7 and which == "v")),
                        waits=[(act, act.n if kc == 0 else 0)],
                        incs=[(pe, 1)] if (kc == 7 and which == "v") else ())
                if which == "k":
                    p.add("tensor", lambda e, ktile=ktile, dst=dst: e.matmul(
                        dst, id_bf, posb[:, ktile * 512:(ktile + 1) * 512],
                        start=False, stop=True),
                        waits=[(act, a_pos), (pool, g_cst)], incs=[(pe, 1)])
                w_mm = pe.n
                p.add("scalar", lambda e, ktile=ktile, dst=dst, dstt=dstt, bcol=bcol: e.activation(
                    dstt[:, ktile * 512:(ktile + 1) * 512], dst, AF.Identity,
                    bias=bq_t[:, bcol:bcol + 1], scale=1.0),
                    waits=[(pe, w_mm)], incs=[(act, 1)])
        a_qkv = act.n
        if pair == 0:
            p.add("sync", lambda e: e.dma_start(out=d_dq, in_=q_pair),
                  waits=[(act, a_qkv)], incs=[(dmo, 16)])
            p.add("sync", lambda e: e.dma_start(out=d_dk, in_=k_pair), incs=[(dmo, 16)])

        for rnd in range(2):
            for j in range(16):
                kc = rnd * 16 + j
                p.add("tensor", lambda e, kc=kc, j=j: e.transpose(
                    PVT[:, j * 128:(j + 1) * 128], v_fm[:, kc * 128:(kc + 1) * 128], id_bf),
                    waits=[(act, a_qkv if j == 0 else 0), (dve, dve.n if j == 0 else 0)],
                    incs=[(pe, 1)])
            w_vt = pe.n
            for (dst, lo) in ((vtmA, 0), (vtmB, 64)):
                p.add("vector", lambda e, rnd=rnd, dst=dst, lo=lo: e.tensor_copy(
                    dst.rearrange("p (c j) -> p c j", c=32)[:, rnd * 16:(rnd + 1) * 16, 0:64],
                    PVT.rearrange("p (c j) -> p c j", c=16)[:, :, lo:lo + 64]),
                    waits=[(pe, w_vt)], incs=[(dve, 1)])
        d_vtm = dve.n

        p.add("scalar", lambda e: e.activation(sqq_t, q_pair, AF.Square),
              waits=[(dve, dve.n), (dmm, dmm.n)], incs=[(act, 1)])
        p.add("scalar", lambda e: e.activation(sqk_t, k_pair, AF.Square), incs=[(act, 1)])
        a_sq = act.n
        for j in range(4):
            p.add("tensor", lambda e, j=j: e.matmul(
                PS_B[0:1, j * 512:(j + 1) * 512], ones_bf, sqq_t[:, j * 512:(j + 1) * 512],
                start=True, stop=True),
                waits=[(act, a_sq), (pool, g_cst)], incs=[(pe, 1)])
        w_cs = pe.n
        p.add("vector", lambda e: e.reduce_max(csst[:, 0:1], PS_B[0:1, 0:2048], AX.X),
              waits=[(pe, w_cs)], incs=[(dve, 1)])
        d_cs = dve.n
        for j in range(4):
            p.add("tensor", lambda e, j=j: e.matmul(
                PS_B[0:1, j * 512:(j + 1) * 512], ones_bf, sqk_t[:, j * 512:(j + 1) * 512],
                start=True, stop=True),
                waits=[(dve, d_cs)], incs=[(pe, 1)])
        w_cs = pe.n
        p.add("vector", lambda e: e.reduce_max(csst[:, 1:2], PS_B[0:1, 0:2048], AX.X),
              waits=[(pe, w_cs)], incs=[(dve, 1)])
        d_cs = dve.n
        for j in range(4):
            p.add("tensor", lambda e, j=j: e.matmul(
                PS_B[0:1, j * 512:(j + 1) * 512], ones_bf,
                sqk_t[:, 2048 + j * 512:2048 + (j + 1) * 512],
                start=True, stop=True),
                waits=[(dve, d_cs)], incs=[(pe, 1)])
        w_cs = pe.n
        p.add("vector", lambda e: e.reduce_max(csst[:, 2:3], PS_B[0:1, 0:2048], AX.X),
              waits=[(pe, w_cs)], incs=[(dve, 1)])
        p.add("vector", lambda e: e.tensor_tensor(csst[:, 1:2], csst[:, 1:2], csst[:, 2:3], OP.max),
              incs=[(dve, 1)])
        p.add("vector", lambda e: e.tensor_tensor(csst[:, 3:4], csst[:, 0:1], csst[:, 1:2], OP.mult),
              incs=[(dve, 1)])
        d_pr = dve.n
        p.add("scalar", lambda e: e.activation(csst[:, 4:5], csst[:, 3:4], AF.Sqrt, scale=1.0 / 64.0),
              waits=[(dve, d_pr)], incs=[(act, 1)])
        a_c = act.n
        p.add("vector", lambda e: e.tensor_scalar(csst[:, 5:6], csst[:, 4:5], -1.0, None, OP.mult),
              waits=[(act, a_c)], incs=[(dve, 1)])
        d_nc = dve.n
        p.add("tensor", lambda e: e.matmul(PS_B[:, 0:1], ones_f, csst[:, 5:6],
                                           start=True, stop=True),
              waits=[(dve, d_nc)], incs=[(pe, 1)])
        w_bc = pe.n
        p.add("vector", lambda e: e.tensor_copy(negc_b, PS_B[:, 0:1]),
              waits=[(pe, w_bc)], incs=[(dve, 1)])

        for head in range(2):
            p.barrier()
            base = head * 64
            vtm = vtmA if head == 0 else vtmB
            d_gate = dve.n
            for kc in range(32):
                PT = PT0 if kc % 2 == 0 else PT1
                for half in range(2):
                    for qh in range(2):
                        qt = half * 2 + qh
                        p.add("tensor", lambda e, kc=kc, qt=qt, qh=qh, base=base: e.matmul(
                            PS_A[:, qh * 512:(qh + 1) * 512],
                            k_pair[base:base + 64, kc * 128:(kc + 1) * 128],
                            q_pair[base:base + 64, qt * 512:(qt + 1) * 512],
                            start=True, stop=True),
                            waits=[(act, act.n), (dve, d_gate)],
                            incs=[(pe, 1)])
                    w_s = pe.n
                    p.add("scalar", lambda e, half=half, PT=PT: e.activation(
                        PT[:, half * 1024:(half + 1) * 1024], PS_A[:, 0:1024], AF.Exp,
                        bias=negc_b, scale=0.125),
                        waits=[(pe, w_s), (dve, d_gate)], incs=[(act, 1)])
                a_e = act.n
                if pair == 0 and head == 0 and kc == 0:
                    p.add("sync", lambda e: e.dma_start(out=d_dpt, in_=PT0),
                          waits=[(act, a_e)], incs=[(dmo, 16)])
                    p.add("sync", lambda e: e.dma_start(out=d_dc, in_=negc_b), incs=[(dmo, 16)])
                    p.add("sync", lambda e: e.dma_start(out=d_dv, in_=vtmA), incs=[(dmo, 16)])
                for qt in range(4):
                    p.add("tensor", lambda e, kc=kc, qt=qt, vtm=vtm, PT=PT: e.matmul(
                        PS_B[0:65, qt * 512:(qt + 1) * 512], vtm[:, kc * 65:kc * 65 + 65],
                        PT[:, qt * 512:(qt + 1) * 512],
                        start=(kc == 0), stop=(kc == 31)),
                        waits=[(act, a_e)],
                        incs=[(pe, 1)] if qt == 3 else ())
            w_av = pe.n
            p.add("vector", lp(lambda e: e.reciprocal(rc_t[0:1, :], PS_B[64:65, 0:2048])),
                  waits=[(pe, w_av)], incs=[(dve, 1)])
            p.add("vector", lambda e: e.tensor_copy(attn_un[0:64, :], PS_B[0:64, 0:2048]),
                  incs=[(dve, 1)])
            d_ev = dve.n
            if pair == 0 and head == 0:
                p.add("sync", lambda e: e.dma_start(out=d_dnum, in_=attn_un),
                      waits=[(dve, d_ev)], incs=[(dmo, 16)])
                p.add("sync", lambda e: e.dma_start(out=d_dden, in_=rc_t), incs=[(dmo, 16)])
            for half in range(2):
                for qh in range(2):
                    p.add("tensor", lambda e, half=half, qh=qh: e.matmul(
                        PS_A[0:64, qh * 512:(qh + 1) * 512], onesr_bf[:, 0:64],
                        rc_t[0:1, half * 1024 + qh * 512:half * 1024 + (qh + 1) * 512],
                        start=True, stop=True),
                        waits=[(dve, None), (act, act.n)], incs=[(pe, 1)])
                w_b = pe.n
                if head == 0:
                    p.add("vector", lambda e, half=half, pair=pair: e.tensor_tensor(
                        attn_fm[0:64, pair, half * 1024:(half + 1) * 1024],
                        attn_un[0:64, half * 1024:(half + 1) * 1024],
                        PS_A[0:64, 0:1024], OP.mult),
                        waits=[(pe, w_b), (dve, d_ev)], incs=[(dve, 1)])
                else:
                    p.add("vector", lambda e, half=half: e.tensor_tensor(
                        attn_un[0:64, half * 1024:(half + 1) * 1024],
                        attn_un[0:64, half * 1024:(half + 1) * 1024],
                        PS_A[0:64, 0:1024], OP.mult),
                        waits=[(pe, w_b), (dve, d_ev)], incs=[(dve, 1)])
            d_nm = dve.n
            if head == 1:
                p.add("sync", lambda e, pair=pair: e.dma_start(
                    out=attn_fm[64:128, pair, :], in_=attn_un[0:64, :]),
                    waits=[(dve, d_nm)], incs=[(dmm, 16)])
    d_attn = dve.n
    w_attnmv = dmm.n
    p.add("sync", lambda e: e.dma_start(out=d_dafm, in_=attn_fmr),
          waits=[(dve, d_attn), (dmm, w_attnmv)], incs=[(dmo, 16)])

    psclose()
    stLN1.close()

    PO, PT2 = psopen([("PO", [128, 2048], F32), ("PT2", [128, 2048], BF16)])
    outattnr = sbin(stOP, "outattnr", [128, 8 * 2048], BF16)
    outattn = outattnr.rearrange("p (c t) -> p c t", c=8)
    wo_s = wsl[:, 0:1024]
    wo_s2 = wsl[:, 1024:2048]
    wo_b = wqkvb[:, 0:1024]

    def wout_dma(dm, buf):
        p.add("sync", lambda e, dm=dm, buf=buf: e.dma_start(
            out=buf.rearrange("p (k j) -> p k j", k=8),
            in_=d_wout[:, dm * 128:(dm + 1) * 128].rearrange("(k p) j -> p k j", p=128)),
            waits=[(dve, dve.n)], incs=[(dmw, 16)])
        return dmw.n

    wo_ready = [0] * 8
    wo_ready[0] = wout_dma(0, wo_s)
    for dm in range(8):
        buf = wo_s if dm % 2 == 0 else wo_s2
        p.add("vector", lambda e, buf=buf: e.tensor_copy(wo_b, buf),
              waits=[(dmw, wo_ready[dm]), (pe, pe.n), (dve, d_attn)], incs=[(dve, 1)])
        d_c = dve.n
        if dm + 1 < 8:
            wo_ready[dm + 1] = wout_dma(dm + 1, wo_s2 if dm % 2 == 0 else wo_s)
        for qt in range(4):
            dst = PO[:, (qt % 4) * 512:(qt % 4 + 1) * 512]
            for hc in range(8):
                p.add("tensor", lambda e, hc=hc, qt=qt, dst=dst: e.matmul(
                    dst, wo_b[:, hc * 128:(hc + 1) * 128],
                    attn_fm[:, hc, qt * 512:(qt + 1) * 512],
                    start=(hc == 0), stop=(hc == 7)),
                    waits=[(dve, d_c if hc == 0 else 0),
                           (dmm, w_attnmv if hc == 0 else 0),
                           (act, act.n if hc == 0 else 0)],
                    incs=[(pe, 1)] if hc == 7 else ())
            w_mm = pe.n
            p.add("scalar", lambda e, dm=dm, qt=qt, dst=dst: e.activation(
                outattn[:, dm, qt * 512:(qt + 1) * 512], dst, AF.Identity,
                bias=bo_t[:, dm:dm + 1], scale=1.0),
                waits=[(pe, w_mm)], incs=[(act, 1)])
    a_op = act.n
    p.add("sync", lambda e: e.dma_start(out=d_doat, in_=outattnr),
          waits=[(act, a_op)], incs=[(dmo, 16)])

    at_tm = sbin(stOP, "at_tm", [128, 1024], BF16)
    x_tile = sbin(stOP, "x_tile", [128, 1024], F32)
    x2_tile = sbin(stOP, "x2_tile", [128, 1024], F32)
    for t in range(16):
        p.add("sync", lambda e, t=t: e.dma_start(
            out=x_tile, in_=d_xa.rearrange("(k p) d -> p k d", p=128)[:, t:t + 1, :]),
            waits=[(dve, dve.n)], incs=[(dmx, 16)])
        w_x = dmx.n
        for c in range(8):
            p.add("tensor", lambda e, c=c, t=t: e.transpose(
                PT2[:, c * 128:(c + 1) * 128], outattn[:, c, t * 128:(t + 1) * 128], id_bf),
                waits=[(act, a_op if c == 0 else 0), (dve, dve.n if c == 0 else 0)],
                incs=[(pe, 1)])
        w_tr = pe.n
        p.add("vector", lambda e: e.tensor_copy(at_tm.rearrange("p (c j) -> p c j", c=8),
                                                PT2[:, 0:1024].rearrange("p (c j) -> p c j", c=8)),
              waits=[(pe, w_tr)], incs=[(dve, 1)])
        ln_stats(at_tm, 0)
        p.add("vector", lambda e: e.tensor_scalar(at_tm, at_tm, stats[:, 0:1], stats[:, 1:2],
                                                  OP.subtract, OP.mult), incs=[(dve, 1)])
        p.add("vector", lambda e: e.tensor_tensor(x2_tile, x_tile, at_tm, OP.add),
              waits=[(dmx, w_x), (dmo, dmo.n)], incs=[(dve, 1)])
        d_x2c = dve.n
        p.add("sync", lambda e, t=t: e.dma_start(
            out=d_x2.rearrange("(k p) d -> p k d", p=128)[:, t:t + 1, :], in_=x2_tile),
            waits=[(dve, d_x2c)], incs=[(dmo, 16)])
    psclose()
    w_x2w = dmo.n

    stOP.close()
    stAtt.close()
    PH, PW = psopen([("PH", [128, 2048], F32), ("PW", [128, 2048], F32)])
    ln3r = sbin(stFFN, "ln3r", [128, 8 * 1024], F32R)
    ln3_fm = ln3r.rearrange("p (c t) -> p c t", c=8)
    sA = sbin(stFFN, "sA", [128, 1024], F32)
    sB = sbin(stFFN, "sB", [128, 1024], I32)
    sC = sbin(stFFN, "sC", [128, 1024], F32)
    sD = sbin(stFFN, "sD", [128, 1024], F32)
    sE = sbin(stFFN, "sE", [128, 1024], F32)
    sF = sbin(stFFN, "sF", [128, 1024], F32)
    sG = sbin(stFFN, "sG", [128, 1024], F32)
    sH = sbin(stFFN, "sH", [128, 1024], F32)
    sI = sbin(stFFN, "sI", [128, 1024], F32)
    sJ = sbin(stFFN, "sJ", [128, 1024], I32)
    sK = sbin(stFFN, "sK", [128, 1024], F32)
    sL = sbin(stFFN, "sL", [128, 1024], F32)
    sA2 = sbin(stFFN, "sA2", [128, 1024], F32)
    sB2 = sbin(stFFN, "sB2", [128, 1024], I32)
    sJ2 = sbin(stFFN, "sJ2", [128, 1024], I32)
    sC2 = sbin(stFFN, "sC2", [128, 1024], F32)
    sK2 = sbin(stFFN, "sK2", [128, 1024], F32)
    hbuf = sbin(stFFN, "hbuf", [128, 2048], BF16)
    x2s = sbin(stFFN, "x2s", [128, 1024], F32)
    fout_tm = sbin(stFFN, "fout_tm", [128, 1024], F32)
    ln3tmp = sbin(stFFN, "ln3tmp", [128, 1024], F32)
    out_f = sbin(stFFN, "out_f", [128, 4096], F32)
    w1s = wsl[:, 0:1024]
    w1s2 = wsl[:, 1024:2048]
    w1rr = sbin(stFFN, "w1rr", [128, 2048], F32R)
    w1r = w1rr[:, 0:1024]
    w1r2 = w1rr[:, 1024:2048]
    w2s = wsl[:, 2048:3072]
    w2s2 = wsl[:, 3072:4096]
    w2bf = sbin(stFFN, "w2bf", [128, 2048], BF16)
    w2b = w2bf[:, 0:1024]
    w2b2 = w2bf[:, 1024:2048]
    hq_in = sbin(stFFN, "hq_in", [128, 512], BF16)
    hq_in2 = sbin(stFFN, "hq_in2", [128, 512], BF16)

    def w1_dma(f, buf):
        p.add("sync", lambda e, f=f, buf=buf: e.dma_start(
            out=buf.rearrange("p (k j) -> p k j", k=8),
            in_=d_w1[:, f * 128:(f + 1) * 128].rearrange("(k p) j -> p k j", p=128)),
            waits=[(dve, dve.n)], incs=[(dmw, 16)])
        return dmw.n

    ev_hist = [0] * 68
    dch_hist = [0] * 68
    gy_hist = [0] * 68
    hq_hist = [0] * 68
    a_h_prev = 0
    for T in range(2):
        for j in range(8):
            t = T * 8 + j
            p.add("sync", lambda e, t=t: e.dma_start(
                out=x2s, in_=d_x2.rearrange("(k p) d -> p k d", p=128)[:, t:t + 1, :]),
                waits=[(dve, dve.n), (dmo, w_x2w)], incs=[(dmx, 16)])
            w_x = dmx.n
            ln_stats(x2s, 0, waits=[(dmx, w_x)])
            p.add("vector", lambda e: e.tensor_scalar(ln3tmp, x2s, stats[:, 0:1], stats[:, 1:2],
                                                      OP.subtract, OP.mult), incs=[(dve, 1)])
            d_ap = dve.n
            for c in range(8):
                p.add("tensor", lambda e, c=c: e.transpose(
                    PH[:, c * 128:(c + 1) * 128], ln3tmp[:, c * 128:(c + 1) * 128], id_f32),
                    waits=[(dve, d_ap if c == 0 else 0), (act, act.n if c == 0 else 0)],
                    incs=[(pe, 1)])
            w_tr = pe.n
            p.add("vector", lambda e, j=j: e.tensor_copy(
                ln3_fm[:, :, j * 128:(j + 1) * 128],
                PH[:, 0:1024].rearrange("p (c j2) -> p c j2", c=8)),
                waits=[(pe, w_tr)], incs=[(dve, 1)])
        d_ln3 = dve.n

        w1_ready = w1_dma(0, w1s)
        for f in range(32):
            gi = T * 32 + f
            par = f % 2
            ph = PH[:, par * 1024:(par + 1) * 1024]
            buf = w1s if par == 0 else w1s2
            w1rb = w1r if par == 0 else w1r2
            vA = sA if par == 0 else sA2
            vB = sB if par == 0 else sB2
            vJ = sJ if par == 0 else sJ2
            vC = sC if par == 0 else sC2
            vK = sK if par == 0 else sK2
            p.add("vector", lambda e, buf=buf, w1rb=w1rb: e.tensor_copy(w1rb, buf),
                  waits=[(dmw, w1_ready), (pe, pe.n)], incs=[(dve, 1)])
            d_c = dve.n
            if f + 1 < 32:
                w1_ready = w1_dma(f + 1, w1s2 if par == 0 else w1s)
            for kc in range(8):
                p.add("tensor", lambda e, kc=kc, ph=ph, w1rb=w1rb: e.matmul(
                    ph[:, 0:512], w1rb[:, kc * 128:(kc + 1) * 128], ln3_fm[:, kc, 0:512],
                    start=(kc == 0), stop=False),
                    waits=[(dve, d_c if kc == 0 else 0),
                           (act, ev_hist[gi] if kc == 0 else 0),
                           (dve, d_ln3 if kc == 0 else 0)])
                p.add("tensor", lambda e, kc=kc, ph=ph, w1rb=w1rb: e.matmul(
                    ph[:, 512:1024], w1rb[:, kc * 128:(kc + 1) * 128], ln3_fm[:, kc, 512:1024],
                    start=(kc == 0), stop=(kc == 7)),
                    incs=[(pe, 1)] if kc == 7 else ())
            w_mm = pe.n
            bcol = b1f[:, f:f + 1]
            bcol4 = b1f4[:, f:f + 1]
            p.add("scalar", lambda e, ph=ph, vA=vA, bcol=bcol: e.activation(
                vA, ph, AF.Identity, bias=bcol, scale=256.0),
                waits=[(pe, w_mm), (dve, d_bias), (dve, dch_hist[gi])], incs=[(act, 1)])
            p.add("scalar", lambda e, ph=ph, vB=vB, bcol=bcol: e.activation(
                vB, ph, AF.Identity, bias=bcol, scale=256.0), incs=[(act, 1)])
            p.add("scalar", lambda e, ph=ph, vJ=vJ, bcol4=bcol4: e.activation(
                vJ, ph, AF.Identity, bias=bcol4, scale=4.0), incs=[(act, 1)])
            ev_hist[gi + 2] = act.n
            a_ev = act.n
            p.add("gpsimd", lambda e, vB=vB, vC=vC: e.tensor_copy(vC, vB),
                  waits=[(act, a_ev), (dve, dch_hist[gi])], incs=[(pool, 1)])
            p.add("gpsimd", lambda e, vJ=vJ, vK=vK: e.tensor_copy(vK, vJ), incs=[(pool, 1)])
            g_f = pool.n
            p.add("vector", lambda e, vA=vA, vC=vC: e.tensor_tensor(sD, vA, vC, OP.subtract),
                  waits=[(act, a_ev), (pool, g_f), (act, a_h_prev)], incs=[(dve, 1)])
            p.add("vector", lambda e, vC=vC: e.scalar_tensor_tensor(sE, sD, 0.5, vC,
                                                                    OP.is_ge, OP.add),
                  incs=[(dve, 1)])
            p.add("vector", lambda e: e.scalar_tensor_tensor(sF, sD, -0.5, sE, OP.is_lt,
                                                             OP.subtract), incs=[(dve, 1)])
            p.add("vector", lambda e, vK=vK: e.scalar_tensor_tensor(sD, sF, -1.0 / 64.0, vK,
                                                                    OP.mult, OP.subtract),
                  incs=[(dve, 1)])
            p.add("vector", lambda e, vK=vK: e.scalar_tensor_tensor(sE, sD, 1.0, vK,
                                                                    OP.is_ge, OP.add),
                  incs=[(dve, 1)])
            p.add("vector", lambda e: e.scalar_tensor_tensor(sG, sD, 0.0, sE, OP.is_lt,
                                                             OP.subtract),
                  waits=[(pool, gy_hist[gi])], incs=[(dve, 1)])
            p.add("vector", lambda e: e.scalar_tensor_tensor(sD, sF, -1.0 / 64.0, sG,
                                                             OP.mult, OP.add), incs=[(dve, 1)])
            d_seg = dve.n
            p.add("scalar", lambda e: e.activation(sH, sG, AF.Gelu, bias=cst[:, 1:2], scale=-0.25),
                  waits=[(dve, d_seg), (pool, gy_hist[gi])], incs=[(act, 1)])
            p.add("scalar", lambda e: e.activation(sI, sG, AF.Gelu, bias=cst[:, 2:3], scale=-0.25),
                  incs=[(act, 1)])
            a_g = act.n
            p.add("gpsimd", lambda e: e.tensor_tensor(sI, sI, sH, OP.subtract),
                  waits=[(act, a_g)], incs=[(pool, 1)])
            g_e = pool.n
            p.add("vector", lambda e: e.tensor_tensor(sE, sD, sI, OP.mult),
                  waits=[(pool, g_e)], incs=[(dve, 1)])
            p.add("gpsimd", lambda e: e.tensor_tensor(sI, sE, sH, OP.add),
                  waits=[(dve, dve.n)], incs=[(pool, 1)])
            gy_hist[gi + 2] = pool.n
            g_y = pool.n
            p.add("vector", lambda e: e.tensor_scalar(sL, sI, 256.0, 16384.0, OP.mult, OP.add),
                  waits=[(pool, g_y)], incs=[(dve, 1)])
            p.add("vector", lambda e, vB=vB: e.tensor_copy(vB, sL), incs=[(dve, 1)])
            p.add("vector", lambda e, vB=vB, vC=vC: e.tensor_copy(vC, vB), incs=[(dve, 1)])
            p.add("vector", lambda e, vC=vC: e.tensor_tensor(sD, sL, vC, OP.subtract),
                  incs=[(dve, 1)])
            p.add("vector", lambda e, vC=vC: e.scalar_tensor_tensor(sE, sD, 0.5, vC,
                                                                    OP.is_ge, OP.add),
                  incs=[(dve, 1)])
            p.add("vector", lambda e: e.scalar_tensor_tensor(sE, sD, 1.5, sE, OP.is_ge, OP.add),
                  incs=[(dve, 1)])
            p.add("vector", lambda e: e.scalar_tensor_tensor(sE, sD, -0.5, sE, OP.is_lt,
                                                             OP.subtract), incs=[(dve, 1)])
            p.add("vector", lambda e: e.scalar_tensor_tensor(sE, sD, -1.5, sE, OP.is_lt, OP.add),
                  incs=[(dve, 1)])
            dch_hist[gi + 2] = dve.n
            d_n2 = dve.n
            hb = hbuf[:, par * 1024:(par + 1) * 1024]
            p.add("scalar", lambda e, hb=hb: e.activation(hb, sE, AF.Identity, bias=cst[:, 1:2],
                                                          scale=-1.0 / 256.0),
                  waits=[(dve, d_n2), (dmo, dmo.n)], incs=[(act, 1)])
            a_h_prev = act.n
            p.add("sync", lambda e, f=f, T=T, hb=hb: e.dma_start(
                out=d_hq[f, :, T * 1024:(T + 1) * 1024], in_=hb),
                waits=[(act, a_h_prev)], incs=[(dmo, 16)])
            hq_hist[gi + 2] = dmo.n
        w_hq = dmo.n

        for tb in range(2):
            t0 = T * 1024 + tb * 512
            mm_hist = [0] * 34
            for fc in range(32):
                par = fc % 2
                buf = w2s if par == 0 else w2s2
                bufb = w2b if par == 0 else w2b2
                hqb = hq_in if par == 0 else hq_in2
                p.add("sync", lambda e, fc=fc, buf=buf: e.dma_start(
                    out=buf, in_=d_w2.rearrange("(c p) d -> p c d", p=128)[:, fc:fc + 1, :]),
                    waits=[(dve, dve.n), (dmo, w_hq), (pe, mm_hist[fc])], incs=[(dmw, 16)])
                p.add("sync", lambda e, fc=fc, t0=t0, hqb=hqb: e.dma_start(
                    out=hqb, in_=d_hq[fc, :, t0:t0 + 512]),
                    incs=[(dmw, 16)])
                w_l = dmw.n
                p.add("vector", lambda e, buf=buf, bufb=bufb: e.tensor_copy(bufb, buf),
                      waits=[(dmw, w_l), (pe, pe.n)], incs=[(dve, 1)])
                d_c = dve.n
                for dm in range(8):
                    pwd = PW if dm < 4 else PH
                    p.add("tensor", lambda e, fc=fc, dm=dm, bufb=bufb, hqb=hqb, pwd=pwd: e.matmul(
                        pwd[:, (dm % 4) * 512:(dm % 4 + 1) * 512],
                        bufb[:, dm * 128:(dm + 1) * 128], hqb,
                        start=(fc == 0), stop=(fc == 31)),
                        waits=[(dve, d_c if dm == 0 else 0),
                               (act, act.n if (fc == 0 and dm == 0) else 0)],
                        incs=[(pe, 1)] if dm == 7 else ())
                mm_hist[fc + 2] = pe.n
            w_w2 = pe.n
            for dp in range(4):
                pwd = PW if dp < 2 else PH
                ps = pwd[:, (dp % 2) * 1024:(dp % 2 + 1) * 1024]
                for k in range(2):
                    dm = dp * 2 + k
                    p.add("scalar", lambda e, ps=ps, k=k, dm=dm: e.activation(
                        sA[:, k * 512:(k + 1) * 512], ps[:, k * 512:(k + 1) * 512], AF.Identity,
                        bias=b2f[:, dm:dm + 1], scale=256.0),
                        waits=[(pe, w_w2), (dve, dve.n), (pool, pool.n)], incs=[(act, 1)])
                    p.add("scalar", lambda e, ps=ps, k=k, dm=dm: e.activation(
                        sB[:, k * 512:(k + 1) * 512], ps[:, k * 512:(k + 1) * 512], AF.Identity,
                        bias=b2f[:, dm:dm + 1], scale=256.0), incs=[(act, 1)])
                a_ev = act.n
                p.add("gpsimd", lambda e: e.tensor_copy(sC, sB), waits=[(act, a_ev)],
                      incs=[(pool, 1)])
                g_f = pool.n
                p.add("vector", lambda e: e.tensor_tensor(sD, sA, sC, OP.subtract),
                      waits=[(act, a_ev), (pool, g_f)], incs=[(dve, 1)])
                p.add("vector", lambda e: e.scalar_tensor_tensor(sE, sD, 0.5, sC, OP.is_ge, OP.add),
                      incs=[(dve, 1)])
                p.add("vector", lambda e: e.scalar_tensor_tensor(sE, sD, 1.5, sE, OP.is_ge, OP.add),
                      incs=[(dve, 1)])
                p.add("vector", lambda e: e.scalar_tensor_tensor(sE, sD, 2.5, sE, OP.is_ge, OP.add),
                      incs=[(dve, 1)])
                p.add("vector", lambda e: e.scalar_tensor_tensor(sE, sD, -0.5, sE, OP.is_lt,
                                                                 OP.subtract), incs=[(dve, 1)])
                p.add("vector", lambda e: e.scalar_tensor_tensor(sE, sD, -1.5, sE, OP.is_lt,
                                                                 OP.add), incs=[(dve, 1)])
                p.add("vector", lambda e: e.scalar_tensor_tensor(sE, sD, -2.5, sE, OP.is_lt,
                                                                 OP.add), incs=[(dve, 1)])
                d_u3 = dve.n
                p.add("scalar", lambda e, dp=dp: e.activation(
                    out_f[:, dp * 1024:(dp + 1) * 1024], sE, AF.Identity,
                    bias=cst[:, 1:2], scale=-1.0 / 256.0),
                    waits=[(dve, d_u3)], incs=[(act, 1)])
            a_of = act.n
            for j in range(4):
                t = (t0 // 128) + j
                p.add("sync", lambda e, t=t: e.dma_start(
                    out=x2s, in_=d_x2.rearrange("(k p) d -> p k d", p=128)[:, t:t + 1, :]),
                    waits=[(dve, dve.n)], incs=[(dmx, 16)])
                w_x = dmx.n
                for c in range(8):
                    p.add("tensor", lambda e, c=c, j=j: e.transpose(
                        PW[:, c * 128:(c + 1) * 128],
                        out_f.rearrange("p (c j2) -> p c j2", c=8)[:, c, j * 128:(j + 1) * 128],
                        id_f32),
                        waits=[(act, a_of if c == 0 else 0), (dve, dve.n if c == 0 else 0)],
                        incs=[(pe, 1)])
                w_tr = pe.n
                p.add("vector", lambda e: e.tensor_copy(
                    fout_tm.rearrange("p (c j2) -> p c j2", c=8),
                    PW[:, 0:1024].rearrange("p (c j2) -> p c j2", c=8)),
                    waits=[(pe, w_tr), (dmo, dmo.n)], incs=[(dve, 1)])
                ln_stats(fout_tm, 0)
                p.add("vector", lambda e: e.tensor_scalar(fout_tm, fout_tm, stats[:, 0:1],
                                                          stats[:, 1:2], OP.subtract, OP.mult),
                      incs=[(dve, 1)])
                p.add("vector", lambda e: e.tensor_tensor(fout_tm, fout_tm, x2s, OP.add),
                      waits=[(dmx, w_x)], incs=[(dve, 1)])
                d_o = dve.n
                p.add("sync", lambda e, t=t: e.dma_start(
                    out=d_out.rearrange("(k p) d -> p k d", p=128)[:, t:t + 1, :], in_=fout_tm),
                    waits=[(dve, d_o)], incs=[(dmo, 16)])
    psclose()
    stFFN.close()

    p.add("sync", lambda e: None, waits=[(dmo, dmo.n)])

    with nc.Block() as block:
        p.emit(block)
    st.close()
    return nc


_NC_CACHE = None
_LAST_RES = None


def kernel(L0_attn_x, L0_attn_W_qkv, L0_attn_b_qkv, L0_attn_W_out, L0_attn_b_out,
           L0_ffn_w1, L0_ffn_b1, L0_ffn_w2, L0_ffn_b2):
    global _NC_CACHE
    x = np.ascontiguousarray(L0_attn_x, dtype=np.float32)
    B, SQ, D = x.shape
    if _NC_CACHE is None:
        _NC_CACHE = build_nc()
    nc = _NC_CACHE

    wqkv = np.ascontiguousarray(L0_attn_W_qkv, np.float32)
    bqkv = np.ascontiguousarray(L0_attn_b_qkv, np.float32).reshape(24, 128)
    wout = np.ascontiguousarray(L0_attn_W_out, np.float32)
    bout = np.ascontiguousarray(L0_attn_b_out, np.float32).reshape(8, 128)
    w1 = np.ascontiguousarray(L0_ffn_w1, np.float32)
    b1 = np.ascontiguousarray(L0_ffn_b1, np.float32).reshape(32, 128)
    w2 = np.ascontiguousarray(L0_ffn_w2, np.float32)
    b2 = np.ascontiguousarray(L0_ffn_b2, np.float32).reshape(8, 128)

    in_maps = []
    for core in range(8):
        b = core // 2
        half = core % 2
        own = x[b, half * 2048:(half + 1) * 2048, :]
        oth = x[b, (1 - half) * 2048:(2 - half) * 2048, :]
        po = np.empty((128, 2), np.float32)
        po[:, 0] = half * 2048.0
        po[:, 1] = (1 - half) * 2048.0
        in_maps.append({
            "xa": own, "xb": oth, "wqkv": wqkv, "bqkv": bqkv, "wout": wout,
            "bout": bout, "w1": w1, "b1": b1, "w2": w2, "b2": b2, "poff": po,
        })
    res = run_bass_kernel_spmd(nc, in_maps, core_ids=list(range(8)))
    global _LAST_RES
    _LAST_RES = res
    out = np.empty((B, SQ, D), np.float32)
    for core in range(8):
        b = core // 2
        half = core % 2
        out[b, half * 2048:(half + 1) * 2048, :] = res.results[core]["out"]
    return out


# version 12
# speedup vs baseline: 1.4311x; 1.4311x over previous
import numpy as np
from contextlib import ExitStack

import concourse.bass as bass
import concourse.mybir as mybir
from concourse.bass_utils import run_bass_kernel_spmd

F32 = mybir.dt.float32
F32R = mybir.dt.float32r
BF16 = mybir.dt.bfloat16
I32 = mybir.dt.int32
AF = mybir.ActivationFunctionType
OP = mybir.AluOpType
AX = mybir.AxisListType

LN1e4 = float(np.log(10000.0))
PI = float(np.pi)

NT = 2048
S = 4096
DM = 1024
FF = 4096
NPAIR = 8


class Ch:
    __slots__ = ("sem", "n", "name")

    def __init__(self, sem, name=""):
        self.sem = sem
        self.n = 0
        self.name = name


class Prog:
    ENGINES = ("sync", "gpsimd", "tensor", "scalar", "vector")
    SELF_WAIT = ("gpsimd", "scalar", "vector")

    def __init__(self, chans):
        self.steps = []
        self.chans = chans

    def add(self, engine, fn, waits=(), incs=()):
        waits = [(c, c.n if v is None else v) for (c, v) in waits]
        selfch = self.chans.get(engine)
        snap = selfch.n if selfch is not None else 0
        st = (engine, fn, waits, [], snap)
        self.steps.append(st)
        for (c, k) in incs:
            c.n += k
            st[3].append((c, k))
        return st

    def barrier(self, engines=("sync", "gpsimd", "tensor", "scalar", "vector")):
        snap = [(c, c.n) for c in set(self.chans.values()) if c is not None]
        for eng in engines:
            self.add(eng, lambda e: None, waits=list(snap))

    def emit(self, block):
        for engine in self.ENGINES:
            mysteps = [s for s in self.steps if s[0] == engine]
            if not mysteps:
                continue
            selfch = self.chans.get(engine)

            def section(e, mysteps=mysteps, engine=engine, selfch=selfch):
                done = {}
                for (_, fn, waits, incs, snap) in mysteps:
                    allw = list(waits)
                    if engine in self.SELF_WAIT and selfch is not None and snap > 0:
                        allw.append((selfch, snap))
                    for (c, v) in allw:
                        if v <= 0:
                            continue
                        if done.get(id(c), 0) >= v:
                            continue
                        e.wait_ge(c.sem, v)
                        done[id(c)] = v
                    inst = fn(e)
                    if inst is not None:
                        for (c, k) in incs:
                            inst.then_inc(c.sem, k)

            getattr(block, engine)(section)


def build_nc():
    nc = bass.Bass("TRN2", target_bir_lowering=False)
    st = ExitStack()

    d_xa = nc.dram_tensor("xa", [NT, DM], F32, kind="ExternalInput").ap()
    d_xb = nc.dram_tensor("xb", [NT, DM], F32, kind="ExternalInput").ap()
    d_wqkv = nc.dram_tensor("wqkv", [DM, 3 * DM], F32, kind="ExternalInput").ap()
    d_bqkv = nc.dram_tensor("bqkv", [24, 128], F32, kind="ExternalInput").ap()
    d_wout = nc.dram_tensor("wout", [DM, DM], F32, kind="ExternalInput").ap()
    d_bout = nc.dram_tensor("bout", [8, 128], F32, kind="ExternalInput").ap()
    d_w1 = nc.dram_tensor("w1", [DM, FF], F32, kind="ExternalInput").ap()
    d_b1 = nc.dram_tensor("b1", [32, 128], F32, kind="ExternalInput").ap()
    d_w2 = nc.dram_tensor("w2", [FF, DM], F32, kind="ExternalInput").ap()
    d_b2 = nc.dram_tensor("b2", [8, 128], F32, kind="ExternalInput").ap()
    d_poff = nc.dram_tensor("poff", [128, 2], F32, kind="ExternalInput").ap()
    d_out = nc.dram_tensor("out", [NT, DM], F32, kind="ExternalOutput").ap()
    d_x2 = nc.dram_tensor("x2scr", [NT, DM], F32, kind="Internal").ap()

    chans = {}
    for name in ("dmw", "dmx", "dmb", "dmp", "dmm", "dmo", "pe", "act", "dve", "pool"):
        sem = st.enter_context(nc.semaphore(name))
        chans[name] = Ch(sem, name)
    (dmw, dmx, dmb, dmp, dmm, dmo, pe, act, dve, pool) = (
        chans[k] for k in ("dmw", "dmx", "dmb", "dmp", "dmm", "dmo", "pe", "act", "dve", "pool"))
    p = Prog({"vector": dve, "scalar": act, "gpsimd": pool, "tensor": pe, "sync": None})
    p.all_chans = list(chans.values())

    def sb(name, shape, dt):
        return st.enter_context(nc.sbuf_tensor(name, shape, dt)).ap()

    cst = sb("cst", [128, 8], F32)
    ones_bf = sb("ones_bf", [128, 1], BF16)
    onesr_bf = sb("onesr_bf", [1, 128], BF16)
    ones_f = sb("ones_f", [1, 128], F32)
    id_bf = sb("id_bf", [128, 128], BF16)
    id_f32 = sb("id_f32", [128, 128], F32)
    braw = sb("braw", [32, 128], F32)
    bq_t = sb("bq_t", [128, 24], F32)
    bo_t = sb("bo_t", [128, 8], F32)
    b1_t = sb("b1_t", [128, 32], F32)
    b2_t = sb("b2_t", [128, 8], F32)
    b1s = sb("b1s", [128, 32], F32)
    b2s = sb("b2s", [128, 8], F32)
    poff = sb("poff_s", [128, 2], F32)
    stats = sb("stats", [128, 16], F32)
    csst = sb("csst", [1, 8], F32)
    negc_b = sb("negc_b", [128, 1], F32)
    sqm = sb("sqm", [128, 1024], BF16)

    stLN1 = ExitStack()
    stAtt = ExitStack()
    stOP = ExitStack()
    stFFN = ExitStack()

    def sbin(stk, name, shape, dt):
        return stk.enter_context(nc.sbuf_tensor(name, shape, dt)).ap()

    pvars = sb("pvars", [128, 16], F32)
    pvi_t = sb("pvi_t", [128, 1], I32)

    wsl = sbin(stAtt, "wsl", [128, 6144], F32)
    wqkvb = sbin(stAtt, "wqkvb", [128, 3072], BF16)
    posb = sbin(stAtt, "posb", [128, S], BF16)

    attn_fmr = sbin(stAtt, "attn_fmr", [128, 8 * 2048], BF16)
    attn_fm = attn_fmr.rearrange("p (c t) -> p c t", c=8)
    q_pair = sbin(stAtt, "q_pair", [128, 2048], BF16)
    k_pair = sbin(stAtt, "k_pair", [128, 4096], BF16)
    v_fm = sbin(stAtt, "v_fm", [128, 4096], BF16)
    vtmA = sbin(stAtt, "vtmA", [128, 2080], BF16)
    vtmB = sbin(stAtt, "vtmB", [128, 2080], BF16)
    PTr = sbin(stAtt, "PTr", [128, 4096], BF16)
    PT0 = PTr[:, 0:2048]
    PT1 = PTr[:, 2048:4096]
    sqk_t = PTr
    attn_un = sbin(stAtt, "attn_un", [128, 2048], BF16)
    rc_t = sbin(stAtt, "rc_t", [2, 2048], BF16)
    sqq_alias = attn_un

    ln1r = sbin(stLN1, "ln1r", [128, 8 * 4096], BF16)
    ln1_fm = ln1r.rearrange("p (c t) -> p c t", c=8)
    xstage = sbin(stLN1, "xstage", [128, 4096], F32)
    ln1tmp_b = sbin(stLN1, "ln1tmp_b", [128, 1024], BF16)

    vvi_t = sbin(stLN1, "vvi_t", [128, 1024], I32)


    def lp(fn):
        def g(e):
            with nc.allow_low_precision(reason="bf16 softmax reciprocal/normalize"):
                return fn(e)
        return g

    ps_stack = ExitStack()

    def psopen(specs):
        out = []
        for (name, shape, dt) in specs:
            out.append(ps_stack.enter_context(nc.psum_tensor(name, shape, dt)).ap())
        return out

    def psclose():
        ps_stack.close()

    p.add("gpsimd", lambda e: e.memset(cst[:, 0:1], 16384.0), incs=[(pool, 1)])
    p.add("gpsimd", lambda e: e.memset(cst[:, 1:2], -64.0), incs=[(pool, 1)])
    p.add("gpsimd", lambda e: e.memset(cst[:, 2:3], -63.75), incs=[(pool, 1)])
    p.add("gpsimd", lambda e: e.memset(cst[:, 3:4], 1e-5), incs=[(pool, 1)])
    p.add("gpsimd", lambda e: e.memset(cst[:, 4:5], 0.25), incs=[(pool, 1)])
    p.add("gpsimd", lambda e: e.memset(ones_bf, 1.0), incs=[(pool, 1)])
    p.add("gpsimd", lambda e: e.memset(onesr_bf, 1.0), incs=[(pool, 1)])
    p.add("gpsimd", lambda e: e.memset(ones_f, 1.0), incs=[(pool, 1)])
    p.add("gpsimd", lambda e: e.memset(id_bf, 1.0), incs=[(pool, 1)])
    p.add("gpsimd", lambda e: e.affine_select(id_bf, id_bf, [[1, 128]], OP.is_equal, 0.0,
                                              base=0, channel_multiplier=-1), incs=[(pool, 1)])
    p.add("gpsimd", lambda e: e.memset(id_f32, 1.0), incs=[(pool, 1)])
    p.add("gpsimd", lambda e: e.affine_select(id_f32, id_f32, [[1, 128]], OP.is_equal, 0.0,
                                              base=0, channel_multiplier=-1), incs=[(pool, 1)])
    p.add("gpsimd", lambda e: e.memset(vtmA, 1.0), incs=[(pool, 1)])
    p.add("gpsimd", lambda e: e.memset(vtmB, 1.0), incs=[(pool, 1)])
    g_cst = pool.n

    pbias, = psopen([("pbias", [128, 128], F32)])
    p.add("sync", lambda e: e.dma_start(out=poff, in_=d_poff), incs=[(dmp, 16)])
    w_po = dmp.n
    for (src, rows, dst) in ((d_bqkv, 24, bq_t), (d_bout, 8, bo_t), (d_b1, 32, b1_t),
                             (d_b2, 8, b2_t)):
        p.add("sync", lambda e, src=src, rows=rows: e.dma_start(out=braw[0:rows, :], in_=src),
              waits=[(pe, pe.n)], incs=[(dmb, 16)])
        w_l = dmb.n
        p.add("tensor", lambda e, rows=rows: e.transpose(pbias[:, 0:rows], braw[0:rows, :],
                                                         id_f32[0:rows, 0:rows]),
              waits=[(dmb, w_l), (pool, g_cst), (dve, None)], incs=[(pe, 1)])
        w_t = pe.n
        p.add("vector", lambda e, rows=rows, dst=dst: e.tensor_copy(dst, pbias[:, 0:rows]),
              waits=[(pe, w_t)], incs=[(dve, 1)])
    p.add("vector", lambda e: e.tensor_scalar(b1s, b1_t, 256.0, None, OP.mult),
          incs=[(dve, 1)])
    p.add("vector", lambda e: e.tensor_scalar(b2s, b2_t, 256.0, None, OP.mult),
          incs=[(dve, 1)])
    d_bias = dve.n

    pv = pvars[:, 0:1]
    pvb = pvars[:, 1:2]
    m64 = pvars[:, 2:3]
    pvh = pvars[:, 3:4]
    pvi = pvi_t
    pvf = pvars[:, 5:6]
    pvd = pvars[:, 6:7]
    pvfl = pvars[:, 7:8]
    m2 = pvars[:, 8:9]
    fi = pvars[:, 9:10]
    freq = pvars[:, 10:11]
    off = pvars[:, 11:12]
    pos_i = xstage[:, 0:1024]
    argt = xstage[:, 1024:2048]
    vvt = xstage[:, 2048:3072]
    posw = xstage[:, 3072:4096]

    p.add("gpsimd", lambda e: e.iota(pv, [[0, 1]], base=0, channel_multiplier=1,
                                     allow_small_or_imprecise_dtypes=True), incs=[(pool, 1)])
    g_iota = pool.n
    p.add("vector", lambda e: e.tensor_scalar(pvb, pv, 64.0, -64.0, OP.is_ge, OP.mult),
          waits=[(pool, g_iota)], incs=[(dve, 1)])
    p.add("vector", lambda e: e.tensor_tensor(m64, pvb, pv, OP.add), incs=[(dve, 1)])
    p.add("vector", lambda e: e.tensor_scalar(pvh, m64, 0.5, None, OP.mult), incs=[(dve, 1)])
    p.add("vector", lambda e: e.tensor_copy(pvi, pvh), incs=[(dve, 1)])
    p.add("vector", lambda e: e.tensor_copy(pvf, pvi), incs=[(dve, 1)])
    p.add("vector", lambda e: e.tensor_tensor(pvd, pvh, pvf, OP.subtract), incs=[(dve, 1)])
    p.add("vector", lambda e: e.scalar_tensor_tensor(pvfl, pvd, 0.0, pvf, OP.is_lt, OP.subtract),
          incs=[(dve, 1)])
    p.add("vector", lambda e: e.scalar_tensor_tensor(m2, pvfl, 2.0, m64, OP.mult, OP.add),
          incs=[(dve, 1)])
    p.add("vector", lambda e: e.tensor_tensor(fi, m64, m2, OP.subtract), incs=[(dve, 1)])
    d_fi = dve.n
    p.add("scalar", lambda e: e.activation(freq, fi, AF.Exp, scale=-LN1e4 / 64.0),
          waits=[(dve, d_fi)], incs=[(act, 1)])
    a_fr = act.n
    p.add("vector", lambda e: e.tensor_scalar(off, m2, PI / 2.0, None, OP.mult), incs=[(dve, 1)])
    for seg in range(4):
        half, r = seg // 2, seg % 2
        p.add("gpsimd", lambda e, r=r: e.iota(pos_i, [[1, 1024]], base=r * 1024,
                                              channel_multiplier=0,
                                              allow_small_or_imprecise_dtypes=True),
              waits=[(dve, dve.n)], incs=[(pool, 1)])
        g_i = pool.n
        p.add("vector", lambda e, half=half: e.tensor_scalar(argt, pos_i, poff[:, half:half + 1],
                                                             freq, OP.add, OP.mult),
              waits=[(act, a_fr), (dmp, w_po), (pool, g_i)], incs=[(dve, 1)])
        p.add("vector", lambda e: e.tensor_scalar(argt, argt, off, None, OP.add), incs=[(dve, 1)])
        p.add("vector", lambda e: e.tensor_scalar(vvt, argt, 1.0 / (2 * PI), None, OP.mult),
              incs=[(dve, 1)])
        p.add("vector", lambda e: e.tensor_copy(vvi_t, vvt), incs=[(dve, 1)])
        p.add("vector", lambda e: e.tensor_copy(posw, vvi_t), incs=[(dve, 1)])
        p.add("vector", lambda e: e.tensor_tensor(vvt, vvt, posw, OP.subtract), incs=[(dve, 1)])
        p.add("vector", lambda e: e.scalar_tensor_tensor(posw, vvt, 0.5, posw, OP.is_ge, OP.add),
              incs=[(dve, 1)])
        p.add("vector", lambda e: e.scalar_tensor_tensor(posw, vvt, -0.5, posw, OP.is_lt, OP.subtract),
              incs=[(dve, 1)])
        p.add("vector", lambda e: e.scalar_tensor_tensor(argt, posw, 2 * PI, argt, OP.mult, OP.add),
              incs=[(dve, 1)])
        p.add("vector", lambda e: e.tensor_scalar(argt, argt, PI, -PI, OP.min, OP.max),
              incs=[(dve, 1)])
        d_th = dve.n
        p.add("scalar", lambda e, seg=seg: e.activation(posb[:, seg * 1024:(seg + 1) * 1024],
                                                        argt, AF.Sin),
              waits=[(dve, d_th)], incs=[(act, 1)])
    a_pos = act.n

    def ln_stats(src_tile, scol, waits=()):
        s_ = stats[:, scol + 2:scol + 3]
        q_ = stats[:, scol + 3:scol + 4]
        p.add("vector", lambda e: e.reduce_sum(s_, src_tile, AX.X), waits=list(waits),
              incs=[(dve, 1)])
        p.add("scalar", lambda e: e.activation(sqm, src_tile, AF.Square, accum_out=q_),
              waits=list(waits) + [(dve, dve.n)], incs=[(act, 1)])
        a_q = act.n
        p.add("vector", lambda e: e.tensor_scalar(stats[:, scol:scol + 1], s_, 1.0 / 1024.0,
                                                  None, OP.mult), incs=[(dve, 1)])
        p.add("vector", lambda e: e.tensor_scalar(q_, q_, 1.0 / 1024.0, None, OP.mult),
              waits=[(act, a_q)], incs=[(dve, 1)])
        p.add("vector", lambda e: e.tensor_tensor(s_, stats[:, scol:scol + 1],
                                                  stats[:, scol:scol + 1], OP.mult), incs=[(dve, 1)])
        p.add("vector", lambda e: e.tensor_tensor(q_, q_, s_, OP.subtract), incs=[(dve, 1)])
        d_v = dve.n
        p.add("scalar", lambda e: e.activation(s_, q_, AF.Sqrt, bias=cst[:, 3:4]),
              waits=[(dve, d_v), (pool, g_cst)], incs=[(act, 1)])
        a_s = act.n
        p.add("vector", lambda e: e.reciprocal(stats[:, scol + 1:scol + 2], s_),
              waits=[(act, a_s)], incs=[(dve, 1)])
        return dve.n

    ptr_bf, = psopen([("ptr_bf", [128, 2048], BF16)])
    for t in range(32):
        src = d_xa if t < 16 else d_xb
        tt = t % 16
        par = (t // 2) % 2
        if t % 2 == 0:
            p.add("sync", lambda e, src=src, tt=tt, par=par: e.dma_start(
                out=xstage[:, par * 2048:(par + 1) * 2048].rearrange("p (k d) -> p k d", k=2),
                in_=src.rearrange("(k p) d -> p k d", p=128)[:, tt:tt + 2, :]),
                waits=[(dve, dve.n), (act, act.n)], incs=[(dmx, 16)])
        w_g = dmx.n
        xt = xstage[:, (par * 2 + (t % 2)) * 1024:(par * 2 + (t % 2) + 1) * 1024]
        ln_stats(xt, 0, waits=[(dmx, w_g)])
        p.add("vector", lambda e, xt=xt: e.tensor_scalar(
            ln1tmp_b, xt, stats[:, 0:1], stats[:, 1:2], OP.subtract, OP.mult), incs=[(dve, 1)])
        d_ap = dve.n
        for c in range(8):
            p.add("tensor", lambda e, c=c: e.transpose(
                ptr_bf[:, c * 128:(c + 1) * 128], ln1tmp_b[:, c * 128:(c + 1) * 128], id_bf),
                waits=[(dve, d_ap), (pool, g_cst)], incs=[(pe, 1)])
        w_tr = pe.n
        p.add("vector", lambda e, t=t: e.tensor_copy(
            ln1_fm[:, :, t * 128:(t + 1) * 128],
            ptr_bf[:, 0:1024].rearrange("p (c j) -> p c j", c=8)),
            waits=[(pe, w_tr)], incs=[(dve, 1)])
    psclose()
    d_ln1 = dve.n

    PS_A, PS_B, PVT = psopen([("PS_A", [128, 1024], F32), ("PS_B", [128, 2048], F32),
                              ("PVT", [128, 2048], BF16)])
    wq_s = wsl[:, 0:1024]
    wk_s = wsl[:, 1024:2048]
    wv_s = wsl[:, 2048:3072]
    wq_s2 = wsl[:, 3072:4096]
    wk_s2 = wsl[:, 4096:5120]
    wv_s2 = wsl[:, 5120:6144]
    wq_b = wqkvb[:, 0:1024]
    wk_b = wqkvb[:, 1024:2048]
    wv_b = wqkvb[:, 2048:3072]
    sqq_t = attn_un

    def qkv_slab_dma(pair, par):
        bufs = (wq_s, wk_s, wv_s) if par == 0 else (wq_s2, wk_s2, wv_s2)
        for (j, buf) in enumerate(bufs):
            col0 = j * 1024 + pair * 128
            p.add("sync", lambda e, buf=buf, col0=col0: e.dma_start(
                out=buf.rearrange("p (k j) -> p k j", k=8),
                in_=d_wqkv[:, col0:col0 + 128].rearrange("(k p) j -> p k j", p=128)),
                waits=[(dve, dve.n)], incs=[(dmw, 16)])
        return dmw.n

    slab_ready = [0] * NPAIR
    slab_ready[0] = qkv_slab_dma(0, 0)

    for pair in range(NPAIR):
        p.barrier()
        par = pair % 2
        bufs = (wq_s, wk_s, wv_s) if par == 0 else (wq_s2, wk_s2, wv_s2)
        p.add("vector", lambda e, b=bufs: e.tensor_copy(wq_b, b[0]),
              waits=[(dmw, slab_ready[pair]), (pe, pe.n)], incs=[(dve, 1)])
        p.add("vector", lambda e, b=bufs: e.tensor_copy(wk_b, b[1]), incs=[(dve, 1)])
        p.add("vector", lambda e, b=bufs: e.tensor_copy(wv_b, b[2]), incs=[(dve, 1)])
        d_cast = dve.n
        if pair + 1 < NPAIR:
            slab_ready[pair + 1] = qkv_slab_dma(pair + 1, 1 - par)

        for qt in range(4):
            pa = qt % 2
            dst = PS_A[:, pa * 512:(pa + 1) * 512]
            for kc in range(8):
                p.add("tensor", lambda e, kc=kc, qt=qt, dst=dst: e.matmul(
                    dst, wq_b[:, kc * 128:(kc + 1) * 128],
                    ln1_fm[:, kc, qt * 512:(qt + 1) * 512],
                    start=(kc == 0), stop=(kc == 7)),
                    waits=[(dve, d_cast if kc == 0 else 0),
                           (act, act.n if kc == 0 else 0)],
                    incs=[(pe, 1)] if kc == 7 else ())
            w_mm = pe.n
            p.add("scalar", lambda e, qt=qt, dst=dst, pair=pair: e.activation(
                q_pair[:, qt * 512:(qt + 1) * 512], dst, AF.Identity,
                bias=bq_t[:, pair:pair + 1], scale=1.0),
                waits=[(pe, w_mm), (dve, d_bias)], incs=[(act, 1)])
        for which in ("k", "v"):
            wb = wk_b if which == "k" else wv_b
            bcol = 8 + pair if which == "k" else 16 + pair
            dstt = k_pair if which == "k" else v_fm
            for ktile in range(8):
                pa = ktile % 2
                dst = PS_A[:, pa * 512:(pa + 1) * 512]
                for kc in range(8):
                    p.add("tensor", lambda e, kc=kc, ktile=ktile, dst=dst, wb=wb,
                          which=which: e.matmul(
                        dst, wb[:, kc * 128:(kc + 1) * 128],
                        ln1_fm[:, kc, ktile * 512:(ktile + 1) * 512],
                        start=(kc == 0), stop=(kc == 7 and which == "v")),
                        waits=[(act, act.n if kc == 0 else 0)],
                        incs=[(pe, 1)] if (kc == 7 and which == "v") else ())
                if which == "k":
                    p.add("tensor", lambda e, ktile=ktile, dst=dst: e.matmul(
                        dst, id_bf, posb[:, ktile * 512:(ktile + 1) * 512],
                        start=False, stop=True),
                        waits=[(act, a_pos), (pool, g_cst)], incs=[(pe, 1)])
                w_mm = pe.n
                p.add("scalar", lambda e, ktile=ktile, dst=dst, dstt=dstt, bcol=bcol: e.activation(
                    dstt[:, ktile * 512:(ktile + 1) * 512], dst, AF.Identity,
                    bias=bq_t[:, bcol:bcol + 1], scale=1.0),
                    waits=[(pe, w_mm)], incs=[(act, 1)])
        a_qkv = act.n

        for rnd in range(2):
            for j in range(16):
                kc = rnd * 16 + j
                p.add("tensor", lambda e, kc=kc, j=j: e.transpose(
                    PVT[:, j * 128:(j + 1) * 128], v_fm[:, kc * 128:(kc + 1) * 128], id_bf),
                    waits=[(act, a_qkv if j == 0 else 0), (dve, dve.n if j == 0 else 0)],
                    incs=[(pe, 1)])
            w_vt = pe.n
            for (dst, lo) in ((vtmA, 0), (vtmB, 64)):
                p.add("vector", lambda e, rnd=rnd, dst=dst, lo=lo: e.tensor_copy(
                    dst.rearrange("p (c j) -> p c j", c=32)[:, rnd * 16:(rnd + 1) * 16, 0:64],
                    PVT.rearrange("p (c j) -> p c j", c=16)[:, :, lo:lo + 64]),
                    waits=[(pe, w_vt)], incs=[(dve, 1)])
        d_vtm = dve.n

        p.add("scalar", lambda e: e.activation(sqq_t, q_pair, AF.Square),
              waits=[(dve, dve.n), (dmm, dmm.n)], incs=[(act, 1)])
        p.add("scalar", lambda e: e.activation(sqk_t, k_pair, AF.Square), incs=[(act, 1)])
        a_sq = act.n
        for j in range(4):
            p.add("tensor", lambda e, j=j: e.matmul(
                PS_B[0:1, j * 512:(j + 1) * 512], ones_bf, sqq_t[:, j * 512:(j + 1) * 512],
                start=True, stop=True),
                waits=[(act, a_sq), (pool, g_cst)], incs=[(pe, 1)])
        w_cs = pe.n
        p.add("vector", lambda e: e.reduce_max(csst[:, 0:1], PS_B[0:1, 0:2048], AX.X),
              waits=[(pe, w_cs)], incs=[(dve, 1)])
        d_cs = dve.n
        for j in range(4):
            p.add("tensor", lambda e, j=j: e.matmul(
                PS_B[0:1, j * 512:(j + 1) * 512], ones_bf, sqk_t[:, j * 512:(j + 1) * 512],
                start=True, stop=True),
                waits=[(dve, d_cs)], incs=[(pe, 1)])
        w_cs = pe.n
        p.add("vector", lambda e: e.reduce_max(csst[:, 1:2], PS_B[0:1, 0:2048], AX.X),
              waits=[(pe, w_cs)], incs=[(dve, 1)])
        d_cs = dve.n
        for j in range(4):
            p.add("tensor", lambda e, j=j: e.matmul(
                PS_B[0:1, j * 512:(j + 1) * 512], ones_bf,
                sqk_t[:, 2048 + j * 512:2048 + (j + 1) * 512],
                start=True, stop=True),
                waits=[(dve, d_cs)], incs=[(pe, 1)])
        w_cs = pe.n
        p.add("vector", lambda e: e.reduce_max(csst[:, 2:3], PS_B[0:1, 0:2048], AX.X),
              waits=[(pe, w_cs)], incs=[(dve, 1)])
        p.add("vector", lambda e: e.tensor_tensor(csst[:, 1:2], csst[:, 1:2], csst[:, 2:3], OP.max),
              incs=[(dve, 1)])
        p.add("vector", lambda e: e.tensor_tensor(csst[:, 3:4], csst[:, 0:1], csst[:, 1:2], OP.mult),
              incs=[(dve, 1)])
        d_pr = dve.n
        p.add("scalar", lambda e: e.activation(csst[:, 4:5], csst[:, 3:4], AF.Sqrt, scale=1.0 / 64.0),
              waits=[(dve, d_pr)], incs=[(act, 1)])
        a_c = act.n
        p.add("vector", lambda e: e.tensor_scalar(csst[:, 5:6], csst[:, 4:5], -1.0, None, OP.mult),
              waits=[(act, a_c)], incs=[(dve, 1)])
        d_nc = dve.n
        p.add("tensor", lambda e: e.matmul(PS_B[:, 0:1], ones_f, csst[:, 5:6],
                                           start=True, stop=True),
              waits=[(dve, d_nc)], incs=[(pe, 1)])
        w_bc = pe.n
        p.add("vector", lambda e: e.tensor_copy(negc_b, PS_B[:, 0:1]),
              waits=[(pe, w_bc)], incs=[(dve, 1)])

        for head in range(2):
            p.barrier()
            base = head * 64
            vtm = vtmA if head == 0 else vtmB
            d_gate = dve.n
            for kc in range(32):
                PT = PT0 if kc % 2 == 0 else PT1
                for half in range(2):
                    for qh in range(2):
                        qt = half * 2 + qh
                        p.add("tensor", lambda e, kc=kc, qt=qt, qh=qh, base=base: e.matmul(
                            PS_A[:, qh * 512:(qh + 1) * 512],
                            k_pair[base:base + 64, kc * 128:(kc + 1) * 128],
                            q_pair[base:base + 64, qt * 512:(qt + 1) * 512],
                            start=True, stop=True),
                            waits=[(act, act.n), (dve, d_gate)],
                            incs=[(pe, 1)])
                    w_s = pe.n
                    p.add("scalar", lambda e, half=half, PT=PT: e.activation(
                        PT[:, half * 1024:(half + 1) * 1024], PS_A[:, 0:1024], AF.Exp,
                        bias=negc_b, scale=0.125),
                        waits=[(pe, w_s), (dve, d_gate)], incs=[(act, 1)])
                a_e = act.n
                for qt in range(4):
                    p.add("tensor", lambda e, kc=kc, qt=qt, vtm=vtm, PT=PT: e.matmul(
                        PS_B[0:65, qt * 512:(qt + 1) * 512], vtm[:, kc * 65:kc * 65 + 65],
                        PT[:, qt * 512:(qt + 1) * 512],
                        start=(kc == 0), stop=(kc == 31)),
                        waits=[(act, a_e)],
                        incs=[(pe, 1)] if qt == 3 else ())
            w_av = pe.n
            p.add("vector", lp(lambda e: e.reciprocal(rc_t[0:1, :], PS_B[64:65, 0:2048])),
                  waits=[(pe, w_av)], incs=[(dve, 1)])
            p.add("vector", lambda e: e.tensor_copy(attn_un[0:64, :], PS_B[0:64, 0:2048]),
                  incs=[(dve, 1)])
            d_ev = dve.n
            for half in range(2):
                for qh in range(2):
                    p.add("tensor", lambda e, half=half, qh=qh: e.matmul(
                        PS_A[0:64, qh * 512:(qh + 1) * 512], onesr_bf[:, 0:64],
                        rc_t[0:1, half * 1024 + qh * 512:half * 1024 + (qh + 1) * 512],
                        start=True, stop=True),
                        waits=[(dve, None), (act, act.n)], incs=[(pe, 1)])
                w_b = pe.n
                if head == 0:
                    p.add("vector", lambda e, half=half, pair=pair: e.tensor_tensor(
                        attn_fm[0:64, pair, half * 1024:(half + 1) * 1024],
                        attn_un[0:64, half * 1024:(half + 1) * 1024],
                        PS_A[0:64, 0:1024], OP.mult),
                        waits=[(pe, w_b), (dve, d_ev)], incs=[(dve, 1)])
                else:
                    p.add("vector", lambda e, half=half: e.tensor_tensor(
                        attn_un[0:64, half * 1024:(half + 1) * 1024],
                        attn_un[0:64, half * 1024:(half + 1) * 1024],
                        PS_A[0:64, 0:1024], OP.mult),
                        waits=[(pe, w_b), (dve, d_ev)], incs=[(dve, 1)])
            d_nm = dve.n
            if head == 1:
                p.add("sync", lambda e, pair=pair: e.dma_start(
                    out=attn_fm[64:128, pair, :], in_=attn_un[0:64, :]),
                    waits=[(dve, d_nm)], incs=[(dmm, 16)])
    d_attn = dve.n
    w_attnmv = dmm.n

    psclose()
    stLN1.close()

    PO, PT2 = psopen([("PO", [128, 2048], F32), ("PT2", [128, 2048], BF16)])
    outattnr = sbin(stOP, "outattnr", [128, 8 * 2048], BF16)
    outattn = outattnr.rearrange("p (c t) -> p c t", c=8)
    wo_s = wsl[:, 0:1024]
    wo_s2 = wsl[:, 1024:2048]
    wo_b = wqkvb[:, 0:1024]

    def wout_dma(dm, buf):
        p.add("sync", lambda e, dm=dm, buf=buf: e.dma_start(
            out=buf.rearrange("p (k j) -> p k j", k=8),
            in_=d_wout[:, dm * 128:(dm + 1) * 128].rearrange("(k p) j -> p k j", p=128)),
            waits=[(dve, dve.n)], incs=[(dmw, 16)])
        return dmw.n

    wo_ready = [0] * 8
    wo_ready[0] = wout_dma(0, wo_s)
    for dm in range(8):
        buf = wo_s if dm % 2 == 0 else wo_s2
        p.add("vector", lambda e, buf=buf: e.tensor_copy(wo_b, buf),
              waits=[(dmw, wo_ready[dm]), (pe, pe.n), (dve, d_attn)], incs=[(dve, 1)])
        d_c = dve.n
        if dm + 1 < 8:
            wo_ready[dm + 1] = wout_dma(dm + 1, wo_s2 if dm % 2 == 0 else wo_s)
        for qt in range(4):
            dst = PO[:, (qt % 4) * 512:(qt % 4 + 1) * 512]
            for hc in range(8):
                p.add("tensor", lambda e, hc=hc, qt=qt, dst=dst: e.matmul(
                    dst, wo_b[:, hc * 128:(hc + 1) * 128],
                    attn_fm[:, hc, qt * 512:(qt + 1) * 512],
                    start=(hc == 0), stop=(hc == 7)),
                    waits=[(dve, d_c if hc == 0 else 0),
                           (dmm, w_attnmv if hc == 0 else 0),
                           (act, act.n if hc == 0 else 0)],
                    incs=[(pe, 1)] if hc == 7 else ())
            w_mm = pe.n
            p.add("scalar", lambda e, dm=dm, qt=qt, dst=dst: e.activation(
                outattn[:, dm, qt * 512:(qt + 1) * 512], dst, AF.Identity,
                bias=bo_t[:, dm:dm + 1], scale=1.0),
                waits=[(pe, w_mm)], incs=[(act, 1)])
    a_op = act.n

    at_tm = sbin(stOP, "at_tm", [128, 1024], BF16)
    x_tile = sbin(stOP, "x_tile", [128, 1024], F32)
    x2_tile = sbin(stOP, "x2_tile", [128, 1024], F32)
    x2w = [0] * 16
    for t in range(16):
        p.add("sync", lambda e, t=t: e.dma_start(
            out=x_tile, in_=d_xa.rearrange("(k p) d -> p k d", p=128)[:, t:t + 1, :]),
            waits=[(dve, dve.n)], incs=[(dmx, 16)])
        w_x = dmx.n
        for c in range(8):
            p.add("tensor", lambda e, c=c, t=t: e.transpose(
                PT2[:, c * 128:(c + 1) * 128], outattn[:, c, t * 128:(t + 1) * 128], id_bf),
                waits=[(act, a_op if c == 0 else 0), (dve, dve.n if c == 0 else 0)],
                incs=[(pe, 1)])
        w_tr = pe.n
        p.add("vector", lambda e: e.tensor_copy(at_tm.rearrange("p (c j) -> p c j", c=8),
                                                PT2[:, 0:1024].rearrange("p (c j) -> p c j", c=8)),
              waits=[(pe, w_tr)], incs=[(dve, 1)])
        ln_stats(at_tm, 0)
        p.add("vector", lambda e: e.tensor_scalar(at_tm, at_tm, stats[:, 0:1], stats[:, 1:2],
                                                  OP.subtract, OP.mult), incs=[(dve, 1)])
        p.add("vector", lambda e: e.tensor_tensor(x2_tile, x_tile, at_tm, OP.add),
              waits=[(dmx, w_x), (dmo, dmo.n)], incs=[(dve, 1)])
        d_x2c = dve.n
        p.add("sync", lambda e, t=t: e.dma_start(
            out=d_x2.rearrange("(k p) d -> p k d", p=128)[:, t:t + 1, :], in_=x2_tile),
            waits=[(dve, d_x2c)], incs=[(dmo, 16)])
        x2w[t] = dmo.n
    psclose()

    stOP.close()
    stAtt.close()
    ln3r = sbin(stFFN, "ln3r", [128, 8 * 1024], BF16)
    ln3_fm = ln3r.rearrange("p (c t) -> p c t", c=8)
    hq_r = sbin(stFFN, "hq_r", [128, 32 * 1024], BF16)
    hq_sb = hq_r.rearrange("p (f t) -> p f t", f=32)
    x2s = sbin(stFFN, "x2s", [128, 1024], F32)
    ln3tmp_b = sbin(stFFN, "ln3tmp_b", [128, 1024], BF16)
    t_hi = sbin(stFFN, "t_hi", [128, 2048], I32)
    t_u = sbin(stFFN, "t_u", [128, 2048], I32)
    t_rf = sbin(stFFN, "t_rf", [128, 2048], F32)
    t_g0 = sbin(stFFN, "t_g0", [128, 2048], F32)
    t_g1 = sbin(stFFN, "t_g1", [128, 2048], F32)
    t_dg = sbin(stFFN, "t_dg", [128, 1024], F32)
    t_tdg = sbin(stFFN, "t_tdg", [128, 1024], F32)
    t_oi = sbin(stFFN, "t_oi", [128, 1024], I32)
    w1b = sbin(stFFN, "w1b", [128, 2048], BF16)
    w2b = sbin(stFFN, "w2b", [128, 1024], BF16)
    out_f = sbin(stFFN, "out_f", [128, 8 * 1024], F32)
    out_fm = out_f.rearrange("p (c t) -> p c t", c=8)
    fout_tm = sbin(stFFN, "fout_tm", [128, 1024], F32)

    hi_done = [0, 0]
    dg_done = [0, 0]
    dgd = {-1: 0, -2: 0}
    tdgd = {-1: 0, -2: 0}
    yd = {-1: 0, -2: 0}
    yid = {-1: 0}
    g1d = {-1: 0, -2: 0}
    hbfd = {-1: 0, -2: 0}
    hbf_act = {}

    for T in range(2):
        PTl, = psopen([(f"PTl{T}", [128, 2048], BF16)])
        for j in range(8):
            t = T * 8 + j
            p.add("sync", lambda e, t=t: e.dma_start(
                out=x2s, in_=d_x2.rearrange("(k p) d -> p k d", p=128)[:, t:t + 1, :]),
                waits=[(dve, dve.n), (dmo, x2w[t])], incs=[(dmx, 16)])
            w_x = dmx.n
            ln_stats(x2s, 0, waits=[(dmx, w_x)])
            p.add("vector", lambda e: e.tensor_scalar(ln3tmp_b, x2s, stats[:, 0:1],
                                                      stats[:, 1:2], OP.subtract, OP.mult),
                  incs=[(dve, 1)])
            d_ap = dve.n
            po = (j % 2) * 1024
            for c in range(8):
                p.add("tensor", lambda e, c=c, po=po: e.transpose(
                    PTl[:, po + c * 128:po + (c + 1) * 128],
                    ln3tmp_b[:, c * 128:(c + 1) * 128], id_bf),
                    waits=[(dve, d_ap if c == 0 else 0)],
                    incs=[(pe, 1)] if c == 7 else ())
            w_tr = pe.n
            p.add("vector", lambda e, j=j, po=po: e.tensor_copy(
                ln3_fm[:, :, j * 128:(j + 1) * 128],
                PTl[:, po:po + 1024].rearrange("p (c j2) -> p c j2", c=8)),
                waits=[(pe, w_tr)], incs=[(dve, 1)])
        psclose()
        d_ln3 = dve.n

        PH, = psopen([(f"PH{T}", [128, 2048], F32)])
        w1_mm_done = [pe.n, pe.n]

        def w1_dma(f, par):
            p.add("gpsimd", lambda e, f=f, par=par: e.dma_start(
                out=w1b[:, par * 1024:(par + 1) * 1024].rearrange("p (k j) -> p k j", k=8),
                in_=d_w1[:, f * 128:(f + 1) * 128].rearrange("(k p) j -> p k j", p=128)),
                waits=[(pe, w1_mm_done[par])], incs=[(dmw, 16)])
            return dmw.n

        w1_ready = [0] * 34
        w1_ready[0] = w1_dma(0, 0)
        w1_ready[1] = w1_dma(1, 1)
        for f in range(32):
            par = f % 2
            ph = PH[:, par * 1024:(par + 1) * 1024]
            wcol = par * 1024
            hi = t_hi[:, par * 1024:(par + 1) * 1024]
            u = t_u[:, par * 1024:(par + 1) * 1024]
            rf = t_rf[:, par * 1024:(par + 1) * 1024]
            g0 = t_g0[:, par * 1024:(par + 1) * 1024]
            g1 = t_g1[:, par * 1024:(par + 1) * 1024]
            for kc in range(8):
                for h in range(2):
                    p.add("tensor", lambda e, kc=kc, h=h, ph=ph, wcol=wcol: e.matmul(
                        ph[:, h * 512:(h + 1) * 512],
                        w1b[:, wcol + kc * 128:wcol + (kc + 1) * 128],
                        ln3_fm[:, kc, h * 512:(h + 1) * 512],
                        start=(kc == 0), stop=(kc == 7)),
                        waits=[(dmw, w1_ready[f] if (kc == 0 and h == 0) else 0),
                               (dve, max(d_ln3, hi_done[par]) if (kc == 0 and h == 0) else 0)],
                        incs=[(pe, 1)] if (kc == 7 and h == 1) else ())
            w1_mm_done[par] = pe.n
            w_mm = pe.n
            if f + 2 < 32:
                w1_ready[f + 2] = w1_dma(f + 2, par)
            p.add("vector", lambda e, ph=ph, hi=hi, f=f: e.tensor_scalar(
                hi, ph, 256.0, b1s[:, f:f + 1], OP.mult, OP.add),
                waits=[(pe, w_mm), (dve, d_bias), (act, hbfd[f - 2])], incs=[(dve, 1)])
            hi_done[par] = dve.n
            p.add("vector", lambda e, u=u, hi=hi: e.tensor_scalar(
                u, hi, 6, None, OP.arith_shift_right), incs=[(dve, 1)])
            d_u = dve.n
            p.add("vector", lambda e, rf=rf, u=u, hi=hi: e.scalar_tensor_tensor(
                rf, u, -64.0, hi, OP.mult, OP.add), incs=[(dve, 1)])
            if f >= 1:
                parp = (f - 1) % 2
                rfp = t_rf[:, parp * 1024:(parp + 1) * 1024]
                g0p = t_g0[:, parp * 1024:(parp + 1) * 1024]
                hip = t_hi[:, parp * 1024:(parp + 1) * 1024]
                p.add("vector", lambda e, rfp=rfp: e.tensor_tensor(
                    t_tdg, rfp, t_dg, OP.mult),
                    waits=[(pool, dgd[f - 1])], incs=[(dve, 1)])
                tdgd[f - 1] = dve.n
                p.add("vector", lambda e, rfp=rfp, g0p=g0p: e.scalar_tensor_tensor(
                    rfp, t_tdg, 0.015625, g0p, OP.mult, OP.add), incs=[(dve, 1)])
                yd[f - 1] = dve.n
                p.add("vector", lambda e, rfp=rfp, hip=hip: e.tensor_scalar(
                    hip, rfp, 256.0, None, OP.mult), incs=[(dve, 1)])
                yid[f - 1] = dve.n
            p.add("scalar", lambda e, g0=g0, u=u: e.activation(g0, u, AF.Gelu, scale=0.25),
                  waits=[(dve, d_u), (pool, dgd[f - 2]), (dve, yd[f - 2])], incs=[(act, 1)])
            p.add("scalar", lambda e, g1=g1, u=u: e.activation(g1, u, AF.Gelu,
                                                               scale=0.25, bias=cst[:, 4:5]),
                  incs=[(act, 1)])
            g1d[f] = act.n
            if f >= 1:
                parp = (f - 1) % 2
                hip = t_hi[:, parp * 1024:(parp + 1) * 1024]
                p.add("scalar", lambda e, hip=hip, fp=f - 1: e.activation(
                    hq_sb[:, fp, :], hip, AF.Identity, scale=0.00390625),
                    waits=[(dve, yid[f - 1])], incs=[(act, 1)])
                hbfd[f - 1] = act.n
                hbf_act[(T, f - 1)] = act.n
            p.add("gpsimd", lambda e, g0=g0, g1=g1: e.tensor_tensor(t_dg, g1, g0, OP.subtract),
                  waits=[(act, g1d[f]), (dve, tdgd[f - 1])], incs=[(pool, 1)])
            dgd[f] = pool.n
        parp = 31 % 2
        rfp = t_rf[:, parp * 1024:(parp + 1) * 1024]
        g0p = t_g0[:, parp * 1024:(parp + 1) * 1024]
        hip = t_hi[:, parp * 1024:(parp + 1) * 1024]
        p.add("vector", lambda e, rfp=rfp: e.tensor_tensor(t_tdg, rfp, t_dg, OP.mult),
              waits=[(pool, dgd[31])], incs=[(dve, 1)])
        tdgd[31] = dve.n
        p.add("vector", lambda e, rfp=rfp, g0p=g0p: e.scalar_tensor_tensor(
            rfp, t_tdg, 0.015625, g0p, OP.mult, OP.add), incs=[(dve, 1)])
        yd[31] = dve.n
        p.add("vector", lambda e, rfp=rfp, hip=hip: e.tensor_scalar(
            hip, rfp, 256.0, None, OP.mult), incs=[(dve, 1)])
        yid[31] = dve.n
        p.add("scalar", lambda e, hip=hip: e.activation(
            hq_sb[:, 31, :], hip, AF.Identity, scale=0.00390625),
            waits=[(dve, yid[31])], incs=[(act, 1)])
        hbfd[31] = act.n
        hbf_act[(T, 31)] = act.n
        psclose()
        d_chain_end = max(hi_done[0], hi_done[1], yid[31])

        PW8r, = psopen([(f"PW8{T}", [128, 4096], F32)])
        PW8 = PW8r
        oi_all = 0
        outff_hist = [0] * 10
        w2_mm_done = [0] * 34
        for dmh in range(2):
            w2_mm_done[0] = w2_mm_done[1] = pe.n

            def w2_dma(fc, par, dmh=dmh):
                p.add("gpsimd", lambda e, fc=fc, par=par, dmh=dmh: e.dma_start(
                    out=w2b[:, par * 512:(par + 1) * 512],
                    in_=d_w2.rearrange("(c p) d -> p c d", p=128)[:, fc, dmh * 512:(dmh + 1) * 512]),
                    waits=[(pe, w2_mm_done[fc])], incs=[(dmw, 16)])
                return dmw.n

            w2_ready = [0] * 34
            w2_ready[0] = w2_dma(0, 0)
            w2_ready[1] = w2_dma(1, 1)
            for fc in range(32):
                par = fc % 2
                for dmc in range(4):
                    for tb in range(2):
                        bank = tb * 4 + dmc
                        p.add("tensor", lambda e, fc=fc, dmc=dmc, tb=tb, bank=bank, par=par: e.matmul(
                            PW8[:, bank * 512:(bank + 1) * 512],
                            w2b[:, par * 512 + dmc * 128:par * 512 + (dmc + 1) * 128],
                            hq_sb[:, fc, tb * 512:(tb + 1) * 512],
                            start=(fc == 0), stop=(fc == 31)),
                            waits=[(dmw, w2_ready[fc] if (dmc == 0 and tb == 0) else 0),
                                   (act, hbf_act[(T, fc)] if (dmc == 0 and tb == 0) else 0),
                                   (dve, (d_chain_end if dmh == 0 else oi_all)
                                        if (fc == 0 and dmc == 0 and tb == 0) else 0)],
                            incs=[(pe, 1)] if (dmc == 3 and tb == 1) else ())
                w2_mm_done[fc + 2] = pe.n
                if fc + 2 < 32:
                    w2_ready[fc + 2] = w2_dma(fc + 2, par)
            w_w2 = pe.n
            for k in range(8):
                tb, dmc = k // 4, k % 4
                dm = dmh * 4 + dmc
                bank = tb * 4 + dmc
                ois = t_oi[:, (k % 2) * 512:(k % 2 + 1) * 512]
                p.add("vector", lambda e, bank=bank, dm=dm, ois=ois: e.tensor_scalar(
                    ois, PW8[:, bank * 512:(bank + 1) * 512], 256.0, b2s[:, dm:dm + 1],
                    OP.mult, OP.add),
                    waits=[(pe, w_w2), (act, outff_hist[k])], incs=[(dve, 1)])
                d_oi = dve.n
                p.add("scalar", lambda e, dm=dm, tb=tb, ois=ois: e.activation(
                    out_fm[:, dm, tb * 512:(tb + 1) * 512], ois, AF.Identity,
                    scale=0.00390625),
                    waits=[(dve, d_oi)], incs=[(act, 1)])
                outff_hist[k + 2] = act.n
            oi_all = dve.n
        a_of = act.n

        for j in range(8):
            t = T * 8 + j
            p.add("sync", lambda e, t=t: e.dma_start(
                out=x2s, in_=d_x2.rearrange("(k p) d -> p k d", p=128)[:, t:t + 1, :]),
                waits=[(dve, dve.n)], incs=[(dmx, 16)])
            w_x = dmx.n
            po = (j % 2) * 1024
            for c in range(8):
                p.add("tensor", lambda e, c=c, j=j, po=po: e.transpose(
                    PW8[:, po + c * 128:po + (c + 1) * 128],
                    out_fm[:, c, j * 128:(j + 1) * 128], id_f32),
                    waits=[(act, a_of if c == 0 else 0),
                           (dve, (oi_all if j == 0 else dve.n) if c == 0 else 0)],
                    incs=[(pe, 1)] if c == 7 else ())
            w_tr = pe.n
            p.add("vector", lambda e, po=po: e.tensor_copy(
                fout_tm.rearrange("p (c j2) -> p c j2", c=8),
                PW8[:, po:po + 1024].rearrange("p (c j2) -> p c j2", c=8)),
                waits=[(pe, w_tr), (dmo, dmo.n)], incs=[(dve, 1)])
            ln_stats(fout_tm, 0)
            p.add("vector", lambda e: e.tensor_scalar(fout_tm, fout_tm, stats[:, 0:1],
                                                      stats[:, 1:2], OP.subtract, OP.mult),
                  incs=[(dve, 1)])
            p.add("vector", lambda e: e.tensor_tensor(fout_tm, fout_tm, x2s, OP.add),
                  waits=[(dmx, w_x)], incs=[(dve, 1)])
            d_o = dve.n
            p.add("sync", lambda e, t=t: e.dma_start(
                out=d_out.rearrange("(k p) d -> p k d", p=128)[:, t:t + 1, :], in_=fout_tm),
                waits=[(dve, d_o)], incs=[(dmo, 16)])
        psclose()
    stFFN.close()

    p.add("sync", lambda e: None, waits=[(dmo, dmo.n)])

    with nc.Block() as block:
        p.emit(block)
    st.close()
    return nc


_NC_CACHE = None
_LAST_RES = None


def kernel(L0_attn_x, L0_attn_W_qkv, L0_attn_b_qkv, L0_attn_W_out, L0_attn_b_out,
           L0_ffn_w1, L0_ffn_b1, L0_ffn_w2, L0_ffn_b2):
    global _NC_CACHE
    x = np.ascontiguousarray(L0_attn_x, dtype=np.float32)
    B, SQ, D = x.shape
    if _NC_CACHE is None:
        _NC_CACHE = build_nc()
    nc = _NC_CACHE

    wqkv = np.ascontiguousarray(L0_attn_W_qkv, np.float32)
    bqkv = np.ascontiguousarray(L0_attn_b_qkv, np.float32).reshape(24, 128)
    wout = np.ascontiguousarray(L0_attn_W_out, np.float32)
    bout = np.ascontiguousarray(L0_attn_b_out, np.float32).reshape(8, 128)
    w1 = np.ascontiguousarray(L0_ffn_w1, np.float32)
    b1 = np.ascontiguousarray(L0_ffn_b1, np.float32).reshape(32, 128)
    w2 = np.ascontiguousarray(L0_ffn_w2, np.float32)
    b2 = np.ascontiguousarray(L0_ffn_b2, np.float32).reshape(8, 128)

    in_maps = []
    for core in range(8):
        b = core // 2
        half = core % 2
        own = x[b, half * 2048:(half + 1) * 2048, :]
        oth = x[b, (1 - half) * 2048:(2 - half) * 2048, :]
        po = np.empty((128, 2), np.float32)
        po[:, 0] = half * 2048.0
        po[:, 1] = (1 - half) * 2048.0
        in_maps.append({
            "xa": own, "xb": oth, "wqkv": wqkv, "bqkv": bqkv, "wout": wout,
            "bout": bout, "w1": w1, "b1": b1, "w2": w2, "b2": b2, "poff": po,
        })
    res = run_bass_kernel_spmd(nc, in_maps, core_ids=list(range(8)))
    global _LAST_RES
    _LAST_RES = res
    out = np.empty((B, SQ, D), np.float32)
    for core in range(8):
        b = core // 2
        half = core % 2
        out[b, half * 2048:(half + 1) * 2048, :] = res.results[core]["out"]
    return out


# version 15
# speedup vs baseline: 1.9238x; 1.3443x over previous
import numpy as np
from contextlib import ExitStack

import concourse.bass as bass
import concourse.mybir as mybir
from concourse.bass_utils import run_bass_kernel_spmd

F32 = mybir.dt.float32
F32R = mybir.dt.float32r
BF16 = mybir.dt.bfloat16
I32 = mybir.dt.int32
AF = mybir.ActivationFunctionType
OP = mybir.AluOpType
AX = mybir.AxisListType

LN1e4 = float(np.log(10000.0))
PI = float(np.pi)

NT = 2048
S = 4096
DM = 1024
FF = 4096
NPAIR = 8


class Ch:
    __slots__ = ("sem", "n", "name")

    def __init__(self, sem, name=""):
        self.sem = sem
        self.n = 0
        self.name = name


class Prog:
    ENGINES = ("sync", "gpsimd", "tensor", "scalar", "vector")
    SELF_WAIT = ("gpsimd", "scalar", "vector")

    def __init__(self, chans):
        self.steps = []
        self.chans = chans

    def add(self, engine, fn, waits=(), incs=()):
        waits = [(c, c.n if v is None else v) for (c, v) in waits]
        selfch = self.chans.get(engine)
        snap = selfch.n if selfch is not None else 0
        st = (engine, fn, waits, [], snap)
        self.steps.append(st)
        for (c, k) in incs:
            c.n += k
            st[3].append((c, k))
        return st

    def barrier(self, engines=("sync", "gpsimd", "tensor", "scalar", "vector")):
        snap = [(c, c.n) for c in set(self.chans.values()) if c is not None]
        for eng in engines:
            self.add(eng, lambda e: None, waits=list(snap))

    def emit(self, block):
        for engine in self.ENGINES:
            mysteps = [s for s in self.steps if s[0] == engine]
            if not mysteps:
                continue
            selfch = self.chans.get(engine)

            def section(e, mysteps=mysteps, engine=engine, selfch=selfch):
                done = {}
                for (_, fn, waits, incs, snap) in mysteps:
                    allw = list(waits)
                    if engine in self.SELF_WAIT and selfch is not None and snap > 0:
                        allw.append((selfch, snap))
                    for (c, v) in allw:
                        if v <= 0:
                            continue
                        if done.get(id(c), 0) >= v:
                            continue
                        e.wait_ge(c.sem, v)
                        done[id(c)] = v
                    inst = fn(e)
                    if inst is not None:
                        for (c, k) in incs:
                            inst.then_inc(c.sem, k)

            getattr(block, engine)(section)


def build_nc():
    nc = bass.Bass("TRN2", target_bir_lowering=False)
    st = ExitStack()

    d_xa = nc.dram_tensor("xa", [NT, DM], F32, kind="ExternalInput").ap()
    d_xb = nc.dram_tensor("xb", [NT, DM], F32, kind="ExternalInput").ap()
    d_wqkv = nc.dram_tensor("wqkv", [DM, 3 * DM], F32, kind="ExternalInput").ap()
    d_bqkv = nc.dram_tensor("bqkv", [24, 128], F32, kind="ExternalInput").ap()
    d_wout = nc.dram_tensor("wout", [DM, DM], F32, kind="ExternalInput").ap()
    d_bout = nc.dram_tensor("bout", [8, 128], F32, kind="ExternalInput").ap()
    d_w1 = nc.dram_tensor("w1", [DM, FF], F32, kind="ExternalInput").ap()
    d_b1 = nc.dram_tensor("b1", [32, 128], F32, kind="ExternalInput").ap()
    d_w2 = nc.dram_tensor("w2", [FF, DM], F32, kind="ExternalInput").ap()
    d_b2 = nc.dram_tensor("b2", [8, 128], F32, kind="ExternalInput").ap()
    d_poff = nc.dram_tensor("poff", [128, 2], F32, kind="ExternalInput").ap()
    d_out = nc.dram_tensor("out", [NT, DM], F32, kind="ExternalOutput").ap()
    d_x2 = nc.dram_tensor("x2scr", [NT, DM], F32, kind="Internal").ap()

    chans = {}
    for name in ("dmw", "dmx", "dmb", "dmp", "dmm", "dmo", "pe", "act", "dve", "pool"):
        sem = st.enter_context(nc.semaphore(name))
        chans[name] = Ch(sem, name)
    (dmw, dmx, dmb, dmp, dmm, dmo, pe, act, dve, pool) = (
        chans[k] for k in ("dmw", "dmx", "dmb", "dmp", "dmm", "dmo", "pe", "act", "dve", "pool"))
    p = Prog({"vector": dve, "scalar": act, "gpsimd": pool, "tensor": pe, "sync": None})
    p.all_chans = list(chans.values())

    def sb(name, shape, dt):
        return st.enter_context(nc.sbuf_tensor(name, shape, dt)).ap()

    cst = sb("cst", [128, 8], F32)
    ones_bf = sb("ones_bf", [128, 1], BF16)
    onesr_bf = sb("onesr_bf", [1, 128], BF16)
    ones_f = sb("ones_f", [1, 128], F32)
    id_bf = sb("id_bf", [128, 128], BF16)
    id_f32 = sb("id_f32", [128, 128], F32)
    braw = sb("braw", [32, 128], F32)
    bq_t = sb("bq_t", [128, 24], F32)
    bo_t = sb("bo_t", [128, 8], F32)
    b1_t = sb("b1_t", [128, 32], F32)
    b2_t = sb("b2_t", [128, 8], F32)
    b1s = sb("b1s", [128, 32], F32)
    b2s = sb("b2s", [128, 8], F32)
    poff = sb("poff_s", [128, 2], F32)
    stats = sb("stats", [128, 16], F32)
    csst = sb("csst", [1, 8], F32)
    negc_b = sb("negc_b", [128, 1], F32)
    sqm = sb("sqm", [128, 1024], BF16)

    stLN1 = ExitStack()
    stAtt = ExitStack()
    stOP = ExitStack()
    stFFN = ExitStack()

    def sbin(stk, name, shape, dt):
        return stk.enter_context(nc.sbuf_tensor(name, shape, dt)).ap()

    pvars = sb("pvars", [128, 16], F32)
    pvi_t = sb("pvi_t", [128, 1], I32)

    wqkvb = sbin(stAtt, "wqkvb", [128, 6144], BF16)
    posb = sbin(stAtt, "posb", [128, S], BF16)

    attn_fmr = sbin(stAtt, "attn_fmr", [128, 8 * 2048], BF16)
    attn_fm = attn_fmr.rearrange("p (c t) -> p c t", c=8)
    q_pair = sbin(stAtt, "q_pair", [128, 2048], BF16)
    k_pair = sbin(stAtt, "k_pair", [128, 4096], BF16)
    v_fm = sbin(stAtt, "v_fm", [128, 4096], BF16)
    vtmA = sbin(stAtt, "vtmA", [128, 2080], BF16)
    vtmB = sbin(stAtt, "vtmB", [128, 2080], BF16)
    PTr = sbin(stAtt, "PTr", [128, 4096], BF16)
    PT0 = PTr[:, 0:2048]
    PT1 = PTr[:, 2048:4096]
    sqk_t = PTr
    attn_un = sbin(stAtt, "attn_un", [128, 2048], BF16)
    rc_t = sbin(stAtt, "rc_t", [2, 2048], BF16)
    sqq_alias = attn_un

    ln1r = sbin(stLN1, "ln1r", [128, 8 * 4096], BF16)
    ln1_fm = ln1r.rearrange("p (c t) -> p c t", c=8)
    xstage = sbin(stLN1, "xstage", [128, 4096], F32)
    ln1tmp_b = sbin(stLN1, "ln1tmp_b", [128, 1024], BF16)

    vvi_t = sbin(stLN1, "vvi_t", [128, 1024], I32)


    def lp(fn):
        def g(e):
            with nc.allow_low_precision(reason="bf16 softmax reciprocal/normalize"):
                return fn(e)
        return g

    ps_stack = ExitStack()

    def psopen(specs):
        out = []
        for (name, shape, dt) in specs:
            out.append(ps_stack.enter_context(nc.psum_tensor(name, shape, dt)).ap())
        return out

    def psclose():
        ps_stack.close()

    p.add("gpsimd", lambda e: e.memset(cst[:, 0:1], 16384.0), incs=[(pool, 1)])
    p.add("gpsimd", lambda e: e.memset(cst[:, 1:2], -64.0), incs=[(pool, 1)])
    p.add("gpsimd", lambda e: e.memset(cst[:, 2:3], -63.75), incs=[(pool, 1)])
    p.add("gpsimd", lambda e: e.memset(cst[:, 3:4], 1e-5), incs=[(pool, 1)])
    p.add("gpsimd", lambda e: e.memset(cst[:, 4:5], 0.25), incs=[(pool, 1)])
    p.add("gpsimd", lambda e: e.memset(ones_bf, 1.0), incs=[(pool, 1)])
    p.add("gpsimd", lambda e: e.memset(onesr_bf, 1.0), incs=[(pool, 1)])
    p.add("gpsimd", lambda e: e.memset(ones_f, 1.0), incs=[(pool, 1)])
    p.add("gpsimd", lambda e: e.memset(id_bf, 1.0), incs=[(pool, 1)])
    p.add("gpsimd", lambda e: e.affine_select(id_bf, id_bf, [[1, 128]], OP.is_equal, 0.0,
                                              base=0, channel_multiplier=-1), incs=[(pool, 1)])
    p.add("gpsimd", lambda e: e.memset(id_f32, 1.0), incs=[(pool, 1)])
    p.add("gpsimd", lambda e: e.affine_select(id_f32, id_f32, [[1, 128]], OP.is_equal, 0.0,
                                              base=0, channel_multiplier=-1), incs=[(pool, 1)])
    p.add("gpsimd", lambda e: e.memset(vtmA, 1.0), incs=[(pool, 1)])
    p.add("gpsimd", lambda e: e.memset(vtmB, 1.0), incs=[(pool, 1)])
    g_cst = pool.n

    pbias, = psopen([("pbias", [128, 128], F32)])
    p.add("sync", lambda e: e.dma_start(out=poff, in_=d_poff), incs=[(dmp, 16)])
    w_po = dmp.n
    for (src, rows, dst) in ((d_bqkv, 24, bq_t), (d_bout, 8, bo_t), (d_b1, 32, b1_t),
                             (d_b2, 8, b2_t)):
        p.add("sync", lambda e, src=src, rows=rows: e.dma_start(out=braw[0:rows, :], in_=src),
              waits=[(pe, pe.n)], incs=[(dmb, 16)])
        w_l = dmb.n
        p.add("tensor", lambda e, rows=rows: e.transpose(pbias[:, 0:rows], braw[0:rows, :],
                                                         id_f32[0:rows, 0:rows]),
              waits=[(dmb, w_l), (pool, g_cst), (dve, None)], incs=[(pe, 1)])
        w_t = pe.n
        p.add("vector", lambda e, rows=rows, dst=dst: e.tensor_copy(dst, pbias[:, 0:rows]),
              waits=[(pe, w_t)], incs=[(dve, 1)])
    p.add("vector", lambda e: e.tensor_scalar(b1s, b1_t, 256.0, None, OP.mult),
          incs=[(dve, 1)])
    p.add("vector", lambda e: e.tensor_scalar(b2s, b2_t, 256.0, None, OP.mult),
          incs=[(dve, 1)])
    d_bias = dve.n

    pv = pvars[:, 0:1]
    pvb = pvars[:, 1:2]
    m64 = pvars[:, 2:3]
    pvh = pvars[:, 3:4]
    pvi = pvi_t
    pvf = pvars[:, 5:6]
    pvd = pvars[:, 6:7]
    pvfl = pvars[:, 7:8]
    m2 = pvars[:, 8:9]
    fi = pvars[:, 9:10]
    freq = pvars[:, 10:11]
    off = pvars[:, 11:12]
    pos_i = xstage[:, 0:1024]
    argt = xstage[:, 1024:2048]
    vvt = xstage[:, 2048:3072]
    posw = xstage[:, 3072:4096]

    p.add("gpsimd", lambda e: e.iota(pv, [[0, 1]], base=0, channel_multiplier=1,
                                     allow_small_or_imprecise_dtypes=True), incs=[(pool, 1)])
    g_iota = pool.n
    p.add("vector", lambda e: e.tensor_scalar(pvb, pv, 64.0, -64.0, OP.is_ge, OP.mult),
          waits=[(pool, g_iota)], incs=[(dve, 1)])
    p.add("vector", lambda e: e.tensor_tensor(m64, pvb, pv, OP.add), incs=[(dve, 1)])
    p.add("vector", lambda e: e.tensor_scalar(pvh, m64, 0.5, None, OP.mult), incs=[(dve, 1)])
    p.add("vector", lambda e: e.tensor_copy(pvi, pvh), incs=[(dve, 1)])
    p.add("vector", lambda e: e.tensor_copy(pvf, pvi), incs=[(dve, 1)])
    p.add("vector", lambda e: e.tensor_tensor(pvd, pvh, pvf, OP.subtract), incs=[(dve, 1)])
    p.add("vector", lambda e: e.scalar_tensor_tensor(pvfl, pvd, 0.0, pvf, OP.is_lt, OP.subtract),
          incs=[(dve, 1)])
    p.add("vector", lambda e: e.scalar_tensor_tensor(m2, pvfl, 2.0, m64, OP.mult, OP.add),
          incs=[(dve, 1)])
    p.add("vector", lambda e: e.tensor_tensor(fi, m64, m2, OP.subtract), incs=[(dve, 1)])
    d_fi = dve.n
    p.add("scalar", lambda e: e.activation(freq, fi, AF.Exp, scale=-LN1e4 / 64.0),
          waits=[(dve, d_fi)], incs=[(act, 1)])
    a_fr = act.n
    p.add("vector", lambda e: e.tensor_scalar(off, m2, PI / 2.0, None, OP.mult), incs=[(dve, 1)])
    for seg in range(4):
        half, r = seg // 2, seg % 2
        p.add("gpsimd", lambda e, r=r: e.iota(pos_i, [[1, 1024]], base=r * 1024,
                                              channel_multiplier=0,
                                              allow_small_or_imprecise_dtypes=True),
              waits=[(dve, dve.n)], incs=[(pool, 1)])
        g_i = pool.n
        p.add("vector", lambda e, half=half: e.tensor_scalar(argt, pos_i, poff[:, half:half + 1],
                                                             freq, OP.add, OP.mult),
              waits=[(act, a_fr), (dmp, w_po), (pool, g_i)], incs=[(dve, 1)])
        p.add("vector", lambda e: e.tensor_scalar(argt, argt, off, None, OP.add), incs=[(dve, 1)])
        p.add("vector", lambda e: e.tensor_scalar(vvt, argt, 1.0 / (2 * PI), None, OP.mult),
              incs=[(dve, 1)])
        p.add("vector", lambda e: e.tensor_copy(vvi_t, vvt), incs=[(dve, 1)])
        p.add("vector", lambda e: e.tensor_copy(posw, vvi_t), incs=[(dve, 1)])
        p.add("vector", lambda e: e.tensor_tensor(vvt, vvt, posw, OP.subtract), incs=[(dve, 1)])
        p.add("vector", lambda e: e.scalar_tensor_tensor(posw, vvt, 0.5, posw, OP.is_ge, OP.add),
              incs=[(dve, 1)])
        p.add("vector", lambda e: e.scalar_tensor_tensor(posw, vvt, -0.5, posw, OP.is_lt, OP.subtract),
              incs=[(dve, 1)])
        p.add("vector", lambda e: e.scalar_tensor_tensor(argt, posw, 2 * PI, argt, OP.mult, OP.add),
              incs=[(dve, 1)])
        p.add("vector", lambda e: e.tensor_scalar(argt, argt, PI, -PI, OP.min, OP.max),
              incs=[(dve, 1)])
        d_th = dve.n
        p.add("scalar", lambda e, seg=seg: e.activation(posb[:, seg * 1024:(seg + 1) * 1024],
                                                        argt, AF.Sin),
              waits=[(dve, d_th)], incs=[(act, 1)])
    a_pos = act.n

    def ln_stats(src_tile, scol, waits=()):
        s_ = stats[:, scol + 2:scol + 3]
        q_ = stats[:, scol + 3:scol + 4]
        p.add("vector", lambda e: e.reduce_sum(s_, src_tile, AX.X), waits=list(waits),
              incs=[(dve, 1)])
        p.add("scalar", lambda e: e.activation(sqm, src_tile, AF.Square, accum_out=q_),
              waits=list(waits) + [(dve, dve.n)], incs=[(act, 1)])
        a_q = act.n
        p.add("vector", lambda e: e.tensor_scalar(stats[:, scol:scol + 1], s_, 1.0 / 1024.0,
                                                  None, OP.mult), incs=[(dve, 1)])
        p.add("vector", lambda e: e.tensor_scalar(q_, q_, 1.0 / 1024.0, None, OP.mult),
              waits=[(act, a_q)], incs=[(dve, 1)])
        p.add("vector", lambda e: e.tensor_tensor(s_, stats[:, scol:scol + 1],
                                                  stats[:, scol:scol + 1], OP.mult), incs=[(dve, 1)])
        p.add("vector", lambda e: e.tensor_tensor(q_, q_, s_, OP.subtract), incs=[(dve, 1)])
        d_v = dve.n
        p.add("scalar", lambda e: e.activation(s_, q_, AF.Sqrt, bias=cst[:, 3:4]),
              waits=[(dve, d_v), (pool, g_cst)], incs=[(act, 1)])
        a_s = act.n
        p.add("vector", lambda e: e.reciprocal(stats[:, scol + 1:scol + 2], s_),
              waits=[(act, a_s)], incs=[(dve, 1)])
        return dve.n

    ptr_bf, = psopen([("ptr_bf", [128, 2048], BF16)])
    for t in range(32):
        src = d_xa if t < 16 else d_xb
        tt = t % 16
        par = (t // 2) % 2
        if t % 2 == 0:
            p.add("sync", lambda e, src=src, tt=tt, par=par: e.dma_start(
                out=xstage[:, par * 2048:(par + 1) * 2048].rearrange("p (k d) -> p k d", k=2),
                in_=src.rearrange("(k p) d -> p k d", p=128)[:, tt:tt + 2, :]),
                waits=[(dve, dve.n), (act, act.n)], incs=[(dmx, 16)])
        w_g = dmx.n
        xt = xstage[:, (par * 2 + (t % 2)) * 1024:(par * 2 + (t % 2) + 1) * 1024]
        ln_stats(xt, 0, waits=[(dmx, w_g)])
        p.add("vector", lambda e, xt=xt: e.tensor_scalar(
            ln1tmp_b, xt, stats[:, 0:1], stats[:, 1:2], OP.subtract, OP.mult), incs=[(dve, 1)])
        d_ap = dve.n
        for c in range(8):
            p.add("tensor", lambda e, c=c: e.transpose(
                ptr_bf[:, c * 128:(c + 1) * 128], ln1tmp_b[:, c * 128:(c + 1) * 128], id_bf),
                waits=[(dve, d_ap), (pool, g_cst)], incs=[(pe, 1)])
        w_tr = pe.n
        p.add("vector", lambda e, t=t: e.tensor_copy(
            ln1_fm[:, :, t * 128:(t + 1) * 128],
            ptr_bf[:, 0:1024].rearrange("p (c j) -> p c j", c=8)),
            waits=[(pe, w_tr)], incs=[(dve, 1)])
    psclose()
    d_ln1 = dve.n

    sqq_t = attn_un
    wq_b0 = wqkvb

    def qkv_slab_dma(pair, slab_free_pe):
        buf = (pair % 2) * 3072
        for j in range(3):
            col0 = j * 1024 + pair * 128
            p.add("gpsimd", lambda e, buf=buf, j=j, col0=col0: e.dma_start(
                out=wqkvb[:, buf + j * 1024:buf + (j + 1) * 1024].rearrange(
                    "p (k j2) -> p k j2", k=8),
                in_=d_wqkv[:, col0:col0 + 128].rearrange("(k p) j2 -> p k j2", p=128)),
                waits=[(pe, slab_free_pe)], incs=[(dmw, 16)])
        return dmw.n

    slab_ready = [0] * NPAIR
    slab_free = [0, 0]
    slab_ready[0] = qkv_slab_dma(0, 0)
    slab_ready[1] = qkv_slab_dma(1, 0)

    exp_act = {}
    s_pe = {}
    av_pe = {}
    gs = 0
    av_last_prev = 0
    for pair in range(NPAIR):
        par = pair % 2
        wq_b = wqkvb[:, par * 3072:par * 3072 + 1024]
        wk_b = wqkvb[:, par * 3072 + 1024:par * 3072 + 2048]
        wv_b = wqkvb[:, par * 3072 + 2048:par * 3072 + 3072]
        d_pair_prev = dve.n

        PPq, PVT, PCS = psopen([(f"PP{pair}", [128, 1024], F32),
                                (f"PVT{pair}", [128, 2048], BF16),
                                (f"PCS{pair}", [128, 2048], F32)])
        slot_evac = [0, 0]
        for qt in range(4):
            pa = qt % 2
            dst = PPq[:, pa * 512:(pa + 1) * 512]
            for kc in range(8):
                p.add("tensor", lambda e, kc=kc, qt=qt, dst=dst, wq_b=wq_b: e.matmul(
                    dst, wq_b[:, kc * 128:(kc + 1) * 128],
                    ln1_fm[:, kc, qt * 512:(qt + 1) * 512],
                    start=(kc == 0), stop=(kc == 7)),
                    waits=[(dmw, slab_ready[pair] if kc == 0 else 0),
                           (dve, (max(d_pair_prev, d_ln1) if qt == 0 else 0) if kc == 0 else 0),
                           (act, slot_evac[pa] if kc == 0 else 0)],
                    incs=[(pe, 1)] if kc == 7 else ())
            w_mm = pe.n
            p.add("scalar", lambda e, qt=qt, dst=dst, pair=pair: e.activation(
                q_pair[:, qt * 512:(qt + 1) * 512], dst, AF.Identity,
                bias=bq_t[:, pair:pair + 1], scale=1.0),
                waits=[(pe, w_mm), (dve, d_bias)], incs=[(act, 1)])
            slot_evac[pa] = act.n
        for which in ("k", "v"):
            wb = wk_b if which == "k" else wv_b
            bcol = 8 + pair if which == "k" else 16 + pair
            dstt = k_pair if which == "k" else v_fm
            for ktile in range(8):
                pa = ktile % 2
                dst = PPq[:, pa * 512:(pa + 1) * 512]
                for kc in range(8):
                    p.add("tensor", lambda e, kc=kc, ktile=ktile, dst=dst, wb=wb,
                          which=which: e.matmul(
                        dst, wb[:, kc * 128:(kc + 1) * 128],
                        ln1_fm[:, kc, ktile * 512:(ktile + 1) * 512],
                        start=(kc == 0), stop=(kc == 7 and which == "v")),
                        waits=[(act, slot_evac[pa] if kc == 0 else 0)],
                        incs=[(pe, 1)] if (kc == 7 and which == "v") else ())
                if which == "k":
                    p.add("tensor", lambda e, ktile=ktile, dst=dst: e.matmul(
                        dst, id_bf, posb[:, ktile * 512:(ktile + 1) * 512],
                        start=False, stop=True),
                        waits=[(act, a_pos), (pool, g_cst)], incs=[(pe, 1)])
                w_mm = pe.n
                p.add("scalar", lambda e, ktile=ktile, dst=dst, dstt=dstt, bcol=bcol: e.activation(
                    dstt[:, ktile * 512:(ktile + 1) * 512], dst, AF.Identity,
                    bias=bq_t[:, bcol:bcol + 1], scale=1.0),
                    waits=[(pe, w_mm)], incs=[(act, 1)])
                slot_evac[pa] = act.n
        a_qkv = act.n
        slab_free[par] = pe.n
        if pair + 2 < NPAIR:
            slab_ready[pair + 2] = qkv_slab_dma(pair + 2, slab_free[par])

        for rnd in range(2):
            for j in range(16):
                kc = rnd * 16 + j
                p.add("tensor", lambda e, kc=kc, j=j: e.transpose(
                    PVT[:, j * 128:(j + 1) * 128], v_fm[:, kc * 128:(kc + 1) * 128], id_bf),
                    waits=[(act, a_qkv if j == 0 else 0),
                           (dve, (d_pair_prev if rnd == 0 else dve.n) if j == 0 else 0)],
                    incs=[(pe, 1)])
            w_vt = pe.n
            for (dst, lo) in ((vtmA, 0), (vtmB, 64)):
                p.add("vector", lambda e, rnd=rnd, dst=dst, lo=lo: e.tensor_copy(
                    dst.rearrange("p (c j) -> p c j", c=32)[:, rnd * 16:(rnd + 1) * 16, 0:64],
                    PVT.rearrange("p (c j) -> p c j", c=16)[:, :, lo:lo + 64]),
                    waits=[(pe, w_vt), (pe, av_last_prev)], incs=[(dve, 1)])
        d_vtm = dve.n

        p.add("scalar", lambda e: e.activation(sqq_t, q_pair, AF.Square),
              waits=[(dve, dve.n), (dmm, dmm.n), (pe, av_last_prev)], incs=[(act, 1)])
        p.add("scalar", lambda e: e.activation(sqk_t, k_pair, AF.Square), incs=[(act, 1)])
        a_sq = act.n
        for j in range(4):
            p.add("tensor", lambda e, j=j: e.matmul(
                PCS[0:1, j * 512:(j + 1) * 512], ones_bf, sqq_t[:, j * 512:(j + 1) * 512],
                start=True, stop=True),
                waits=[(act, a_sq), (pool, g_cst)], incs=[(pe, 1)])
        w_cs = pe.n
        p.add("vector", lambda e: e.reduce_max(csst[:, 0:1], PCS[0:1, 0:2048], AX.X),
              waits=[(pe, w_cs)], incs=[(dve, 1)])
        d_cs = dve.n
        for j in range(4):
            p.add("tensor", lambda e, j=j: e.matmul(
                PCS[0:1, j * 512:(j + 1) * 512], ones_bf, sqk_t[:, j * 512:(j + 1) * 512],
                start=True, stop=True),
                waits=[(dve, d_cs)], incs=[(pe, 1)])
        w_cs = pe.n
        p.add("vector", lambda e: e.reduce_max(csst[:, 1:2], PCS[0:1, 0:2048], AX.X),
              waits=[(pe, w_cs)], incs=[(dve, 1)])
        d_cs = dve.n
        for j in range(4):
            p.add("tensor", lambda e, j=j: e.matmul(
                PCS[0:1, j * 512:(j + 1) * 512], ones_bf,
                sqk_t[:, 2048 + j * 512:2048 + (j + 1) * 512],
                start=True, stop=True),
                waits=[(dve, d_cs)], incs=[(pe, 1)])
        w_cs = pe.n
        p.add("vector", lambda e: e.reduce_max(csst[:, 2:3], PCS[0:1, 0:2048], AX.X),
              waits=[(pe, w_cs)], incs=[(dve, 1)])
        p.add("vector", lambda e: e.tensor_tensor(csst[:, 1:2], csst[:, 1:2], csst[:, 2:3], OP.max),
              incs=[(dve, 1)])
        p.add("vector", lambda e: e.tensor_tensor(csst[:, 3:4], csst[:, 0:1], csst[:, 1:2], OP.mult),
              incs=[(dve, 1)])
        d_pr = dve.n
        p.add("scalar", lambda e: e.activation(csst[:, 4:5], csst[:, 3:4], AF.Sqrt, scale=1.0 / 64.0),
              waits=[(dve, d_pr)], incs=[(act, 1)])
        a_c = act.n
        p.add("vector", lambda e: e.tensor_scalar(csst[:, 5:6], csst[:, 4:5], -1.0, None, OP.mult),
              waits=[(act, a_c)], incs=[(dve, 1)])
        d_nc0 = dve.n
        p.add("tensor", lambda e: e.matmul(PCS[:, 0:1], ones_f, csst[:, 5:6],
                                           start=True, stop=True),
              waits=[(dve, d_nc0)], incs=[(pe, 1)])
        w_bc = pe.n
        p.add("vector", lambda e: e.tensor_copy(negc_b, PCS[:, 0:1]),
              waits=[(pe, w_bc)], incs=[(dve, 1)])
        d_nc = dve.n
        psclose()

        SA, AVp = psopen([(f"SA{pair}", [128, 2048], F32), (f"AV{pair}", [128, 2048], F32)])
        d_avfree = [dve.n, dve.n]

        def emit_S2(qhalf, s):
            head, kc = s % 2, s // 2
            base = head * 64
            g = gs + s
            sa = SA[:, (s % 2) * 1024:(s % 2 + 1) * 1024]
            for q2 in range(2):
                qt = qhalf * 2 + q2
                p.add("tensor", lambda e, kc=kc, qt=qt, q2=q2, base=base, sa=sa: e.matmul(
                    sa[:, q2 * 512:(q2 + 1) * 512],
                    k_pair[base:base + 64, kc * 128:(kc + 1) * 128],
                    q_pair[base:base + 64, qt * 512:(qt + 1) * 512],
                    start=True, stop=True, tile_position=(base, 0)),
                    waits=[(act, max(exp_act.get(g - 2, 0), a_qkv) if q2 == 0 else 0),
                           (dve, d_nc if q2 == 0 else 0)],
                    incs=[(pe, 1)] if q2 == 1 else ())
            s_pe[g] = pe.n

        for qhalf in range(2):
            emit_S2(qhalf, 0)
            for s in range(64):
                g = gs + s
                head, kc = s % 2, s // 2
                vtm = vtmA if head == 0 else vtmB
                if s < 63:
                    emit_S2(qhalf, s + 1)
                sa = SA[:, (s % 2) * 1024:(s % 2 + 1) * 1024]
                ptb = PTr[:, (g % 4) * 1024:(g % 4 + 1) * 1024]
                p.add("scalar", lambda e, sa=sa, ptb=ptb: e.activation(
                    ptb, sa, AF.Exp, bias=negc_b, scale=0.125),
                    waits=[(pe, max(s_pe[g], av_pe.get(g - 4, 0))), (dve, d_nc)],
                    incs=[(act, 1)])
                exp_act[g] = act.n
                for q2 in range(2):
                    p.add("tensor", lambda e, kc=kc, head=head, q2=q2, vtm=vtm,
                          ptb=ptb: e.matmul(
                        AVp[0:65, head * 1024 + q2 * 512:head * 1024 + (q2 + 1) * 512],
                        vtm[:, kc * 65:kc * 65 + 65],
                        ptb[:, q2 * 512:(q2 + 1) * 512],
                        start=(kc == 0), stop=(kc == 31)),
                        waits=[(act, exp_act[g] if q2 == 0 else 0),
                               (dve, d_avfree[head] if (kc == 0 and q2 == 0) else 0)],
                        incs=[(pe, 1)] if q2 == 1 else ())
                av_pe[g] = pe.n
            gs += 64
            for head in range(2):
                w_av = av_pe[gs - 2 + head]
                hr = slice(head * 1024, (head + 1) * 1024)
                p.add("vector", lp(lambda e, hr=hr: e.reciprocal(
                    rc_t[0:1, hr], AVp[64:65, hr])),
                    waits=[(pe, w_av)], incs=[(dve, 1)])
                p.add("vector", lambda e, hr=hr: e.tensor_copy(
                    attn_un[0:64, hr], AVp[0:64, hr]), incs=[(dve, 1)])
                d_ev = dve.n
                for qh in range(2):
                    p.add("tensor", lambda e, head=head, qh=qh: e.matmul(
                        AVp[0:64, head * 1024 + qh * 512:head * 1024 + (qh + 1) * 512],
                        onesr_bf[:, 0:64],
                        rc_t[0:1, head * 1024 + qh * 512:head * 1024 + (qh + 1) * 512],
                        start=True, stop=True),
                        waits=[(dve, d_ev)], incs=[(pe, 1)])
                w_b = pe.n
                if head == 0:
                    p.add("vector", lambda e, hr=hr, pair=pair, qhalf=qhalf: e.tensor_tensor(
                        attn_fm[0:64, pair, qhalf * 1024:(qhalf + 1) * 1024],
                        attn_un[0:64, hr], AVp[0:64, hr], OP.mult),
                        waits=[(pe, w_b)], incs=[(dve, 1)])
                else:
                    p.add("vector", lambda e, hr=hr: e.tensor_tensor(
                        attn_un[0:64, hr], attn_un[0:64, hr], AVp[0:64, hr], OP.mult),
                        waits=[(pe, w_b)], incs=[(dve, 1)])
                d_avfree[head] = dve.n
            d_nm = dve.n
            p.add("sync", lambda e, pair=pair, qhalf=qhalf: e.dma_start(
                out=attn_fm[64:128, pair, qhalf * 1024:(qhalf + 1) * 1024],
                in_=attn_un[0:64, 1024:2048]),
                waits=[(dve, d_nm)], incs=[(dmm, 16)])
        av_last_prev = av_pe[gs - 1]
        psclose()
    d_attn = dve.n
    w_attnmv = dmm.n

    psclose()
    stLN1.close()

    PO, PT2 = psopen([("PO", [128, 2048], F32), ("PT2", [128, 2048], BF16)])
    outattnr = sbin(stOP, "outattnr", [128, 8 * 2048], BF16)
    outattn = outattnr.rearrange("p (c t) -> p c t", c=8)

    def wout_dma(dm, free_pe):
        p.add("gpsimd", lambda e, dm=dm: e.dma_start(
            out=wqkvb[:, (dm % 2) * 1024:(dm % 2) * 1024 + 1024].rearrange(
                "p (k j) -> p k j", k=8),
            in_=d_wout[:, dm * 128:(dm + 1) * 128].rearrange("(k p) j -> p k j", p=128)),
            waits=[(pe, free_pe)], incs=[(dmw, 16)])
        return dmw.n

    wo_ready = [0] * 8
    wo_mm_done = [0, 0]
    wo_ready[0] = wout_dma(0, av_last_prev)
    wo_ready[1] = wout_dma(1, av_last_prev)
    for dm in range(8):
        wo_b = wqkvb[:, (dm % 2) * 1024:(dm % 2) * 1024 + 1024]
        slot_evac = [0, 0]
        for qt in range(4):
            dst = PO[:, (qt % 2) * 512:(qt % 2 + 1) * 512]
            for hc in range(8):
                p.add("tensor", lambda e, hc=hc, qt=qt, dst=dst, wo_b=wo_b: e.matmul(
                    dst, wo_b[:, hc * 128:(hc + 1) * 128],
                    attn_fm[:, hc, qt * 512:(qt + 1) * 512],
                    start=(hc == 0), stop=(hc == 7)),
                    waits=[(dmw, wo_ready[dm] if (hc == 0 and qt == 0) else 0),
                           (dve, (d_attn if (dm == 0 and qt == 0) else 0) if hc == 0 else 0),
                           (dmm, (w_attnmv if (dm == 0 and qt == 0) else 0) if hc == 0 else 0),
                           (act, slot_evac[qt % 2] if hc == 0 else 0)],
                    incs=[(pe, 1)] if hc == 7 else ())
            w_mm = pe.n
            p.add("scalar", lambda e, dm=dm, qt=qt, dst=dst: e.activation(
                outattn[:, dm, qt * 512:(qt + 1) * 512], dst, AF.Identity,
                bias=bo_t[:, dm:dm + 1], scale=1.0),
                waits=[(pe, w_mm)], incs=[(act, 1)])
            slot_evac[qt % 2] = act.n
        wo_mm_done[dm % 2] = pe.n
        if dm + 2 < 8:
            wo_ready[dm + 2] = wout_dma(dm + 2, wo_mm_done[dm % 2])
    a_op = act.n

    at_tm = sbin(stOP, "at_tm", [128, 1024], BF16)
    x_tile = sbin(stOP, "x_tile", [128, 1024], F32)
    x2_tile = sbin(stOP, "x2_tile", [128, 1024], F32)
    x2w = [0] * 16
    for t in range(16):
        p.add("sync", lambda e, t=t: e.dma_start(
            out=x_tile, in_=d_xa.rearrange("(k p) d -> p k d", p=128)[:, t:t + 1, :]),
            waits=[(dve, dve.n)], incs=[(dmx, 16)])
        w_x = dmx.n
        for c in range(8):
            p.add("tensor", lambda e, c=c, t=t: e.transpose(
                PT2[:, c * 128:(c + 1) * 128], outattn[:, c, t * 128:(t + 1) * 128], id_bf),
                waits=[(act, a_op if c == 0 else 0), (dve, dve.n if c == 0 else 0)],
                incs=[(pe, 1)])
        w_tr = pe.n
        p.add("vector", lambda e: e.tensor_copy(at_tm.rearrange("p (c j) -> p c j", c=8),
                                                PT2[:, 0:1024].rearrange("p (c j) -> p c j", c=8)),
              waits=[(pe, w_tr)], incs=[(dve, 1)])
        ln_stats(at_tm, 0)
        p.add("vector", lambda e: e.tensor_scalar(at_tm, at_tm, stats[:, 0:1], stats[:, 1:2],
                                                  OP.subtract, OP.mult), incs=[(dve, 1)])
        p.add("vector", lambda e: e.tensor_tensor(x2_tile, x_tile, at_tm, OP.add),
              waits=[(dmx, w_x), (dmo, dmo.n)], incs=[(dve, 1)])
        d_x2c = dve.n
        p.add("sync", lambda e, t=t: e.dma_start(
            out=d_x2.rearrange("(k p) d -> p k d", p=128)[:, t:t + 1, :], in_=x2_tile),
            waits=[(dve, d_x2c)], incs=[(dmo, 16)])
        x2w[t] = dmo.n
    psclose()

    stOP.close()
    stAtt.close()
    ln3r = sbin(stFFN, "ln3r", [128, 8 * 1024], BF16)
    ln3_fm = ln3r.rearrange("p (c t) -> p c t", c=8)
    hq_r = sbin(stFFN, "hq_r", [128, 32 * 1024], BF16)
    hq_sb = hq_r.rearrange("p (f t) -> p f t", f=32)
    x2s = sbin(stFFN, "x2s", [128, 1024], F32)
    ln3tmp_b = sbin(stFFN, "ln3tmp_b", [128, 1024], BF16)
    t_hi = sbin(stFFN, "t_hi", [128, 2048], I32)
    t_u = sbin(stFFN, "t_u", [128, 2048], I32)
    t_rf = sbin(stFFN, "t_rf", [128, 2048], F32)
    t_g0 = sbin(stFFN, "t_g0", [128, 2048], F32)
    t_g1 = sbin(stFFN, "t_g1", [128, 2048], F32)
    t_dg = sbin(stFFN, "t_dg", [128, 1024], F32)
    t_tdg = sbin(stFFN, "t_tdg", [128, 1024], F32)
    t_oi = sbin(stFFN, "t_oi", [128, 1024], I32)
    w1b = sbin(stFFN, "w1b", [128, 2048], BF16)
    w2b = sbin(stFFN, "w2b", [128, 1024], BF16)
    out_f = sbin(stFFN, "out_f", [128, 8 * 1024], F32)
    out_fm = out_f.rearrange("p (c t) -> p c t", c=8)
    fout_tm = sbin(stFFN, "fout_tm", [128, 1024], F32)

    hi_done = [0, 0]
    dg_done = [0, 0]
    dgd = {-1: 0, -2: 0}
    tdgd = {-1: 0, -2: 0}
    yd = {-1: 0, -2: 0}
    yid = {-1: 0}
    g1d = {-1: 0, -2: 0}
    hbfd = {-1: 0, -2: 0}
    hbf_act = {}

    for T in range(2):
        PTl, = psopen([(f"PTl{T}", [128, 2048], BF16)])
        for j in range(8):
            t = T * 8 + j
            p.add("sync", lambda e, t=t: e.dma_start(
                out=x2s, in_=d_x2.rearrange("(k p) d -> p k d", p=128)[:, t:t + 1, :]),
                waits=[(dve, dve.n), (dmo, x2w[t])], incs=[(dmx, 16)])
            w_x = dmx.n
            ln_stats(x2s, 0, waits=[(dmx, w_x)])
            p.add("vector", lambda e: e.tensor_scalar(ln3tmp_b, x2s, stats[:, 0:1],
                                                      stats[:, 1:2], OP.subtract, OP.mult),
                  incs=[(dve, 1)])
            d_ap = dve.n
            po = (j % 2) * 1024
            for c in range(8):
                p.add("tensor", lambda e, c=c, po=po: e.transpose(
                    PTl[:, po + c * 128:po + (c + 1) * 128],
                    ln3tmp_b[:, c * 128:(c + 1) * 128], id_bf),
                    waits=[(dve, d_ap if c == 0 else 0)],
                    incs=[(pe, 1)] if c == 7 else ())
            w_tr = pe.n
            p.add("vector", lambda e, j=j, po=po: e.tensor_copy(
                ln3_fm[:, :, j * 128:(j + 1) * 128],
                PTl[:, po:po + 1024].rearrange("p (c j2) -> p c j2", c=8)),
                waits=[(pe, w_tr)], incs=[(dve, 1)])
        psclose()
        d_ln3 = dve.n

        PH, = psopen([(f"PH{T}", [128, 2048], F32)])
        w1_mm_done = [pe.n, pe.n]

        def w1_dma(f, par):
            p.add("gpsimd", lambda e, f=f, par=par: e.dma_start(
                out=w1b[:, par * 1024:(par + 1) * 1024].rearrange("p (k j) -> p k j", k=8),
                in_=d_w1[:, f * 128:(f + 1) * 128].rearrange("(k p) j -> p k j", p=128)),
                waits=[(pe, w1_mm_done[par])], incs=[(dmw, 16)])
            return dmw.n

        w1_ready = [0] * 34
        w1_ready[0] = w1_dma(0, 0)
        w1_ready[1] = w1_dma(1, 1)
        for f in range(32):
            par = f % 2
            ph = PH[:, par * 1024:(par + 1) * 1024]
            wcol = par * 1024
            hi = t_hi[:, par * 1024:(par + 1) * 1024]
            u = t_u[:, par * 1024:(par + 1) * 1024]
            rf = t_rf[:, par * 1024:(par + 1) * 1024]
            g0 = t_g0[:, par * 1024:(par + 1) * 1024]
            g1 = t_g1[:, par * 1024:(par + 1) * 1024]
            for kc in range(8):
                for h in range(2):
                    p.add("tensor", lambda e, kc=kc, h=h, ph=ph, wcol=wcol: e.matmul(
                        ph[:, h * 512:(h + 1) * 512],
                        w1b[:, wcol + kc * 128:wcol + (kc + 1) * 128],
                        ln3_fm[:, kc, h * 512:(h + 1) * 512],
                        start=(kc == 0), stop=(kc == 7)),
                        waits=[(dmw, w1_ready[f] if (kc == 0 and h == 0) else 0),
                               (dve, max(d_ln3, hi_done[par]) if (kc == 0 and h == 0) else 0)],
                        incs=[(pe, 1)] if (kc == 7 and h == 1) else ())
            w1_mm_done[par] = pe.n
            w_mm = pe.n
            if f + 2 < 32:
                w1_ready[f + 2] = w1_dma(f + 2, par)
            p.add("vector", lambda e, ph=ph, hi=hi, f=f: e.tensor_scalar(
                hi, ph, 256.0, b1s[:, f:f + 1], OP.mult, OP.add),
                waits=[(pe, w_mm), (dve, d_bias), (act, hbfd[f - 2])], incs=[(dve, 1)])
            hi_done[par] = dve.n
            p.add("vector", lambda e, u=u, hi=hi: e.tensor_scalar(
                u, hi, 6, None, OP.arith_shift_right), incs=[(dve, 1)])
            d_u = dve.n
            p.add("vector", lambda e, rf=rf, u=u, hi=hi: e.scalar_tensor_tensor(
                rf, u, -64.0, hi, OP.mult, OP.add), incs=[(dve, 1)])
            if f >= 1:
                parp = (f - 1) % 2
                rfp = t_rf[:, parp * 1024:(parp + 1) * 1024]
                g0p = t_g0[:, parp * 1024:(parp + 1) * 1024]
                hip = t_hi[:, parp * 1024:(parp + 1) * 1024]
                p.add("vector", lambda e, rfp=rfp: e.tensor_tensor(
                    t_tdg, rfp, t_dg, OP.mult),
                    waits=[(pool, dgd[f - 1])], incs=[(dve, 1)])
                tdgd[f - 1] = dve.n
                p.add("vector", lambda e, rfp=rfp, g0p=g0p: e.scalar_tensor_tensor(
                    rfp, t_tdg, 0.015625, g0p, OP.mult, OP.add), incs=[(dve, 1)])
                yd[f - 1] = dve.n
                p.add("vector", lambda e, rfp=rfp, hip=hip: e.tensor_scalar(
                    hip, rfp, 256.0, None, OP.mult), incs=[(dve, 1)])
                yid[f - 1] = dve.n
            p.add("scalar", lambda e, g0=g0, u=u: e.activation(g0, u, AF.Gelu, scale=0.25),
                  waits=[(dve, d_u), (pool, dgd[f - 2]), (dve, yd[f - 2])], incs=[(act, 1)])
            p.add("scalar", lambda e, g1=g1, u=u: e.activation(g1, u, AF.Gelu,
                                                               scale=0.25, bias=cst[:, 4:5]),
                  incs=[(act, 1)])
            g1d[f] = act.n
            if f >= 1:
                parp = (f - 1) % 2
                hip = t_hi[:, parp * 1024:(parp + 1) * 1024]
                p.add("scalar", lambda e, hip=hip, fp=f - 1: e.activation(
                    hq_sb[:, fp, :], hip, AF.Identity, scale=0.00390625),
                    waits=[(dve, yid[f - 1])], incs=[(act, 1)])
                hbfd[f - 1] = act.n
                hbf_act[(T, f - 1)] = act.n
            p.add("gpsimd", lambda e, g0=g0, g1=g1: e.tensor_tensor(t_dg, g1, g0, OP.subtract),
                  waits=[(act, g1d[f]), (dve, tdgd[f - 1])], incs=[(pool, 1)])
            dgd[f] = pool.n
        parp = 31 % 2
        rfp = t_rf[:, parp * 1024:(parp + 1) * 1024]
        g0p = t_g0[:, parp * 1024:(parp + 1) * 1024]
        hip = t_hi[:, parp * 1024:(parp + 1) * 1024]
        p.add("vector", lambda e, rfp=rfp: e.tensor_tensor(t_tdg, rfp, t_dg, OP.mult),
              waits=[(pool, dgd[31])], incs=[(dve, 1)])
        tdgd[31] = dve.n
        p.add("vector", lambda e, rfp=rfp, g0p=g0p: e.scalar_tensor_tensor(
            rfp, t_tdg, 0.015625, g0p, OP.mult, OP.add), incs=[(dve, 1)])
        yd[31] = dve.n
        p.add("vector", lambda e, rfp=rfp, hip=hip: e.tensor_scalar(
            hip, rfp, 256.0, None, OP.mult), incs=[(dve, 1)])
        yid[31] = dve.n
        p.add("scalar", lambda e, hip=hip: e.activation(
            hq_sb[:, 31, :], hip, AF.Identity, scale=0.00390625),
            waits=[(dve, yid[31])], incs=[(act, 1)])
        hbfd[31] = act.n
        hbf_act[(T, 31)] = act.n
        psclose()
        d_chain_end = max(hi_done[0], hi_done[1], yid[31])

        PW8r, = psopen([(f"PW8{T}", [128, 4096], F32)])
        PW8 = PW8r
        oi_all = 0
        outff_hist = [0] * 10
        w2_mm_done = [0] * 34
        for dmh in range(2):
            w2_mm_done[0] = w2_mm_done[1] = pe.n

            def w2_dma(fc, par, dmh=dmh):
                p.add("gpsimd", lambda e, fc=fc, par=par, dmh=dmh: e.dma_start(
                    out=w2b[:, par * 512:(par + 1) * 512],
                    in_=d_w2.rearrange("(c p) d -> p c d", p=128)[:, fc, dmh * 512:(dmh + 1) * 512]),
                    waits=[(pe, w2_mm_done[fc])], incs=[(dmw, 16)])
                return dmw.n

            w2_ready = [0] * 34
            w2_ready[0] = w2_dma(0, 0)
            w2_ready[1] = w2_dma(1, 1)
            for fc in range(32):
                par = fc % 2
                for dmc in range(4):
                    for tb in range(2):
                        bank = tb * 4 + dmc
                        p.add("tensor", lambda e, fc=fc, dmc=dmc, tb=tb, bank=bank, par=par: e.matmul(
                            PW8[:, bank * 512:(bank + 1) * 512],
                            w2b[:, par * 512 + dmc * 128:par * 512 + (dmc + 1) * 128],
                            hq_sb[:, fc, tb * 512:(tb + 1) * 512],
                            start=(fc == 0), stop=(fc == 31)),
                            waits=[(dmw, w2_ready[fc] if (dmc == 0 and tb == 0) else 0),
                                   (act, hbf_act[(T, fc)] if (dmc == 0 and tb == 0) else 0),
                                   (dve, (d_chain_end if dmh == 0 else oi_all)
                                        if (fc == 0 and dmc == 0 and tb == 0) else 0)],
                            incs=[(pe, 1)] if (dmc == 3 and tb == 1) else ())
                w2_mm_done[fc + 2] = pe.n
                if fc + 2 < 32:
                    w2_ready[fc + 2] = w2_dma(fc + 2, par)
            w_w2 = pe.n
            for k in range(8):
                tb, dmc = k // 4, k % 4
                dm = dmh * 4 + dmc
                bank = tb * 4 + dmc
                ois = t_oi[:, (k % 2) * 512:(k % 2 + 1) * 512]
                p.add("vector", lambda e, bank=bank, dm=dm, ois=ois: e.tensor_scalar(
                    ois, PW8[:, bank * 512:(bank + 1) * 512], 256.0, b2s[:, dm:dm + 1],
                    OP.mult, OP.add),
                    waits=[(pe, w_w2), (act, outff_hist[k])], incs=[(dve, 1)])
                d_oi = dve.n
                p.add("scalar", lambda e, dm=dm, tb=tb, ois=ois: e.activation(
                    out_fm[:, dm, tb * 512:(tb + 1) * 512], ois, AF.Identity,
                    scale=0.00390625),
                    waits=[(dve, d_oi)], incs=[(act, 1)])
                outff_hist[k + 2] = act.n
            oi_all = dve.n
        a_of = act.n

        for j in range(8):
            t = T * 8 + j
            p.add("sync", lambda e, t=t: e.dma_start(
                out=x2s, in_=d_x2.rearrange("(k p) d -> p k d", p=128)[:, t:t + 1, :]),
                waits=[(dve, dve.n)], incs=[(dmx, 16)])
            w_x = dmx.n
            po = (j % 2) * 1024
            for c in range(8):
                p.add("tensor", lambda e, c=c, j=j, po=po: e.transpose(
                    PW8[:, po + c * 128:po + (c + 1) * 128],
                    out_fm[:, c, j * 128:(j + 1) * 128], id_f32),
                    waits=[(act, a_of if c == 0 else 0),
                           (dve, (oi_all if j == 0 else dve.n) if c == 0 else 0)],
                    incs=[(pe, 1)] if c == 7 else ())
            w_tr = pe.n
            p.add("vector", lambda e, po=po: e.tensor_copy(
                fout_tm.rearrange("p (c j2) -> p c j2", c=8),
                PW8[:, po:po + 1024].rearrange("p (c j2) -> p c j2", c=8)),
                waits=[(pe, w_tr), (dmo, dmo.n)], incs=[(dve, 1)])
            ln_stats(fout_tm, 0)
            p.add("vector", lambda e: e.tensor_scalar(fout_tm, fout_tm, stats[:, 0:1],
                                                      stats[:, 1:2], OP.subtract, OP.mult),
                  incs=[(dve, 1)])
            p.add("vector", lambda e: e.tensor_tensor(fout_tm, fout_tm, x2s, OP.add),
                  waits=[(dmx, w_x)], incs=[(dve, 1)])
            d_o = dve.n
            p.add("sync", lambda e, t=t: e.dma_start(
                out=d_out.rearrange("(k p) d -> p k d", p=128)[:, t:t + 1, :], in_=fout_tm),
                waits=[(dve, d_o)], incs=[(dmo, 16)])
        psclose()
    stFFN.close()

    p.add("sync", lambda e: None, waits=[(dmo, dmo.n)])

    with nc.Block() as block:
        p.emit(block)
    st.close()
    return nc


_NC_CACHE = None
_LAST_RES = None


def kernel(L0_attn_x, L0_attn_W_qkv, L0_attn_b_qkv, L0_attn_W_out, L0_attn_b_out,
           L0_ffn_w1, L0_ffn_b1, L0_ffn_w2, L0_ffn_b2):
    global _NC_CACHE
    x = np.ascontiguousarray(L0_attn_x, dtype=np.float32)
    B, SQ, D = x.shape
    if _NC_CACHE is None:
        _NC_CACHE = build_nc()
    nc = _NC_CACHE

    wqkv = np.ascontiguousarray(L0_attn_W_qkv, np.float32)
    bqkv = np.ascontiguousarray(L0_attn_b_qkv, np.float32).reshape(24, 128)
    wout = np.ascontiguousarray(L0_attn_W_out, np.float32)
    bout = np.ascontiguousarray(L0_attn_b_out, np.float32).reshape(8, 128)
    w1 = np.ascontiguousarray(L0_ffn_w1, np.float32)
    b1 = np.ascontiguousarray(L0_ffn_b1, np.float32).reshape(32, 128)
    w2 = np.ascontiguousarray(L0_ffn_w2, np.float32)
    b2 = np.ascontiguousarray(L0_ffn_b2, np.float32).reshape(8, 128)

    in_maps = []
    for core in range(8):
        b = core // 2
        half = core % 2
        own = x[b, half * 2048:(half + 1) * 2048, :]
        oth = x[b, (1 - half) * 2048:(2 - half) * 2048, :]
        po = np.empty((128, 2), np.float32)
        po[:, 0] = half * 2048.0
        po[:, 1] = (1 - half) * 2048.0
        in_maps.append({
            "xa": own, "xb": oth, "wqkv": wqkv, "bqkv": bqkv, "wout": wout,
            "bout": bout, "w1": w1, "b1": b1, "w2": w2, "b2": b2, "poff": po,
        })
    res = run_bass_kernel_spmd(nc, in_maps, core_ids=list(range(8)))
    global _LAST_RES
    _LAST_RES = res
    out = np.empty((B, SQ, D), np.float32)
    for core in range(8):
        b = core // 2
        half = core % 2
        out[b, half * 2048:(half + 1) * 2048, :] = res.results[core]["out"]
    return out


# version 17
# speedup vs baseline: 1.9572x; 1.0174x over previous
import numpy as np
from contextlib import ExitStack

import concourse.bass as bass
import concourse.mybir as mybir
from concourse.bass_utils import run_bass_kernel_spmd

F32 = mybir.dt.float32
F32R = mybir.dt.float32r
BF16 = mybir.dt.bfloat16
I32 = mybir.dt.int32
AF = mybir.ActivationFunctionType
OP = mybir.AluOpType
AX = mybir.AxisListType

LN1e4 = float(np.log(10000.0))
PI = float(np.pi)

NT = 2048
S = 4096
DM = 1024
FF = 4096
NPAIR = 8


class Ch:
    __slots__ = ("sem", "n", "name")

    def __init__(self, sem, name=""):
        self.sem = sem
        self.n = 0
        self.name = name


class Prog:
    ENGINES = ("sync", "gpsimd", "tensor", "scalar", "vector")
    SELF_WAIT = ("gpsimd", "scalar", "vector")

    def __init__(self, chans):
        self.steps = []
        self.chans = chans

    def add(self, engine, fn, waits=(), incs=()):
        waits = [(c, c.n if v is None else v) for (c, v) in waits]
        selfch = self.chans.get(engine)
        snap = selfch.n if selfch is not None else 0
        st = (engine, fn, waits, [], snap)
        self.steps.append(st)
        for (c, k) in incs:
            c.n += k
            st[3].append((c, k))
        return st

    def barrier(self, engines=("sync", "gpsimd", "tensor", "scalar", "vector")):
        snap = [(c, c.n) for c in set(self.chans.values()) if c is not None]
        for eng in engines:
            self.add(eng, lambda e: None, waits=list(snap))

    def emit(self, block):
        for engine in self.ENGINES:
            mysteps = [s for s in self.steps if s[0] == engine]
            if not mysteps:
                continue
            selfch = self.chans.get(engine)

            def section(e, mysteps=mysteps, engine=engine, selfch=selfch):
                done = {}
                for (_, fn, waits, incs, snap) in mysteps:
                    allw = list(waits)
                    if engine in self.SELF_WAIT and selfch is not None and snap > 0:
                        allw.append((selfch, snap))
                    for (c, v) in allw:
                        if v <= 0:
                            continue
                        if done.get(id(c), 0) >= v:
                            continue
                        e.wait_ge(c.sem, v)
                        done[id(c)] = v
                    inst = fn(e)
                    if inst is not None:
                        for (c, k) in incs:
                            inst.then_inc(c.sem, k)

            getattr(block, engine)(section)


def build_nc():
    nc = bass.Bass("TRN2", target_bir_lowering=False)
    st = ExitStack()

    d_xa = nc.dram_tensor("xa", [NT, DM], F32, kind="ExternalInput").ap()
    d_xb = nc.dram_tensor("xb", [NT, DM], F32, kind="ExternalInput").ap()
    d_wqkv = nc.dram_tensor("wqkv", [DM, 3 * DM], F32, kind="ExternalInput").ap()
    d_bqkv = nc.dram_tensor("bqkv", [24, 128], F32, kind="ExternalInput").ap()
    d_wout = nc.dram_tensor("wout", [DM, DM], F32, kind="ExternalInput").ap()
    d_bout = nc.dram_tensor("bout", [8, 128], F32, kind="ExternalInput").ap()
    d_w1 = nc.dram_tensor("w1", [DM, FF], F32, kind="ExternalInput").ap()
    d_b1 = nc.dram_tensor("b1", [32, 128], F32, kind="ExternalInput").ap()
    d_w2 = nc.dram_tensor("w2", [FF, DM], F32, kind="ExternalInput").ap()
    d_b2 = nc.dram_tensor("b2", [8, 128], F32, kind="ExternalInput").ap()
    d_poff = nc.dram_tensor("poff", [128, 2], F32, kind="ExternalInput").ap()
    d_out = nc.dram_tensor("out", [NT, DM], F32, kind="ExternalOutput").ap()
    d_x2 = nc.dram_tensor("x2scr", [NT, DM], F32, kind="Internal").ap()

    chans = {}
    for name in ("dmw", "dmw2", "dmz", "dmx", "dmb", "dmp", "dmm", "dmo", "pe", "act", "dve", "pool"):
        sem = st.enter_context(nc.semaphore(name))
        chans[name] = Ch(sem, name)
    (dmw, dmw2, dmz, dmx, dmb, dmp, dmm, dmo, pe, act, dve, pool) = (
        chans[k] for k in ("dmw", "dmw2", "dmz", "dmx", "dmb", "dmp", "dmm", "dmo",
                           "pe", "act", "dve", "pool"))
    wch = [dmw, dmw2]
    p = Prog({"vector": dve, "scalar": act, "gpsimd": pool, "tensor": pe, "sync": None})
    p.all_chans = list(chans.values())

    def sb(name, shape, dt):
        return st.enter_context(nc.sbuf_tensor(name, shape, dt)).ap()

    cst = sb("cst", [128, 8], F32)
    ones_bf = sb("ones_bf", [128, 1], BF16)
    onesr_bf = sb("onesr_bf", [1, 128], BF16)
    ones_f = sb("ones_f", [1, 128], F32)
    id_bf = sb("id_bf", [128, 128], BF16)
    id_f32 = sb("id_f32", [128, 128], F32)
    braw = sb("braw", [32, 128], F32)
    bq_t = sb("bq_t", [128, 24], F32)
    bo_t = sb("bo_t", [128, 8], F32)
    b1_t = sb("b1_t", [128, 32], F32)
    b2_t = sb("b2_t", [128, 8], F32)
    b1s = sb("b1s", [128, 32], F32)
    b2s = sb("b2s", [128, 8], F32)
    poff = sb("poff_s", [128, 2], F32)
    stats = sb("stats", [128, 16], F32)
    csst = sb("csst", [1, 8], F32)
    negc_b = sb("negc_b", [128, 1], F32)
    sqm = sb("sqm", [128, 1024], BF16)

    stLN1 = ExitStack()
    stAtt = ExitStack()
    stOP = ExitStack()
    stFFN = ExitStack()

    def sbin(stk, name, shape, dt):
        return stk.enter_context(nc.sbuf_tensor(name, shape, dt)).ap()

    pvars = sb("pvars", [128, 16], F32)
    pvi_t = sb("pvi_t", [128, 1], I32)

    wqkvb = sbin(stAtt, "wqkvb", [128, 6144], BF16)
    posb = sbin(stAtt, "posb", [128, S], BF16)

    attn_fmr = sbin(stAtt, "attn_fmr", [128, 8 * 2048], BF16)
    attn_fm = attn_fmr.rearrange("p (c t) -> p c t", c=8)
    q_pair = sbin(stAtt, "q_pair", [128, 2048], BF16)
    k_pair = sbin(stAtt, "k_pair", [128, 4096], BF16)
    v_fm = sbin(stAtt, "v_fm", [128, 4096], BF16)
    vtmA = sbin(stAtt, "vtmA", [128, 2080], BF16)
    vtmB = sbin(stAtt, "vtmB", [128, 2080], BF16)
    PTr = sbin(stAtt, "PTr", [128, 4096], BF16)
    PT0 = PTr[:, 0:2048]
    PT1 = PTr[:, 2048:4096]
    sqk_t = PTr
    attn_un = sbin(stAtt, "attn_un", [128, 2048], BF16)
    rc_t = sbin(stAtt, "rc_t", [2, 2048], BF16)
    sqq_alias = attn_un

    ln1r = sbin(stLN1, "ln1r", [128, 8 * 4096], BF16)
    ln1_fm = ln1r.rearrange("p (c t) -> p c t", c=8)
    xstage = sbin(stLN1, "xstage", [128, 4096], F32)
    ln1tmp_b = sbin(stLN1, "ln1tmp_b", [128, 1024], BF16)

    vvi_t = sbin(stLN1, "vvi_t", [128, 1024], I32)


    def lp(fn):
        def g(e):
            with nc.allow_low_precision(reason="bf16 softmax reciprocal/normalize"):
                return fn(e)
        return g

    ps_stack = ExitStack()

    def psopen(specs):
        out = []
        for (name, shape, dt) in specs:
            out.append(ps_stack.enter_context(nc.psum_tensor(name, shape, dt)).ap())
        return out

    def psclose():
        ps_stack.close()

    p.add("gpsimd", lambda e: e.memset(cst[:, 0:1], 16384.0), incs=[(pool, 1)])
    p.add("gpsimd", lambda e: e.memset(cst[:, 1:2], -64.0), incs=[(pool, 1)])
    p.add("gpsimd", lambda e: e.memset(cst[:, 2:3], -63.75), incs=[(pool, 1)])
    p.add("gpsimd", lambda e: e.memset(cst[:, 3:4], 1e-5), incs=[(pool, 1)])
    p.add("gpsimd", lambda e: e.memset(cst[:, 4:5], 0.25), incs=[(pool, 1)])
    p.add("gpsimd", lambda e: e.memset(ones_bf, 1.0), incs=[(pool, 1)])
    p.add("gpsimd", lambda e: e.memset(onesr_bf, 1.0), incs=[(pool, 1)])
    p.add("gpsimd", lambda e: e.memset(ones_f, 1.0), incs=[(pool, 1)])
    p.add("gpsimd", lambda e: e.memset(id_bf, 1.0), incs=[(pool, 1)])
    p.add("gpsimd", lambda e: e.affine_select(id_bf, id_bf, [[1, 128]], OP.is_equal, 0.0,
                                              base=0, channel_multiplier=-1), incs=[(pool, 1)])
    p.add("gpsimd", lambda e: e.memset(id_f32, 1.0), incs=[(pool, 1)])
    p.add("gpsimd", lambda e: e.affine_select(id_f32, id_f32, [[1, 128]], OP.is_equal, 0.0,
                                              base=0, channel_multiplier=-1), incs=[(pool, 1)])
    p.add("gpsimd", lambda e: e.memset(vtmA, 1.0), incs=[(pool, 1)])
    p.add("gpsimd", lambda e: e.memset(vtmB, 1.0), incs=[(pool, 1)])
    g_cst = pool.n

    pbias, = psopen([("pbias", [128, 128], F32)])
    p.add("sync", lambda e: e.dma_start(out=poff, in_=d_poff), incs=[(dmp, 16)])
    w_po = dmp.n
    for (src, rows, dst) in ((d_bqkv, 24, bq_t), (d_bout, 8, bo_t), (d_b1, 32, b1_t),
                             (d_b2, 8, b2_t)):
        p.add("sync", lambda e, src=src, rows=rows: e.dma_start(out=braw[0:rows, :], in_=src),
              waits=[(pe, pe.n)], incs=[(dmb, 16)])
        w_l = dmb.n
        p.add("tensor", lambda e, rows=rows: e.transpose(pbias[:, 0:rows], braw[0:rows, :],
                                                         id_f32[0:rows, 0:rows]),
              waits=[(dmb, w_l), (pool, g_cst), (dve, None)], incs=[(pe, 1)])
        w_t = pe.n
        p.add("vector", lambda e, rows=rows, dst=dst: e.tensor_copy(dst, pbias[:, 0:rows]),
              waits=[(pe, w_t)], incs=[(dve, 1)])
    p.add("vector", lambda e: e.tensor_scalar(b1s, b1_t, 256.0, None, OP.mult),
          incs=[(dve, 1)])
    p.add("vector", lambda e: e.tensor_scalar(b2s, b2_t, 256.0, None, OP.mult),
          incs=[(dve, 1)])
    d_bias = dve.n

    pv = pvars[:, 0:1]
    pvb = pvars[:, 1:2]
    m64 = pvars[:, 2:3]
    pvh = pvars[:, 3:4]
    pvi = pvi_t
    pvf = pvars[:, 5:6]
    pvd = pvars[:, 6:7]
    pvfl = pvars[:, 7:8]
    m2 = pvars[:, 8:9]
    fi = pvars[:, 9:10]
    freq = pvars[:, 10:11]
    off = pvars[:, 11:12]
    pos_i = xstage[:, 0:1024]
    argt = xstage[:, 1024:2048]
    vvt = xstage[:, 2048:3072]
    posw = xstage[:, 3072:4096]

    p.add("gpsimd", lambda e: e.iota(pv, [[0, 1]], base=0, channel_multiplier=1,
                                     allow_small_or_imprecise_dtypes=True), incs=[(pool, 1)])
    g_iota = pool.n
    p.add("vector", lambda e: e.tensor_scalar(pvb, pv, 64.0, -64.0, OP.is_ge, OP.mult),
          waits=[(pool, g_iota)], incs=[(dve, 1)])
    p.add("vector", lambda e: e.tensor_tensor(m64, pvb, pv, OP.add), incs=[(dve, 1)])
    p.add("vector", lambda e: e.tensor_scalar(pvh, m64, 0.5, None, OP.mult), incs=[(dve, 1)])
    p.add("vector", lambda e: e.tensor_copy(pvi, pvh), incs=[(dve, 1)])
    p.add("vector", lambda e: e.tensor_copy(pvf, pvi), incs=[(dve, 1)])
    p.add("vector", lambda e: e.tensor_tensor(pvd, pvh, pvf, OP.subtract), incs=[(dve, 1)])
    p.add("vector", lambda e: e.scalar_tensor_tensor(pvfl, pvd, 0.0, pvf, OP.is_lt, OP.subtract),
          incs=[(dve, 1)])
    p.add("vector", lambda e: e.scalar_tensor_tensor(m2, pvfl, 2.0, m64, OP.mult, OP.add),
          incs=[(dve, 1)])
    p.add("vector", lambda e: e.tensor_tensor(fi, m64, m2, OP.subtract), incs=[(dve, 1)])
    d_fi = dve.n
    p.add("scalar", lambda e: e.activation(freq, fi, AF.Exp, scale=-LN1e4 / 64.0),
          waits=[(dve, d_fi)], incs=[(act, 1)])
    a_fr = act.n
    p.add("vector", lambda e: e.tensor_scalar(off, m2, PI / 2.0, None, OP.mult), incs=[(dve, 1)])
    for seg in range(4):
        half, r = seg // 2, seg % 2
        p.add("gpsimd", lambda e, r=r: e.iota(pos_i, [[1, 1024]], base=r * 1024,
                                              channel_multiplier=0,
                                              allow_small_or_imprecise_dtypes=True),
              waits=[(dve, dve.n)], incs=[(pool, 1)])
        g_i = pool.n
        p.add("vector", lambda e, half=half: e.tensor_scalar(argt, pos_i, poff[:, half:half + 1],
                                                             freq, OP.add, OP.mult),
              waits=[(act, act.n), (dmp, w_po), (pool, g_i)], incs=[(dve, 1)])
        p.add("vector", lambda e: e.tensor_scalar(argt, argt, off, None, OP.add), incs=[(dve, 1)])
        p.add("vector", lambda e: e.tensor_scalar(vvt, argt, 1.0 / (2 * PI), None, OP.mult),
              incs=[(dve, 1)])
        p.add("vector", lambda e: e.tensor_copy(vvi_t, vvt), incs=[(dve, 1)])
        p.add("vector", lambda e: e.tensor_copy(posw, vvi_t), incs=[(dve, 1)])
        p.add("vector", lambda e: e.tensor_tensor(vvt, vvt, posw, OP.subtract), incs=[(dve, 1)])
        p.add("vector", lambda e: e.scalar_tensor_tensor(posw, vvt, 0.5, posw, OP.is_ge, OP.add),
              incs=[(dve, 1)])
        p.add("vector", lambda e: e.scalar_tensor_tensor(posw, vvt, -0.5, posw, OP.is_lt, OP.subtract),
              incs=[(dve, 1)])
        p.add("vector", lambda e: e.scalar_tensor_tensor(argt, posw, 2 * PI, argt, OP.mult, OP.add),
              incs=[(dve, 1)])
        p.add("vector", lambda e: e.tensor_scalar(argt, argt, PI, -PI, OP.min, OP.max),
              incs=[(dve, 1)])
        d_th = dve.n
        p.add("scalar", lambda e, seg=seg: e.activation(posb[:, seg * 1024:(seg + 1) * 1024],
                                                        argt, AF.Sin),
              waits=[(dve, d_th)], incs=[(act, 1)])
    a_pos = act.n

    def ln_stats(src_tile, scol, waits=()):
        s_ = stats[:, scol + 2:scol + 3]
        q_ = stats[:, scol + 3:scol + 4]
        p.add("vector", lambda e: e.reduce_sum(s_, src_tile, AX.X), waits=list(waits),
              incs=[(dve, 1)])
        p.add("scalar", lambda e: e.activation(sqm, src_tile, AF.Square, accum_out=q_),
              waits=list(waits) + [(dve, dve.n)], incs=[(act, 1)])
        a_q = act.n
        p.add("vector", lambda e: e.tensor_scalar(stats[:, scol:scol + 1], s_, 1.0 / 1024.0,
                                                  None, OP.mult), incs=[(dve, 1)])
        p.add("vector", lambda e: e.tensor_scalar(q_, q_, 1.0 / 1024.0, None, OP.mult),
              waits=[(act, a_q)], incs=[(dve, 1)])
        p.add("vector", lambda e: e.tensor_tensor(s_, stats[:, scol:scol + 1],
                                                  stats[:, scol:scol + 1], OP.mult), incs=[(dve, 1)])
        p.add("vector", lambda e: e.tensor_tensor(q_, q_, s_, OP.subtract), incs=[(dve, 1)])
        d_v = dve.n
        p.add("scalar", lambda e: e.activation(s_, q_, AF.Sqrt, bias=cst[:, 3:4]),
              waits=[(dve, d_v), (pool, g_cst)], incs=[(act, 1)])
        a_s = act.n
        p.add("vector", lambda e: e.reciprocal(stats[:, scol + 1:scol + 2], s_),
              waits=[(act, a_s)], incs=[(dve, 1)])
        return dve.n

    ptr_bf, = psopen([("ptr_bf", [128, 2048], BF16)])
    for t in range(32):
        src = d_xa if t < 16 else d_xb
        tt = t % 16
        par = (t // 2) % 2
        if t % 2 == 0:
            p.add("sync", lambda e, src=src, tt=tt, par=par: e.dma_start(
                out=xstage[:, par * 2048:(par + 1) * 2048].rearrange("p (k d) -> p k d", k=2),
                in_=src.rearrange("(k p) d -> p k d", p=128)[:, tt:tt + 2, :]),
                waits=[(dve, dve.n), (act, act.n)], incs=[(dmx, 16)])
        w_g = dmx.n
        xt = xstage[:, (par * 2 + (t % 2)) * 1024:(par * 2 + (t % 2) + 1) * 1024]
        ln_stats(xt, 0, waits=[(dmx, w_g)])
        p.add("vector", lambda e, xt=xt: e.tensor_scalar(
            ln1tmp_b, xt, stats[:, 0:1], stats[:, 1:2], OP.subtract, OP.mult), incs=[(dve, 1)])
        d_ap = dve.n
        for c in range(8):
            p.add("tensor", lambda e, c=c: e.transpose(
                ptr_bf[:, c * 128:(c + 1) * 128], ln1tmp_b[:, c * 128:(c + 1) * 128], id_bf),
                waits=[(dve, d_ap), (pool, g_cst)], incs=[(pe, 1)])
        w_tr = pe.n
        p.add("vector", lambda e, t=t: e.tensor_copy(
            ln1_fm[:, :, t * 128:(t + 1) * 128],
            ptr_bf[:, 0:1024].rearrange("p (c j) -> p c j", c=8)),
            waits=[(pe, w_tr)], incs=[(dve, 1)])
    psclose()
    d_ln1 = dve.n

    sqq_t = attn_un
    wq_b0 = wqkvb

    def qkv_slab_dma(pair, slab_free_pe):
        buf = (pair % 2) * 3072
        ch = wch[pair % 2]
        for j in range(3):
            col0 = j * 1024 + pair * 128
            p.add("gpsimd", lambda e, buf=buf, j=j, col0=col0: e.dma_start(
                out=wqkvb[:, buf + j * 1024:buf + (j + 1) * 1024].rearrange(
                    "p (k j2) -> p k j2", k=8),
                in_=d_wqkv[:, col0:col0 + 128].rearrange("(k p) j2 -> p k j2", p=128)),
                waits=[(pe, slab_free_pe)], incs=[(ch, 16)])
        return ch.n

    slab_ready = [0] * NPAIR
    slab_free = [0, 0]
    slab_ready[0] = qkv_slab_dma(0, 0)
    slab_ready[1] = qkv_slab_dma(1, 0)

    exp_act = {}
    s_pe = {}
    av_pe = {}
    gs = 0
    av_last_prev = 0
    for pair in range(NPAIR):
        par = pair % 2
        wq_b = wqkvb[:, par * 3072:par * 3072 + 1024]
        wk_b = wqkvb[:, par * 3072 + 1024:par * 3072 + 2048]
        wv_b = wqkvb[:, par * 3072 + 2048:par * 3072 + 3072]
        d_pair_prev = dve.n

        PPq, PVT, PCS = psopen([(f"PP{pair}", [128, 1024], F32),
                                (f"PVT{pair}", [128, 2048], BF16),
                                (f"PCS{pair}", [128, 2048], F32)])
        slot_evac = [0, 0]
        for qt in range(4):
            pa = qt % 2
            dst = PPq[:, pa * 512:(pa + 1) * 512]
            for kc in range(8):
                p.add("tensor", lambda e, kc=kc, qt=qt, dst=dst, wq_b=wq_b: e.matmul(
                    dst, wq_b[:, kc * 128:(kc + 1) * 128],
                    ln1_fm[:, kc, qt * 512:(qt + 1) * 512],
                    start=(kc == 0), stop=(kc == 7)),
                    waits=[(wch[pair % 2], slab_ready[pair] if kc == 0 else 0),
                           (dve, (max(d_pair_prev, d_ln1) if qt == 0 else 0) if kc == 0 else 0),
                           (act, slot_evac[pa] if kc == 0 else 0)],
                    incs=[(pe, 1)] if kc == 7 else ())
            w_mm = pe.n
            p.add("scalar", lambda e, qt=qt, dst=dst, pair=pair: e.activation(
                q_pair[:, qt * 512:(qt + 1) * 512], dst, AF.Identity,
                bias=bq_t[:, pair:pair + 1], scale=1.0),
                waits=[(pe, w_mm), (dve, d_bias)], incs=[(act, 1)])
            slot_evac[pa] = act.n
        for which in ("k", "v"):
            wb = wk_b if which == "k" else wv_b
            bcol = 8 + pair if which == "k" else 16 + pair
            dstt = k_pair if which == "k" else v_fm
            for ktile in range(8):
                pa = ktile % 2
                dst = PPq[:, pa * 512:(pa + 1) * 512]
                for kc in range(8):
                    p.add("tensor", lambda e, kc=kc, ktile=ktile, dst=dst, wb=wb,
                          which=which: e.matmul(
                        dst, wb[:, kc * 128:(kc + 1) * 128],
                        ln1_fm[:, kc, ktile * 512:(ktile + 1) * 512],
                        start=(kc == 0), stop=(kc == 7 and which == "v")),
                        waits=[(act, slot_evac[pa] if kc == 0 else 0)],
                        incs=[(pe, 1)] if (kc == 7 and which == "v") else ())
                if which == "k":
                    p.add("tensor", lambda e, ktile=ktile, dst=dst: e.matmul(
                        dst, id_bf, posb[:, ktile * 512:(ktile + 1) * 512],
                        start=False, stop=True),
                        waits=[(act, a_pos), (pool, g_cst)], incs=[(pe, 1)])
                w_mm = pe.n
                p.add("scalar", lambda e, ktile=ktile, dst=dst, dstt=dstt, bcol=bcol: e.activation(
                    dstt[:, ktile * 512:(ktile + 1) * 512], dst, AF.Identity,
                    bias=bq_t[:, bcol:bcol + 1], scale=1.0),
                    waits=[(pe, w_mm)], incs=[(act, 1)])
                slot_evac[pa] = act.n
        a_qkv = act.n
        slab_free[par] = pe.n
        if pair + 2 < NPAIR:
            slab_ready[pair + 2] = qkv_slab_dma(pair + 2, slab_free[par])

        for rnd in range(2):
            for j in range(16):
                kc = rnd * 16 + j
                p.add("tensor", lambda e, kc=kc, j=j: e.transpose(
                    PVT[:, j * 128:(j + 1) * 128], v_fm[:, kc * 128:(kc + 1) * 128], id_bf),
                    waits=[(act, a_qkv if j == 0 else 0),
                           (dve, (d_pair_prev if rnd == 0 else dve.n) if j == 0 else 0)],
                    incs=[(pe, 1)])
            w_vt = pe.n
            for (dst, lo) in ((vtmA, 0), (vtmB, 64)):
                p.add("vector", lambda e, rnd=rnd, dst=dst, lo=lo: e.tensor_copy(
                    dst.rearrange("p (c j) -> p c j", c=32)[:, rnd * 16:(rnd + 1) * 16, 0:64],
                    PVT.rearrange("p (c j) -> p c j", c=16)[:, :, lo:lo + 64]),
                    waits=[(pe, w_vt), (pe, av_last_prev)], incs=[(dve, 1)])
        d_vtm = dve.n

        p.add("scalar", lambda e: e.activation(sqq_t, q_pair, AF.Square),
              waits=[(dve, dve.n), (dmm, dmm.n), (pe, av_last_prev)], incs=[(act, 1)])
        p.add("scalar", lambda e: e.activation(sqk_t, k_pair, AF.Square), incs=[(act, 1)])
        a_sq = act.n
        for j in range(4):
            p.add("tensor", lambda e, j=j: e.matmul(
                PCS[0:1, j * 512:(j + 1) * 512], ones_bf, sqq_t[:, j * 512:(j + 1) * 512],
                start=True, stop=True),
                waits=[(act, a_sq), (pool, g_cst)], incs=[(pe, 1)])
        w_cs = pe.n
        p.add("vector", lambda e: e.reduce_max(csst[:, 0:1], PCS[0:1, 0:2048], AX.X),
              waits=[(pe, w_cs)], incs=[(dve, 1)])
        d_cs = dve.n
        for j in range(4):
            p.add("tensor", lambda e, j=j: e.matmul(
                PCS[0:1, j * 512:(j + 1) * 512], ones_bf, sqk_t[:, j * 512:(j + 1) * 512],
                start=True, stop=True),
                waits=[(dve, d_cs)], incs=[(pe, 1)])
        w_cs = pe.n
        p.add("vector", lambda e: e.reduce_max(csst[:, 1:2], PCS[0:1, 0:2048], AX.X),
              waits=[(pe, w_cs)], incs=[(dve, 1)])
        d_cs = dve.n
        for j in range(4):
            p.add("tensor", lambda e, j=j: e.matmul(
                PCS[0:1, j * 512:(j + 1) * 512], ones_bf,
                sqk_t[:, 2048 + j * 512:2048 + (j + 1) * 512],
                start=True, stop=True),
                waits=[(dve, d_cs)], incs=[(pe, 1)])
        w_cs = pe.n
        p.add("vector", lambda e: e.reduce_max(csst[:, 2:3], PCS[0:1, 0:2048], AX.X),
              waits=[(pe, w_cs)], incs=[(dve, 1)])
        p.add("vector", lambda e: e.tensor_tensor(csst[:, 1:2], csst[:, 1:2], csst[:, 2:3], OP.max),
              incs=[(dve, 1)])
        p.add("vector", lambda e: e.tensor_tensor(csst[:, 3:4], csst[:, 0:1], csst[:, 1:2], OP.mult),
              incs=[(dve, 1)])
        d_pr = dve.n
        p.add("scalar", lambda e: e.activation(csst[:, 4:5], csst[:, 3:4], AF.Sqrt, scale=1.0 / 64.0),
              waits=[(dve, d_pr)], incs=[(act, 1)])
        a_c = act.n
        p.add("vector", lambda e: e.tensor_scalar(csst[:, 5:6], csst[:, 4:5], -1.0, None, OP.mult),
              waits=[(act, a_c)], incs=[(dve, 1)])
        d_nc0 = dve.n
        p.add("tensor", lambda e: e.matmul(PCS[:, 0:1], ones_f, csst[:, 5:6],
                                           start=True, stop=True),
              waits=[(dve, d_nc0)], incs=[(pe, 1)])
        w_bc = pe.n
        p.add("vector", lambda e: e.tensor_copy(negc_b, PCS[:, 0:1]),
              waits=[(pe, w_bc)], incs=[(dve, 1)])
        d_nc = dve.n
        psclose()

        SA, AVp = psopen([(f"SA{pair}", [128, 2048], F32), (f"AV{pair}", [128, 2048], F32)])
        d_avfree = [dve.n, dve.n]

        def emit_S2(qhalf, s):
            head, kc = s % 2, s // 2
            base = head * 64
            g = gs + s
            sa = SA[:, (s % 2) * 1024:(s % 2 + 1) * 1024]
            for q2 in range(2):
                qt = qhalf * 2 + q2
                p.add("tensor", lambda e, kc=kc, qt=qt, q2=q2, base=base, sa=sa: e.matmul(
                    sa[:, q2 * 512:(q2 + 1) * 512],
                    k_pair[base:base + 64, kc * 128:(kc + 1) * 128],
                    q_pair[base:base + 64, qt * 512:(qt + 1) * 512],
                    start=True, stop=True, tile_position=(base, 0)),
                    waits=[(act, max(exp_act.get(g - 2, 0), a_qkv) if q2 == 0 else 0),
                           (dve, d_nc if q2 == 0 else 0)],
                    incs=[(pe, 1)] if q2 == 1 else ())
            s_pe[g] = pe.n

        def emit_AV(s):
            g = gs + s
            head, kc = s % 2, s // 2
            vtm = vtmA if head == 0 else vtmB
            ptb = PTr[:, (g % 4) * 1024:(g % 4 + 1) * 1024]
            for q2 in range(2):
                p.add("tensor", lambda e, kc=kc, head=head, q2=q2, vtm=vtm,
                      ptb=ptb: e.matmul(
                    AVp[0:65, head * 1024 + q2 * 512:head * 1024 + (q2 + 1) * 512],
                    vtm[:, kc * 65:kc * 65 + 65],
                    ptb[:, q2 * 512:(q2 + 1) * 512],
                    start=(kc == 0), stop=(kc == 31)),
                    waits=[(act, exp_act[g] if q2 == 0 else 0),
                           (dve, d_avfree[head] if (kc == 0 and q2 == 0) else 0)],
                    incs=[(pe, 1)] if q2 == 1 else ())
            av_pe[g] = pe.n

        for qhalf in range(2):
            emit_S2(qhalf, 0)
            for s in range(64):
                g = gs + s
                if s < 63:
                    emit_S2(qhalf, s + 1)
                sa = SA[:, (s % 2) * 1024:(s % 2 + 1) * 1024]
                ptb = PTr[:, (g % 4) * 1024:(g % 4 + 1) * 1024]
                p.add("scalar", lambda e, sa=sa, ptb=ptb: e.activation(
                    ptb, sa, AF.Exp, bias=negc_b, scale=0.125),
                    waits=[(pe, max(s_pe[g], av_pe.get(g - 4, 0))), (dve, d_nc)],
                    incs=[(act, 1)])
                exp_act[g] = act.n
                if s >= 1:
                    emit_AV(s - 1)
            emit_AV(63)
            gs += 64
            for head in range(2):
                w_av = av_pe[gs - 2 + head]
                hr = slice(head * 1024, (head + 1) * 1024)
                p.add("vector", lp(lambda e, hr=hr: e.reciprocal(
                    rc_t[0:1, hr], AVp[64:65, hr])),
                    waits=[(pe, w_av)], incs=[(dve, 1)])
                p.add("vector", lambda e, hr=hr: e.tensor_copy(
                    attn_un[0:64, hr], AVp[0:64, hr]), incs=[(dve, 1)])
                d_ev = dve.n
                for qh in range(2):
                    p.add("tensor", lambda e, head=head, qh=qh: e.matmul(
                        AVp[0:64, head * 1024 + qh * 512:head * 1024 + (qh + 1) * 512],
                        onesr_bf[:, 0:64],
                        rc_t[0:1, head * 1024 + qh * 512:head * 1024 + (qh + 1) * 512],
                        start=True, stop=True),
                        waits=[(dve, d_ev)], incs=[(pe, 1)])
                w_b = pe.n
                if head == 0:
                    p.add("vector", lambda e, hr=hr, pair=pair, qhalf=qhalf: e.tensor_tensor(
                        attn_fm[0:64, pair, qhalf * 1024:(qhalf + 1) * 1024],
                        attn_un[0:64, hr], AVp[0:64, hr], OP.mult),
                        waits=[(pe, w_b)], incs=[(dve, 1)])
                else:
                    p.add("vector", lambda e, hr=hr: e.tensor_tensor(
                        attn_un[0:64, hr], attn_un[0:64, hr], AVp[0:64, hr], OP.mult),
                        waits=[(pe, w_b)], incs=[(dve, 1)])
                d_avfree[head] = dve.n
            d_nm = dve.n
            p.add("sync", lambda e, pair=pair, qhalf=qhalf: e.dma_start(
                out=attn_fm[64:128, pair, qhalf * 1024:(qhalf + 1) * 1024],
                in_=attn_un[0:64, 1024:2048]),
                waits=[(dve, d_nm)], incs=[(dmm, 16)])
        av_last_prev = av_pe[gs - 1]
        psclose()
    d_attn = dve.n
    w_attnmv = dmm.n

    psclose()
    stLN1.close()

    PO, PT2 = psopen([("PO", [128, 2048], F32), ("PT2", [128, 2048], BF16)])
    outattnr = sbin(stOP, "outattnr", [128, 8 * 2048], BF16)
    outattn = outattnr.rearrange("p (c t) -> p c t", c=8)

    def wout_dma(dm, free_pe):
        ch = wch[dm % 2]
        p.add("gpsimd", lambda e, dm=dm: e.dma_start(
            out=wqkvb[:, (dm % 2) * 1024:(dm % 2) * 1024 + 1024].rearrange(
                "p (k j) -> p k j", k=8),
            in_=d_wout[:, dm * 128:(dm + 1) * 128].rearrange("(k p) j -> p k j", p=128)),
            waits=[(pe, free_pe)], incs=[(ch, 16)])
        return ch.n

    wo_ready = [0] * 8
    wo_mm_done = [0, 0]
    wo_ready[0] = wout_dma(0, av_last_prev)
    wo_ready[1] = wout_dma(1, av_last_prev)
    for dm in range(8):
        wo_b = wqkvb[:, (dm % 2) * 1024:(dm % 2) * 1024 + 1024]
        slot_evac = [0, 0]
        for qt in range(4):
            dst = PO[:, (qt % 2) * 512:(qt % 2 + 1) * 512]
            for hc in range(8):
                p.add("tensor", lambda e, hc=hc, qt=qt, dst=dst, wo_b=wo_b: e.matmul(
                    dst, wo_b[:, hc * 128:(hc + 1) * 128],
                    attn_fm[:, hc, qt * 512:(qt + 1) * 512],
                    start=(hc == 0), stop=(hc == 7)),
                    waits=[(wch[dm % 2], wo_ready[dm] if (hc == 0 and qt == 0) else 0),
                           (dve, (d_attn if (dm == 0 and qt == 0) else 0) if hc == 0 else 0),
                           (dmm, (w_attnmv if (dm == 0 and qt == 0) else 0) if hc == 0 else 0),
                           (act, slot_evac[qt % 2] if hc == 0 else 0)],
                    incs=[(pe, 1)] if hc == 7 else ())
            w_mm = pe.n
            p.add("scalar", lambda e, dm=dm, qt=qt, dst=dst: e.activation(
                outattn[:, dm, qt * 512:(qt + 1) * 512], dst, AF.Identity,
                bias=bo_t[:, dm:dm + 1], scale=1.0),
                waits=[(pe, w_mm)], incs=[(act, 1)])
            slot_evac[qt % 2] = act.n
        wo_mm_done[dm % 2] = pe.n
        if dm + 2 < 8:
            wo_ready[dm + 2] = wout_dma(dm + 2, wo_mm_done[dm % 2])
    a_op = act.n

    at_tm = sbin(stOP, "at_tm", [128, 1024], BF16)
    x_tile = sbin(stOP, "x_tile", [128, 1024], F32)
    x2_tile = sbin(stOP, "x2_tile", [128, 1024], F32)
    x2w = [0] * 16
    for t in range(16):
        p.add("sync", lambda e, t=t: e.dma_start(
            out=x_tile, in_=d_xa.rearrange("(k p) d -> p k d", p=128)[:, t:t + 1, :]),
            waits=[(dve, dve.n)], incs=[(dmx, 16)])
        w_x = dmx.n
        for c in range(8):
            p.add("tensor", lambda e, c=c, t=t: e.transpose(
                PT2[:, c * 128:(c + 1) * 128], outattn[:, c, t * 128:(t + 1) * 128], id_bf),
                waits=[(act, a_op if c == 0 else 0), (dve, dve.n if c == 0 else 0)],
                incs=[(pe, 1)])
        w_tr = pe.n
        p.add("vector", lambda e: e.tensor_copy(at_tm.rearrange("p (c j) -> p c j", c=8),
                                                PT2[:, 0:1024].rearrange("p (c j) -> p c j", c=8)),
              waits=[(pe, w_tr)], incs=[(dve, 1)])
        ln_stats(at_tm, 0)
        p.add("vector", lambda e: e.tensor_scalar(at_tm, at_tm, stats[:, 0:1], stats[:, 1:2],
                                                  OP.subtract, OP.mult), incs=[(dve, 1)])
        p.add("vector", lambda e: e.tensor_tensor(x2_tile, x_tile, at_tm, OP.add),
              waits=[(dmx, w_x), (dmz, dmz.n)], incs=[(dve, 1)])
        d_x2c = dve.n
        p.add("sync", lambda e, t=t: e.dma_start(
            out=d_x2.rearrange("(k p) d -> p k d", p=128)[:, t:t + 1, :], in_=x2_tile),
            waits=[(dve, d_x2c)], incs=[(dmz, 16)])
        x2w[t] = dmz.n
    psclose()
    x2w_all = dmz.n

    stOP.close()
    stAtt.close()
    ln3r = sbin(stFFN, "ln3r", [128, 8 * 1024], BF16)
    ln3_fm = ln3r.rearrange("p (c t) -> p c t", c=8)
    hq_r = sbin(stFFN, "hq_r", [128, 32 * 1024], BF16)
    hq_sb = hq_r.rearrange("p (f t) -> p f t", f=32)
    x2s = sbin(stFFN, "x2s", [128, 1024], F32)
    ln3tmp_b = sbin(stFFN, "ln3tmp_b", [128, 1024], BF16)
    t_hi = sbin(stFFN, "t_hi", [128, 2048], I32)
    t_u = sbin(stFFN, "t_u", [128, 2048], I32)
    t_rf = sbin(stFFN, "t_rf", [128, 2048], F32)
    t_g0 = sbin(stFFN, "t_g0", [128, 2048], F32)
    t_g1 = sbin(stFFN, "t_g1", [128, 2048], F32)
    t_dg = sbin(stFFN, "t_dg", [128, 1024], F32)
    t_tdg = sbin(stFFN, "t_tdg", [128, 1024], F32)
    t_oi = sbin(stFFN, "t_oi", [128, 1024], I32)
    w1b = sbin(stFFN, "w1b", [128, 2048], BF16)
    w2b = sbin(stFFN, "w2b", [128, 1024], BF16)
    out_f = sbin(stFFN, "out_f", [128, 8 * 1024], F32)
    out_fm = out_f.rearrange("p (c t) -> p c t", c=8)
    fout_tm = sbin(stFFN, "fout_tm", [128, 1024], F32)

    hi_done = [0, 0]
    dg_done = [0, 0]
    dgd = {-1: 0, -2: 0}
    tdgd = {-1: 0, -2: 0}
    yd = {-1: 0, -2: 0}
    yid = {-1: 0}
    g1d = {-1: 0, -2: 0}
    hbfd = {-1: 0, -2: 0}
    hbf_act = {}

    for T in range(2):
        PTl, = psopen([(f"PTl{T}", [128, 2048], BF16)])
        for j in range(8):
            t = T * 8 + j
            p.add("sync", lambda e, t=t: e.dma_start(
                out=x2s, in_=d_x2.rearrange("(k p) d -> p k d", p=128)[:, t:t + 1, :]),
                waits=[(dve, dve.n), (dmz, x2w_all)], incs=[(dmx, 16)])
            w_x = dmx.n
            ln_stats(x2s, 0, waits=[(dmx, w_x)])
            p.add("vector", lambda e: e.tensor_scalar(ln3tmp_b, x2s, stats[:, 0:1],
                                                      stats[:, 1:2], OP.subtract, OP.mult),
                  incs=[(dve, 1)])
            d_ap = dve.n
            po = (j % 2) * 1024
            for c in range(8):
                p.add("tensor", lambda e, c=c, po=po: e.transpose(
                    PTl[:, po + c * 128:po + (c + 1) * 128],
                    ln3tmp_b[:, c * 128:(c + 1) * 128], id_bf),
                    waits=[(dve, d_ap if c == 0 else 0)],
                    incs=[(pe, 1)] if c == 7 else ())
            w_tr = pe.n
            p.add("vector", lambda e, j=j, po=po: e.tensor_copy(
                ln3_fm[:, :, j * 128:(j + 1) * 128],
                PTl[:, po:po + 1024].rearrange("p (c j2) -> p c j2", c=8)),
                waits=[(pe, w_tr)], incs=[(dve, 1)])
        psclose()
        d_ln3 = dve.n

        PH, = psopen([(f"PH{T}", [128, 2048], F32)])
        w1_mm_done = [pe.n, pe.n]

        def w1_dma(f, par):
            ch = wch[par]
            p.add("gpsimd", lambda e, f=f, par=par: e.dma_start(
                out=w1b[:, par * 1024:(par + 1) * 1024].rearrange("p (k j) -> p k j", k=8),
                in_=d_w1[:, f * 128:(f + 1) * 128].rearrange("(k p) j -> p k j", p=128)),
                waits=[(pe, w1_mm_done[par])], incs=[(ch, 16)])
            return ch.n

        w1_ready = [0] * 34
        w1_ready[0] = w1_dma(0, 0)
        w1_ready[1] = w1_dma(1, 1)
        for f in range(32):
            par = f % 2
            ph = PH[:, par * 1024:(par + 1) * 1024]
            wcol = par * 1024
            hi = t_hi[:, par * 1024:(par + 1) * 1024]
            u = t_u[:, par * 1024:(par + 1) * 1024]
            rf = t_rf[:, par * 1024:(par + 1) * 1024]
            g0 = t_g0[:, par * 1024:(par + 1) * 1024]
            g1 = t_g1[:, par * 1024:(par + 1) * 1024]
            for kc in range(8):
                for h in range(2):
                    p.add("tensor", lambda e, kc=kc, h=h, ph=ph, wcol=wcol: e.matmul(
                        ph[:, h * 512:(h + 1) * 512],
                        w1b[:, wcol + kc * 128:wcol + (kc + 1) * 128],
                        ln3_fm[:, kc, h * 512:(h + 1) * 512],
                        start=(kc == 0), stop=(kc == 7)),
                        waits=[(wch[par], w1_ready[f] if (kc == 0 and h == 0) else 0),
                               (dve, max(d_ln3, hi_done[par]) if (kc == 0 and h == 0) else 0)],
                        incs=[(pe, 1)] if (kc == 7 and h == 1) else ())
            w1_mm_done[par] = pe.n
            w_mm = pe.n
            if f + 2 < 32:
                w1_ready[f + 2] = w1_dma(f + 2, par)
            p.add("vector", lambda e, ph=ph, hi=hi, f=f: e.tensor_scalar(
                hi, ph, 256.0, b1s[:, f:f + 1], OP.mult, OP.add),
                waits=[(pe, w_mm), (dve, d_bias), (act, hbfd[f - 2])], incs=[(dve, 1)])
            hi_done[par] = dve.n
            p.add("vector", lambda e, u=u, hi=hi: e.tensor_scalar(
                u, hi, 6, None, OP.arith_shift_right), incs=[(dve, 1)])
            d_u = dve.n
            p.add("vector", lambda e, rf=rf, u=u, hi=hi: e.scalar_tensor_tensor(
                rf, u, -64.0, hi, OP.mult, OP.add), incs=[(dve, 1)])
            if f >= 1:
                parp = (f - 1) % 2
                rfp = t_rf[:, parp * 1024:(parp + 1) * 1024]
                g0p = t_g0[:, parp * 1024:(parp + 1) * 1024]
                hip = t_hi[:, parp * 1024:(parp + 1) * 1024]
                p.add("vector", lambda e, rfp=rfp: e.tensor_tensor(
                    t_tdg, rfp, t_dg, OP.mult),
                    waits=[(dve, dgd[f - 1])], incs=[(dve, 1)])
                tdgd[f - 1] = dve.n
                p.add("vector", lambda e, rfp=rfp, g0p=g0p: e.scalar_tensor_tensor(
                    rfp, t_tdg, 0.015625, g0p, OP.mult, OP.add), incs=[(dve, 1)])
                yd[f - 1] = dve.n
                p.add("vector", lambda e, rfp=rfp, hip=hip: e.tensor_scalar(
                    hip, rfp, 256.0, None, OP.mult), incs=[(dve, 1)])
                yid[f - 1] = dve.n
            p.add("scalar", lambda e, g0=g0, u=u: e.activation(g0, u, AF.Gelu, scale=0.25),
                  waits=[(dve, max(d_u, dgd[f - 2], yd[f - 2]))], incs=[(act, 1)])
            p.add("scalar", lambda e, g1=g1, u=u: e.activation(g1, u, AF.Gelu,
                                                               scale=0.25, bias=cst[:, 4:5]),
                  incs=[(act, 1)])
            g1d[f] = act.n
            if f >= 1:
                parp = (f - 1) % 2
                hip = t_hi[:, parp * 1024:(parp + 1) * 1024]
                p.add("scalar", lambda e, hip=hip, fp=f - 1: e.activation(
                    hq_sb[:, fp, :], hip, AF.Identity, scale=0.00390625),
                    waits=[(dve, yid[f - 1])], incs=[(act, 1)])
                hbfd[f - 1] = act.n
                hbf_act[(T, f - 1)] = act.n
            p.add("vector", lambda e, g0=g0, g1=g1: e.tensor_tensor(t_dg, g1, g0, OP.subtract),
                  waits=[(act, g1d[f])], incs=[(dve, 1)])
            dgd[f] = dve.n
        parp = 31 % 2
        rfp = t_rf[:, parp * 1024:(parp + 1) * 1024]
        g0p = t_g0[:, parp * 1024:(parp + 1) * 1024]
        hip = t_hi[:, parp * 1024:(parp + 1) * 1024]
        p.add("vector", lambda e, rfp=rfp: e.tensor_tensor(t_tdg, rfp, t_dg, OP.mult),
              waits=[(dve, dgd[31])], incs=[(dve, 1)])
        tdgd[31] = dve.n
        p.add("vector", lambda e, rfp=rfp, g0p=g0p: e.scalar_tensor_tensor(
            rfp, t_tdg, 0.015625, g0p, OP.mult, OP.add), incs=[(dve, 1)])
        yd[31] = dve.n
        p.add("vector", lambda e, rfp=rfp, hip=hip: e.tensor_scalar(
            hip, rfp, 256.0, None, OP.mult), incs=[(dve, 1)])
        yid[31] = dve.n
        p.add("scalar", lambda e, hip=hip: e.activation(
            hq_sb[:, 31, :], hip, AF.Identity, scale=0.00390625),
            waits=[(dve, yid[31])], incs=[(act, 1)])
        hbfd[31] = act.n
        hbf_act[(T, 31)] = act.n
        psclose()
        d_chain_end = max(hi_done[0], hi_done[1], yid[31])

        PW8r, = psopen([(f"PW8{T}", [128, 4096], F32)])
        PW8 = PW8r
        oi_all = 0
        outff_hist = [0] * 10
        w2_mm_done = [0] * 34
        for dmh in range(2):
            w2_mm_done[0] = w2_mm_done[1] = pe.n

            def w2_dma(fc, par, dmh=dmh):
                ch = wch[par]
                p.add("gpsimd", lambda e, fc=fc, par=par, dmh=dmh: e.dma_start(
                    out=w2b[:, par * 512:(par + 1) * 512],
                    in_=d_w2.rearrange("(c p) d -> p c d", p=128)[:, fc, dmh * 512:(dmh + 1) * 512]),
                    waits=[(pe, w2_mm_done[fc])], incs=[(ch, 16)])
                return ch.n

            w2_ready = [0] * 34
            w2_ready[0] = w2_dma(0, 0)
            w2_ready[1] = w2_dma(1, 1)
            for fc in range(32):
                par = fc % 2
                for dmc in range(4):
                    for tb in range(2):
                        bank = tb * 4 + dmc
                        p.add("tensor", lambda e, fc=fc, dmc=dmc, tb=tb, bank=bank, par=par: e.matmul(
                            PW8[:, bank * 512:(bank + 1) * 512],
                            w2b[:, par * 512 + dmc * 128:par * 512 + (dmc + 1) * 128],
                            hq_sb[:, fc, tb * 512:(tb + 1) * 512],
                            start=(fc == 0), stop=(fc == 31)),
                            waits=[(wch[fc % 2], w2_ready[fc] if (dmc == 0 and tb == 0) else 0),
                                   (act, hbf_act[(T, fc)] if (dmc == 0 and tb == 0) else 0),
                                   (dve, (d_chain_end if dmh == 0 else oi_all)
                                        if (fc == 0 and dmc == 0 and tb == 0) else 0)],
                            incs=[(pe, 1)] if (dmc == 3 and tb == 1) else ())
                w2_mm_done[fc + 2] = pe.n
                if fc + 2 < 32:
                    w2_ready[fc + 2] = w2_dma(fc + 2, par)
            w_w2 = pe.n
            for k in range(8):
                tb, dmc = k // 4, k % 4
                dm = dmh * 4 + dmc
                bank = tb * 4 + dmc
                ois = t_oi[:, (k % 2) * 512:(k % 2 + 1) * 512]
                p.add("vector", lambda e, bank=bank, dm=dm, ois=ois: e.tensor_scalar(
                    ois, PW8[:, bank * 512:(bank + 1) * 512], 256.0, b2s[:, dm:dm + 1],
                    OP.mult, OP.add),
                    waits=[(pe, w_w2), (act, outff_hist[k])], incs=[(dve, 1)])
                d_oi = dve.n
                p.add("scalar", lambda e, dm=dm, tb=tb, ois=ois: e.activation(
                    out_fm[:, dm, tb * 512:(tb + 1) * 512], ois, AF.Identity,
                    scale=0.00390625),
                    waits=[(dve, d_oi)], incs=[(act, 1)])
                outff_hist[k + 2] = act.n
            oi_all = dve.n
        a_of = act.n

        for j in range(8):
            t = T * 8 + j
            p.add("sync", lambda e, t=t: e.dma_start(
                out=x2s, in_=d_x2.rearrange("(k p) d -> p k d", p=128)[:, t:t + 1, :]),
                waits=[(dve, dve.n)], incs=[(dmx, 16)])
            w_x = dmx.n
            po = (j % 2) * 1024
            for c in range(8):
                p.add("tensor", lambda e, c=c, j=j, po=po: e.transpose(
                    PW8[:, po + c * 128:po + (c + 1) * 128],
                    out_fm[:, c, j * 128:(j + 1) * 128], id_f32),
                    waits=[(act, a_of if c == 0 else 0),
                           (dve, (oi_all if j == 0 else dve.n) if c == 0 else 0)],
                    incs=[(pe, 1)] if c == 7 else ())
            w_tr = pe.n
            p.add("vector", lambda e, po=po: e.tensor_copy(
                fout_tm.rearrange("p (c j2) -> p c j2", c=8),
                PW8[:, po:po + 1024].rearrange("p (c j2) -> p c j2", c=8)),
                waits=[(pe, w_tr), (dmo, dmo.n)], incs=[(dve, 1)])
            ln_stats(fout_tm, 0)
            p.add("vector", lambda e: e.tensor_scalar(fout_tm, fout_tm, stats[:, 0:1],
                                                      stats[:, 1:2], OP.subtract, OP.mult),
                  incs=[(dve, 1)])
            p.add("vector", lambda e: e.tensor_tensor(fout_tm, fout_tm, x2s, OP.add),
                  waits=[(dmx, w_x)], incs=[(dve, 1)])
            d_o = dve.n
            p.add("sync", lambda e, t=t: e.dma_start(
                out=d_out.rearrange("(k p) d -> p k d", p=128)[:, t:t + 1, :], in_=fout_tm),
                waits=[(dve, d_o)], incs=[(dmo, 16)])
        psclose()
    stFFN.close()

    p.add("sync", lambda e: None, waits=[(dmo, dmo.n)])

    with nc.Block() as block:
        p.emit(block)
    st.close()
    return nc


_NC_CACHE = None
_LAST_RES = None


def kernel(L0_attn_x, L0_attn_W_qkv, L0_attn_b_qkv, L0_attn_W_out, L0_attn_b_out,
           L0_ffn_w1, L0_ffn_b1, L0_ffn_w2, L0_ffn_b2):
    global _NC_CACHE
    x = np.ascontiguousarray(L0_attn_x, dtype=np.float32)
    B, SQ, D = x.shape
    if _NC_CACHE is None:
        _NC_CACHE = build_nc()
    nc = _NC_CACHE

    wqkv = np.ascontiguousarray(L0_attn_W_qkv, np.float32)
    bqkv = np.ascontiguousarray(L0_attn_b_qkv, np.float32).reshape(24, 128)
    wout = np.ascontiguousarray(L0_attn_W_out, np.float32)
    bout = np.ascontiguousarray(L0_attn_b_out, np.float32).reshape(8, 128)
    w1 = np.ascontiguousarray(L0_ffn_w1, np.float32)
    b1 = np.ascontiguousarray(L0_ffn_b1, np.float32).reshape(32, 128)
    w2 = np.ascontiguousarray(L0_ffn_w2, np.float32)
    b2 = np.ascontiguousarray(L0_ffn_b2, np.float32).reshape(8, 128)

    in_maps = []
    for core in range(8):
        b = core // 2
        half = core % 2
        own = x[b, half * 2048:(half + 1) * 2048, :]
        oth = x[b, (1 - half) * 2048:(2 - half) * 2048, :]
        po = np.empty((128, 2), np.float32)
        po[:, 0] = half * 2048.0
        po[:, 1] = (1 - half) * 2048.0
        in_maps.append({
            "xa": own, "xb": oth, "wqkv": wqkv, "bqkv": bqkv, "wout": wout,
            "bout": bout, "w1": w1, "b1": b1, "w2": w2, "b2": b2, "poff": po,
        })
    res = run_bass_kernel_spmd(nc, in_maps, core_ids=list(range(8)))
    global _LAST_RES
    _LAST_RES = res
    out = np.empty((B, SQ, D), np.float32)
    for core in range(8):
        b = core // 2
        half = core % 2
        out[b, half * 2048:(half + 1) * 2048, :] = res.results[core]["out"]
    return out


# version 18 (final)
# speedup vs baseline: 1.9620x; 1.0025x over previous
import numpy as np
from contextlib import ExitStack

import concourse.bass as bass
import concourse.mybir as mybir
from concourse.bass_utils import run_bass_kernel_spmd

F32 = mybir.dt.float32
F32R = mybir.dt.float32r
BF16 = mybir.dt.bfloat16
I32 = mybir.dt.int32
AF = mybir.ActivationFunctionType
OP = mybir.AluOpType
AX = mybir.AxisListType

LN1e4 = float(np.log(10000.0))
PI = float(np.pi)

NT = 2048
S = 4096
DM = 1024
FF = 4096
NPAIR = 8


class Ch:
    __slots__ = ("sem", "n", "name")

    def __init__(self, sem, name=""):
        self.sem = sem
        self.n = 0
        self.name = name


class Prog:
    ENGINES = ("sync", "gpsimd", "tensor", "scalar", "vector")
    SELF_WAIT = ("gpsimd", "scalar", "vector")

    def __init__(self, chans):
        self.steps = []
        self.chans = chans

    def add(self, engine, fn, waits=(), incs=()):
        waits = [(c, c.n if v is None else v) for (c, v) in waits]
        selfch = self.chans.get(engine)
        snap = selfch.n if selfch is not None else 0
        st = (engine, fn, waits, [], snap)
        self.steps.append(st)
        for (c, k) in incs:
            c.n += k
            st[3].append((c, k))
        return st

    def barrier(self, engines=("sync", "gpsimd", "tensor", "scalar", "vector")):
        snap = [(c, c.n) for c in set(self.chans.values()) if c is not None]
        for eng in engines:
            self.add(eng, lambda e: None, waits=list(snap))

    def emit(self, block):
        for engine in self.ENGINES:
            mysteps = [s for s in self.steps if s[0] == engine]
            if not mysteps:
                continue
            selfch = self.chans.get(engine)

            def section(e, mysteps=mysteps, engine=engine, selfch=selfch):
                done = {}
                for (_, fn, waits, incs, snap) in mysteps:
                    allw = list(waits)
                    if engine in self.SELF_WAIT and selfch is not None and snap > 0:
                        allw.append((selfch, snap))
                    for (c, v) in allw:
                        if v <= 0:
                            continue
                        if done.get(id(c), 0) >= v:
                            continue
                        e.wait_ge(c.sem, v)
                        done[id(c)] = v
                    inst = fn(e)
                    if inst is not None:
                        for (c, k) in incs:
                            inst.then_inc(c.sem, k)

            getattr(block, engine)(section)


def build_nc():
    nc = bass.Bass("TRN2", target_bir_lowering=False)
    st = ExitStack()

    d_xa = nc.dram_tensor("xa", [NT, DM], F32, kind="ExternalInput").ap()
    d_xb = nc.dram_tensor("xb", [NT, DM], F32, kind="ExternalInput").ap()
    d_wqkv = nc.dram_tensor("wqkv", [DM, 3 * DM], F32, kind="ExternalInput").ap()
    d_bqkv = nc.dram_tensor("bqkv", [24, 128], F32, kind="ExternalInput").ap()
    d_wout = nc.dram_tensor("wout", [DM, DM], F32, kind="ExternalInput").ap()
    d_bout = nc.dram_tensor("bout", [8, 128], F32, kind="ExternalInput").ap()
    d_w1 = nc.dram_tensor("w1", [DM, FF], F32, kind="ExternalInput").ap()
    d_b1 = nc.dram_tensor("b1", [32, 128], F32, kind="ExternalInput").ap()
    d_w2 = nc.dram_tensor("w2", [FF, DM], F32, kind="ExternalInput").ap()
    d_b2 = nc.dram_tensor("b2", [8, 128], F32, kind="ExternalInput").ap()
    d_poff = nc.dram_tensor("poff", [128, 2], F32, kind="ExternalInput").ap()
    d_out = nc.dram_tensor("out", [NT, DM], F32, kind="ExternalOutput").ap()
    d_x2 = nc.dram_tensor("x2scr", [NT, DM], F32, kind="Internal").ap()

    chans = {}
    for name in ("dmw", "dmw2", "dmz", "dmx", "dmb", "dmp", "dmm", "dmo", "pe", "act", "dve", "pool"):
        sem = st.enter_context(nc.semaphore(name))
        chans[name] = Ch(sem, name)
    (dmw, dmw2, dmz, dmx, dmb, dmp, dmm, dmo, pe, act, dve, pool) = (
        chans[k] for k in ("dmw", "dmw2", "dmz", "dmx", "dmb", "dmp", "dmm", "dmo",
                           "pe", "act", "dve", "pool"))
    wch = [dmw, dmw2]
    p = Prog({"vector": dve, "scalar": act, "gpsimd": pool, "tensor": pe, "sync": None})
    p.all_chans = list(chans.values())

    def sb(name, shape, dt):
        return st.enter_context(nc.sbuf_tensor(name, shape, dt)).ap()

    cst = sb("cst", [128, 8], F32)
    ones_bf = sb("ones_bf", [128, 1], BF16)
    onesr_bf = sb("onesr_bf", [1, 128], BF16)
    ones_f = sb("ones_f", [1, 128], F32)
    id_bf = sb("id_bf", [128, 128], BF16)
    id_f32 = sb("id_f32", [128, 128], F32)
    braw = sb("braw", [32, 128], F32)
    bq_t = sb("bq_t", [128, 24], F32)
    bo_t = sb("bo_t", [128, 8], F32)
    b1_t = sb("b1_t", [128, 32], F32)
    b2_t = sb("b2_t", [128, 8], F32)
    b1s = sb("b1s", [128, 32], F32)
    b2s = sb("b2s", [128, 8], F32)
    poff = sb("poff_s", [128, 2], F32)
    stats = sb("stats", [128, 16], F32)
    csst = sb("csst", [1, 8], F32)
    negc_b = sb("negc_b", [128, 1], F32)
    sqm = sb("sqm", [128, 1024], BF16)

    stLN1 = ExitStack()
    stAtt = ExitStack()
    stOP = ExitStack()
    stFFN = ExitStack()

    def sbin(stk, name, shape, dt):
        return stk.enter_context(nc.sbuf_tensor(name, shape, dt)).ap()

    pvars = sb("pvars", [128, 16], F32)
    pvi_t = sb("pvi_t", [128, 1], I32)

    wqkvb = sbin(stAtt, "wqkvb", [128, 6144], BF16)
    posb = sbin(stAtt, "posb", [128, S], BF16)

    attn_fmr = sbin(stAtt, "attn_fmr", [128, 8 * 2048], BF16)
    attn_fm = attn_fmr.rearrange("p (c t) -> p c t", c=8)
    q_pair = sbin(stAtt, "q_pair", [128, 2048], BF16)
    k_pair = sbin(stAtt, "k_pair", [128, 4096], BF16)
    v_fm = sbin(stAtt, "v_fm", [128, 4096], BF16)
    vtmA = sbin(stAtt, "vtmA", [128, 2080], BF16)
    vtmB = sbin(stAtt, "vtmB", [128, 2080], BF16)
    PTr = sbin(stAtt, "PTr", [128, 4096], BF16)
    PT0 = PTr[:, 0:2048]
    PT1 = PTr[:, 2048:4096]
    sqk_t = PTr
    attn_un = sbin(stAtt, "attn_un", [128, 2048], BF16)
    rc_t = sbin(stAtt, "rc_t", [2, 2048], BF16)
    sqq_alias = attn_un

    ln1r = sbin(stLN1, "ln1r", [128, 8 * 4096], BF16)
    ln1_fm = ln1r.rearrange("p (c t) -> p c t", c=8)
    xstage = sbin(stLN1, "xstage", [128, 4096], F32)
    ln1tmp_b = sbin(stLN1, "ln1tmp_b", [128, 1024], BF16)

    vvi_t = sbin(stLN1, "vvi_t", [128, 1024], I32)


    def lp(fn):
        def g(e):
            with nc.allow_low_precision(reason="bf16 softmax reciprocal/normalize"):
                return fn(e)
        return g

    ps_stack = ExitStack()

    def psopen(specs):
        out = []
        for (name, shape, dt) in specs:
            out.append(ps_stack.enter_context(nc.psum_tensor(name, shape, dt)).ap())
        return out

    def psclose():
        ps_stack.close()

    p.add("gpsimd", lambda e: e.memset(cst[:, 0:1], 16384.0), incs=[(pool, 1)])
    p.add("gpsimd", lambda e: e.memset(cst[:, 1:2], -64.0), incs=[(pool, 1)])
    p.add("gpsimd", lambda e: e.memset(cst[:, 2:3], -63.75), incs=[(pool, 1)])
    p.add("gpsimd", lambda e: e.memset(cst[:, 3:4], 1e-5), incs=[(pool, 1)])
    p.add("gpsimd", lambda e: e.memset(cst[:, 4:5], 0.25), incs=[(pool, 1)])
    p.add("gpsimd", lambda e: e.memset(ones_bf, 1.0), incs=[(pool, 1)])
    p.add("gpsimd", lambda e: e.memset(onesr_bf, 1.0), incs=[(pool, 1)])
    p.add("gpsimd", lambda e: e.memset(ones_f, 1.0), incs=[(pool, 1)])
    p.add("gpsimd", lambda e: e.memset(id_bf, 1.0), incs=[(pool, 1)])
    p.add("gpsimd", lambda e: e.affine_select(id_bf, id_bf, [[1, 128]], OP.is_equal, 0.0,
                                              base=0, channel_multiplier=-1), incs=[(pool, 1)])
    p.add("gpsimd", lambda e: e.memset(id_f32, 1.0), incs=[(pool, 1)])
    p.add("gpsimd", lambda e: e.affine_select(id_f32, id_f32, [[1, 128]], OP.is_equal, 0.0,
                                              base=0, channel_multiplier=-1), incs=[(pool, 1)])
    p.add("gpsimd", lambda e: e.memset(vtmA, 1.0), incs=[(pool, 1)])
    p.add("gpsimd", lambda e: e.memset(vtmB, 1.0), incs=[(pool, 1)])
    g_cst = pool.n

    pbias, = psopen([("pbias", [128, 128], F32)])
    p.add("sync", lambda e: e.dma_start(out=poff, in_=d_poff), incs=[(dmp, 16)])
    w_po = dmp.n
    for (src, rows, dst) in ((d_bqkv, 24, bq_t), (d_bout, 8, bo_t), (d_b1, 32, b1_t),
                             (d_b2, 8, b2_t)):
        p.add("sync", lambda e, src=src, rows=rows: e.dma_start(out=braw[0:rows, :], in_=src),
              waits=[(pe, pe.n)], incs=[(dmb, 16)])
        w_l = dmb.n
        p.add("tensor", lambda e, rows=rows: e.transpose(pbias[:, 0:rows], braw[0:rows, :],
                                                         id_f32[0:rows, 0:rows]),
              waits=[(dmb, w_l), (pool, g_cst), (dve, None)], incs=[(pe, 1)])
        w_t = pe.n
        p.add("vector", lambda e, rows=rows, dst=dst: e.tensor_copy(dst, pbias[:, 0:rows]),
              waits=[(pe, w_t)], incs=[(dve, 1)])
    p.add("vector", lambda e: e.tensor_scalar(b1s, b1_t, 256.0, None, OP.mult),
          incs=[(dve, 1)])
    p.add("vector", lambda e: e.tensor_scalar(b2s, b2_t, 256.0, None, OP.mult),
          incs=[(dve, 1)])
    d_bias = dve.n

    pv = pvars[:, 0:1]
    pvb = pvars[:, 1:2]
    m64 = pvars[:, 2:3]
    pvh = pvars[:, 3:4]
    pvi = pvi_t
    pvf = pvars[:, 5:6]
    pvd = pvars[:, 6:7]
    pvfl = pvars[:, 7:8]
    m2 = pvars[:, 8:9]
    fi = pvars[:, 9:10]
    freq = pvars[:, 10:11]
    off = pvars[:, 11:12]
    pos_i = xstage[:, 0:1024]
    argt = xstage[:, 1024:2048]
    vvt = xstage[:, 2048:3072]
    posw = xstage[:, 3072:4096]

    p.add("gpsimd", lambda e: e.iota(pv, [[0, 1]], base=0, channel_multiplier=1,
                                     allow_small_or_imprecise_dtypes=True), incs=[(pool, 1)])
    g_iota = pool.n
    p.add("vector", lambda e: e.tensor_scalar(pvb, pv, 64.0, -64.0, OP.is_ge, OP.mult),
          waits=[(pool, g_iota)], incs=[(dve, 1)])
    p.add("vector", lambda e: e.tensor_tensor(m64, pvb, pv, OP.add), incs=[(dve, 1)])
    p.add("vector", lambda e: e.tensor_scalar(pvh, m64, 0.5, None, OP.mult), incs=[(dve, 1)])
    p.add("vector", lambda e: e.tensor_copy(pvi, pvh), incs=[(dve, 1)])
    p.add("vector", lambda e: e.tensor_copy(pvf, pvi), incs=[(dve, 1)])
    p.add("vector", lambda e: e.tensor_tensor(pvd, pvh, pvf, OP.subtract), incs=[(dve, 1)])
    p.add("vector", lambda e: e.scalar_tensor_tensor(pvfl, pvd, 0.0, pvf, OP.is_lt, OP.subtract),
          incs=[(dve, 1)])
    p.add("vector", lambda e: e.scalar_tensor_tensor(m2, pvfl, 2.0, m64, OP.mult, OP.add),
          incs=[(dve, 1)])
    p.add("vector", lambda e: e.tensor_tensor(fi, m64, m2, OP.subtract), incs=[(dve, 1)])
    d_fi = dve.n
    p.add("scalar", lambda e: e.activation(freq, fi, AF.Exp, scale=-LN1e4 / 64.0),
          waits=[(dve, d_fi)], incs=[(act, 1)])
    a_fr = act.n
    p.add("vector", lambda e: e.tensor_scalar(off, m2, PI / 2.0, None, OP.mult), incs=[(dve, 1)])
    for seg in range(4):
        half, r = seg // 2, seg % 2
        p.add("gpsimd", lambda e, r=r: e.iota(pos_i, [[1, 1024]], base=r * 1024,
                                              channel_multiplier=0,
                                              allow_small_or_imprecise_dtypes=True),
              waits=[(dve, dve.n)], incs=[(pool, 1)])
        g_i = pool.n
        p.add("vector", lambda e, half=half: e.tensor_scalar(argt, pos_i, poff[:, half:half + 1],
                                                             freq, OP.add, OP.mult),
              waits=[(act, act.n), (dmp, w_po), (pool, g_i)], incs=[(dve, 1)])
        p.add("vector", lambda e: e.tensor_scalar(argt, argt, off, None, OP.add), incs=[(dve, 1)])
        p.add("vector", lambda e: e.tensor_scalar(vvt, argt, 1.0 / (2 * PI), None, OP.mult),
              incs=[(dve, 1)])
        p.add("vector", lambda e: e.tensor_copy(vvi_t, vvt), incs=[(dve, 1)])
        p.add("vector", lambda e: e.tensor_copy(posw, vvi_t), incs=[(dve, 1)])
        p.add("vector", lambda e: e.tensor_tensor(vvt, vvt, posw, OP.subtract), incs=[(dve, 1)])
        p.add("vector", lambda e: e.scalar_tensor_tensor(posw, vvt, 0.5, posw, OP.is_ge, OP.add),
              incs=[(dve, 1)])
        p.add("vector", lambda e: e.scalar_tensor_tensor(posw, vvt, -0.5, posw, OP.is_lt, OP.subtract),
              incs=[(dve, 1)])
        p.add("vector", lambda e: e.scalar_tensor_tensor(argt, posw, 2 * PI, argt, OP.mult, OP.add),
              incs=[(dve, 1)])
        p.add("vector", lambda e: e.tensor_scalar(argt, argt, PI, -PI, OP.min, OP.max),
              incs=[(dve, 1)])
        d_th = dve.n
        p.add("scalar", lambda e, seg=seg: e.activation(posb[:, seg * 1024:(seg + 1) * 1024],
                                                        argt, AF.Sin),
              waits=[(dve, d_th)], incs=[(act, 1)])
    a_pos = act.n

    def ln_stats(src_tile, scol, waits=()):
        s_ = stats[:, scol + 2:scol + 3]
        q_ = stats[:, scol + 3:scol + 4]
        p.add("vector", lambda e: e.reduce_sum(s_, src_tile, AX.X), waits=list(waits),
              incs=[(dve, 1)])
        p.add("scalar", lambda e: e.activation(sqm, src_tile, AF.Square, accum_out=q_),
              waits=list(waits) + [(dve, dve.n)], incs=[(act, 1)])
        a_q = act.n
        p.add("vector", lambda e: e.tensor_scalar(stats[:, scol:scol + 1], s_, 1.0 / 1024.0,
                                                  None, OP.mult), incs=[(dve, 1)])
        p.add("vector", lambda e: e.tensor_scalar(q_, q_, 1.0 / 1024.0, None, OP.mult),
              waits=[(act, a_q)], incs=[(dve, 1)])
        p.add("vector", lambda e: e.tensor_tensor(s_, stats[:, scol:scol + 1],
                                                  stats[:, scol:scol + 1], OP.mult), incs=[(dve, 1)])
        p.add("vector", lambda e: e.tensor_tensor(q_, q_, s_, OP.subtract), incs=[(dve, 1)])
        d_v = dve.n
        p.add("scalar", lambda e: e.activation(s_, q_, AF.Sqrt, bias=cst[:, 3:4]),
              waits=[(dve, d_v), (pool, g_cst)], incs=[(act, 1)])
        a_s = act.n
        p.add("vector", lambda e: e.reciprocal(stats[:, scol + 1:scol + 2], s_),
              waits=[(act, a_s)], incs=[(dve, 1)])
        return dve.n

    ptr_bf, = psopen([("ptr_bf", [128, 2048], BF16)])
    for t in range(32):
        src = d_xa if t < 16 else d_xb
        tt = t % 16
        par = (t // 2) % 2
        if t % 2 == 0:
            p.add("sync", lambda e, src=src, tt=tt, par=par: e.dma_start(
                out=xstage[:, par * 2048:(par + 1) * 2048].rearrange("p (k d) -> p k d", k=2),
                in_=src.rearrange("(k p) d -> p k d", p=128)[:, tt:tt + 2, :]),
                waits=[(dve, dve.n), (act, act.n)], incs=[(dmx, 16)])
        w_g = dmx.n
        xt = xstage[:, (par * 2 + (t % 2)) * 1024:(par * 2 + (t % 2) + 1) * 1024]
        ln_stats(xt, 0, waits=[(dmx, w_g)])
        p.add("vector", lambda e, xt=xt: e.tensor_scalar(
            ln1tmp_b, xt, stats[:, 0:1], stats[:, 1:2], OP.subtract, OP.mult), incs=[(dve, 1)])
        d_ap = dve.n
        for c in range(8):
            p.add("tensor", lambda e, c=c: e.transpose(
                ptr_bf[:, c * 128:(c + 1) * 128], ln1tmp_b[:, c * 128:(c + 1) * 128], id_bf),
                waits=[(dve, d_ap), (pool, g_cst)], incs=[(pe, 1)])
        w_tr = pe.n
        p.add("vector", lambda e, t=t: e.tensor_copy(
            ln1_fm[:, :, t * 128:(t + 1) * 128],
            ptr_bf[:, 0:1024].rearrange("p (c j) -> p c j", c=8)),
            waits=[(pe, w_tr)], incs=[(dve, 1)])
    psclose()
    d_ln1 = dve.n

    sqq_t = attn_un
    wq_b0 = wqkvb

    def qkv_slab_dma(pair, slab_free_pe):
        buf = (pair % 2) * 3072
        ch = wch[pair % 2]
        for j in range(3):
            col0 = j * 1024 + pair * 128
            p.add("gpsimd", lambda e, buf=buf, j=j, col0=col0: e.dma_start(
                out=wqkvb[:, buf + j * 1024:buf + (j + 1) * 1024].rearrange(
                    "p (k j2) -> p k j2", k=8),
                in_=d_wqkv[:, col0:col0 + 128].rearrange("(k p) j2 -> p k j2", p=128)),
                waits=[(pe, slab_free_pe)], incs=[(ch, 16)])
        return ch.n

    slab_ready = [0] * NPAIR
    slab_free = [0, 0]
    slab_ready[0] = qkv_slab_dma(0, 0)
    slab_ready[1] = qkv_slab_dma(1, 0)

    exp_act = {}
    s_pe = {}
    av_pe = {}
    gs = 0
    av_last_prev = 0
    for pair in range(NPAIR):
        par = pair % 2
        wq_b = wqkvb[:, par * 3072:par * 3072 + 1024]
        wk_b = wqkvb[:, par * 3072 + 1024:par * 3072 + 2048]
        wv_b = wqkvb[:, par * 3072 + 2048:par * 3072 + 3072]
        d_pair_prev = dve.n

        PPq, PVT, PCS = psopen([(f"PP{pair}", [128, 1024], F32),
                                (f"PVT{pair}", [128, 2048], BF16),
                                (f"PCS{pair}", [128, 2048], F32)])
        slot_evac = [act.n, act.n]
        for qt in range(4):
            pa = qt % 2
            dst = PPq[:, pa * 512:(pa + 1) * 512]
            for kc in range(8):
                p.add("tensor", lambda e, kc=kc, qt=qt, dst=dst, wq_b=wq_b: e.matmul(
                    dst, wq_b[:, kc * 128:(kc + 1) * 128],
                    ln1_fm[:, kc, qt * 512:(qt + 1) * 512],
                    start=(kc == 0), stop=(kc == 7)),
                    waits=[(wch[pair % 2], slab_ready[pair] if kc == 0 else 0),
                           (dve, (max(d_pair_prev, d_ln1) if qt == 0 else 0) if kc == 0 else 0),
                           (act, slot_evac[pa] if kc == 0 else 0)],
                    incs=[(pe, 1)] if kc == 7 else ())
            w_mm = pe.n
            p.add("scalar", lambda e, qt=qt, dst=dst, pair=pair: e.activation(
                q_pair[:, qt * 512:(qt + 1) * 512], dst, AF.Identity,
                bias=bq_t[:, pair:pair + 1], scale=1.0),
                waits=[(pe, w_mm), (dve, d_bias)], incs=[(act, 1)])
            slot_evac[pa] = act.n
        for which in ("k", "v"):
            wb = wk_b if which == "k" else wv_b
            bcol = 8 + pair if which == "k" else 16 + pair
            dstt = k_pair if which == "k" else v_fm
            for ktile in range(8):
                pa = ktile % 2
                dst = PPq[:, pa * 512:(pa + 1) * 512]
                for kc in range(8):
                    p.add("tensor", lambda e, kc=kc, ktile=ktile, dst=dst, wb=wb,
                          which=which: e.matmul(
                        dst, wb[:, kc * 128:(kc + 1) * 128],
                        ln1_fm[:, kc, ktile * 512:(ktile + 1) * 512],
                        start=(kc == 0), stop=(kc == 7 and which == "v")),
                        waits=[(act, slot_evac[pa] if kc == 0 else 0)],
                        incs=[(pe, 1)] if (kc == 7 and which == "v") else ())
                if which == "k":
                    p.add("tensor", lambda e, ktile=ktile, dst=dst: e.matmul(
                        dst, id_bf, posb[:, ktile * 512:(ktile + 1) * 512],
                        start=False, stop=True),
                        waits=[(act, a_pos), (pool, g_cst)], incs=[(pe, 1)])
                w_mm = pe.n
                p.add("scalar", lambda e, ktile=ktile, dst=dst, dstt=dstt, bcol=bcol: e.activation(
                    dstt[:, ktile * 512:(ktile + 1) * 512], dst, AF.Identity,
                    bias=bq_t[:, bcol:bcol + 1], scale=1.0),
                    waits=[(pe, w_mm)], incs=[(act, 1)])
                slot_evac[pa] = act.n
        a_qkv = act.n
        slab_free[par] = pe.n
        if pair + 2 < NPAIR:
            slab_ready[pair + 2] = qkv_slab_dma(pair + 2, slab_free[par])

        for rnd in range(2):
            for j in range(16):
                kc = rnd * 16 + j
                p.add("tensor", lambda e, kc=kc, j=j: e.transpose(
                    PVT[:, j * 128:(j + 1) * 128], v_fm[:, kc * 128:(kc + 1) * 128], id_bf),
                    waits=[(act, a_qkv if j == 0 else 0),
                           (dve, (d_pair_prev if rnd == 0 else dve.n) if j == 0 else 0)],
                    incs=[(pe, 1)])
            w_vt = pe.n
            for (dst, lo) in ((vtmA, 0), (vtmB, 64)):
                p.add("vector", lambda e, rnd=rnd, dst=dst, lo=lo: e.tensor_copy(
                    dst.rearrange("p (c j) -> p c j", c=32)[:, rnd * 16:(rnd + 1) * 16, 0:64],
                    PVT.rearrange("p (c j) -> p c j", c=16)[:, :, lo:lo + 64]),
                    waits=[(pe, w_vt), (pe, av_last_prev)], incs=[(dve, 1)])
        d_vtm = dve.n

        p.add("scalar", lambda e: e.activation(sqq_t, q_pair, AF.Square),
              waits=[(dve, dve.n), (dmm, dmm.n), (pe, av_last_prev)], incs=[(act, 1)])
        p.add("scalar", lambda e: e.activation(sqk_t, k_pair, AF.Square), incs=[(act, 1)])
        a_sq = act.n
        for j in range(4):
            p.add("tensor", lambda e, j=j: e.matmul(
                PCS[0:1, j * 512:(j + 1) * 512], ones_bf, sqq_t[:, j * 512:(j + 1) * 512],
                start=True, stop=True),
                waits=[(act, a_sq), (pool, g_cst)], incs=[(pe, 1)])
        w_cs = pe.n
        p.add("vector", lambda e: e.reduce_max(csst[:, 0:1], PCS[0:1, 0:2048], AX.X),
              waits=[(pe, w_cs)], incs=[(dve, 1)])
        d_cs = dve.n
        for j in range(4):
            p.add("tensor", lambda e, j=j: e.matmul(
                PCS[0:1, j * 512:(j + 1) * 512], ones_bf, sqk_t[:, j * 512:(j + 1) * 512],
                start=True, stop=True),
                waits=[(dve, d_cs)], incs=[(pe, 1)])
        w_cs = pe.n
        p.add("vector", lambda e: e.reduce_max(csst[:, 1:2], PCS[0:1, 0:2048], AX.X),
              waits=[(pe, w_cs)], incs=[(dve, 1)])
        d_cs = dve.n
        for j in range(4):
            p.add("tensor", lambda e, j=j: e.matmul(
                PCS[0:1, j * 512:(j + 1) * 512], ones_bf,
                sqk_t[:, 2048 + j * 512:2048 + (j + 1) * 512],
                start=True, stop=True),
                waits=[(dve, d_cs)], incs=[(pe, 1)])
        w_cs = pe.n
        p.add("vector", lambda e: e.reduce_max(csst[:, 2:3], PCS[0:1, 0:2048], AX.X),
              waits=[(pe, w_cs)], incs=[(dve, 1)])
        p.add("vector", lambda e: e.tensor_tensor(csst[:, 1:2], csst[:, 1:2], csst[:, 2:3], OP.max),
              incs=[(dve, 1)])
        p.add("vector", lambda e: e.tensor_tensor(csst[:, 3:4], csst[:, 0:1], csst[:, 1:2], OP.mult),
              incs=[(dve, 1)])
        d_pr = dve.n
        p.add("scalar", lambda e: e.activation(csst[:, 4:5], csst[:, 3:4], AF.Sqrt, scale=1.0 / 64.0),
              waits=[(dve, d_pr)], incs=[(act, 1)])
        a_c = act.n
        p.add("vector", lambda e: e.tensor_scalar(csst[:, 5:6], csst[:, 4:5], -1.0, None, OP.mult),
              waits=[(act, a_c)], incs=[(dve, 1)])
        d_nc0 = dve.n
        p.add("tensor", lambda e: e.matmul(PCS[:, 0:1], ones_f, csst[:, 5:6],
                                           start=True, stop=True),
              waits=[(dve, d_nc0)], incs=[(pe, 1)])
        w_bc = pe.n
        p.add("vector", lambda e: e.tensor_copy(negc_b, PCS[:, 0:1]),
              waits=[(pe, w_bc)], incs=[(dve, 1)])
        d_nc = dve.n
        psclose()

        SA, AVp = psopen([(f"SA{pair}", [128, 2048], F32), (f"AV{pair}", [128, 2048], F32)])
        d_avfree = [dve.n, dve.n]

        def emit_S2(qhalf, s):
            head, kc = s % 2, s // 2
            base = head * 64
            g = gs + s
            sa = SA[:, (s % 2) * 1024:(s % 2 + 1) * 1024]
            for q2 in range(2):
                qt = qhalf * 2 + q2
                p.add("tensor", lambda e, kc=kc, qt=qt, q2=q2, base=base, sa=sa: e.matmul(
                    sa[:, q2 * 512:(q2 + 1) * 512],
                    k_pair[base:base + 64, kc * 128:(kc + 1) * 128],
                    q_pair[base:base + 64, qt * 512:(qt + 1) * 512],
                    start=True, stop=True, tile_position=(base, 0)),
                    waits=[(act, max(exp_act.get(g - 2, 0), a_qkv) if q2 == 0 else 0),
                           (dve, d_nc if q2 == 0 else 0)],
                    incs=[(pe, 1)] if q2 == 1 else ())
            s_pe[g] = pe.n

        def emit_AV(s):
            g = gs + s
            head, kc = s % 2, s // 2
            vtm = vtmA if head == 0 else vtmB
            ptb = PTr[:, (g % 4) * 1024:(g % 4 + 1) * 1024]
            for q2 in range(2):
                p.add("tensor", lambda e, kc=kc, head=head, q2=q2, vtm=vtm,
                      ptb=ptb: e.matmul(
                    AVp[0:65, head * 1024 + q2 * 512:head * 1024 + (q2 + 1) * 512],
                    vtm[:, kc * 65:kc * 65 + 65],
                    ptb[:, q2 * 512:(q2 + 1) * 512],
                    start=(kc == 0), stop=(kc == 31)),
                    waits=[(act, exp_act[g] if q2 == 0 else 0),
                           (dve, d_avfree[head] if (kc == 0 and q2 == 0) else 0)],
                    incs=[(pe, 1)] if q2 == 1 else ())
            av_pe[g] = pe.n

        for qhalf in range(2):
            emit_S2(qhalf, 0)
            for s in range(64):
                g = gs + s
                if s < 63:
                    emit_S2(qhalf, s + 1)
                sa = SA[:, (s % 2) * 1024:(s % 2 + 1) * 1024]
                ptb = PTr[:, (g % 4) * 1024:(g % 4 + 1) * 1024]
                p.add("scalar", lambda e, sa=sa, ptb=ptb: e.activation(
                    ptb, sa, AF.Exp, bias=negc_b, scale=0.125),
                    waits=[(pe, max(s_pe[g], av_pe.get(g - 4, 0))), (dve, d_nc)],
                    incs=[(act, 1)])
                exp_act[g] = act.n
                if s >= 1:
                    emit_AV(s - 1)
            emit_AV(63)
            gs += 64
            for head in range(2):
                w_av = av_pe[gs - 2 + head]
                hr = slice(head * 1024, (head + 1) * 1024)
                p.add("vector", lp(lambda e, hr=hr: e.reciprocal(
                    rc_t[0:1, hr], AVp[64:65, hr])),
                    waits=[(pe, w_av)], incs=[(dve, 1)])
                p.add("vector", lambda e, hr=hr: e.tensor_copy(
                    attn_un[0:64, hr], AVp[0:64, hr]),
                    waits=[(dmm, dmm.n)], incs=[(dve, 1)])
                d_ev = dve.n
                for qh in range(2):
                    p.add("tensor", lambda e, head=head, qh=qh: e.matmul(
                        AVp[0:64, head * 1024 + qh * 512:head * 1024 + (qh + 1) * 512],
                        onesr_bf[:, 0:64],
                        rc_t[0:1, head * 1024 + qh * 512:head * 1024 + (qh + 1) * 512],
                        start=True, stop=True),
                        waits=[(dve, d_ev)], incs=[(pe, 1)])
                w_b = pe.n
                if head == 0:
                    p.add("vector", lambda e, hr=hr, pair=pair, qhalf=qhalf: e.tensor_tensor(
                        attn_fm[0:64, pair, qhalf * 1024:(qhalf + 1) * 1024],
                        attn_un[0:64, hr], AVp[0:64, hr], OP.mult),
                        waits=[(pe, w_b)], incs=[(dve, 1)])
                else:
                    p.add("vector", lambda e, hr=hr: e.tensor_tensor(
                        attn_un[0:64, hr], attn_un[0:64, hr], AVp[0:64, hr], OP.mult),
                        waits=[(pe, w_b)], incs=[(dve, 1)])
                d_avfree[head] = dve.n
            d_nm = dve.n
            p.add("sync", lambda e, pair=pair, qhalf=qhalf: e.dma_start(
                out=attn_fm[64:128, pair, qhalf * 1024:(qhalf + 1) * 1024],
                in_=attn_un[0:64, 1024:2048]),
                waits=[(dve, d_nm)], incs=[(dmm, 16)])
        av_last_prev = av_pe[gs - 1]
        psclose()
    d_attn = dve.n
    w_attnmv = dmm.n

    psclose()
    stLN1.close()

    PO, PT2 = psopen([("PO", [128, 2048], F32), ("PT2", [128, 2048], BF16)])
    outattnr = sbin(stOP, "outattnr", [128, 8 * 2048], BF16)
    outattn = outattnr.rearrange("p (c t) -> p c t", c=8)

    def wout_dma(dm, free_pe):
        ch = wch[dm % 2]
        p.add("gpsimd", lambda e, dm=dm: e.dma_start(
            out=wqkvb[:, (dm % 2) * 1024:(dm % 2) * 1024 + 1024].rearrange(
                "p (k j) -> p k j", k=8),
            in_=d_wout[:, dm * 128:(dm + 1) * 128].rearrange("(k p) j -> p k j", p=128)),
            waits=[(pe, free_pe)], incs=[(ch, 16)])
        return ch.n

    wo_ready = [0] * 8
    wo_mm_done = [0, 0]
    wo_ready[0] = wout_dma(0, av_last_prev)
    wo_ready[1] = wout_dma(1, av_last_prev)
    slot_evac = [act.n, act.n]
    for dm in range(8):
        wo_b = wqkvb[:, (dm % 2) * 1024:(dm % 2) * 1024 + 1024]
        for qt in range(4):
            dst = PO[:, (qt % 2) * 512:(qt % 2 + 1) * 512]
            for hc in range(8):
                p.add("tensor", lambda e, hc=hc, qt=qt, dst=dst, wo_b=wo_b: e.matmul(
                    dst, wo_b[:, hc * 128:(hc + 1) * 128],
                    attn_fm[:, hc, qt * 512:(qt + 1) * 512],
                    start=(hc == 0), stop=(hc == 7)),
                    waits=[(wch[dm % 2], wo_ready[dm] if (hc == 0 and qt == 0) else 0),
                           (dve, (d_attn if (dm == 0 and qt == 0) else 0) if hc == 0 else 0),
                           (dmm, (w_attnmv if (dm == 0 and qt == 0) else 0) if hc == 0 else 0),
                           (act, slot_evac[qt % 2] if hc == 0 else 0)],
                    incs=[(pe, 1)] if hc == 7 else ())
            w_mm = pe.n
            p.add("scalar", lambda e, dm=dm, qt=qt, dst=dst: e.activation(
                outattn[:, dm, qt * 512:(qt + 1) * 512], dst, AF.Identity,
                bias=bo_t[:, dm:dm + 1], scale=1.0),
                waits=[(pe, w_mm)], incs=[(act, 1)])
            slot_evac[qt % 2] = act.n
        wo_mm_done[dm % 2] = pe.n
        if dm + 2 < 8:
            wo_ready[dm + 2] = wout_dma(dm + 2, wo_mm_done[dm % 2])
    a_op = act.n

    at_tm = sbin(stOP, "at_tm", [128, 1024], BF16)
    x_tile = sbin(stOP, "x_tile", [128, 1024], F32)
    x2_tile = sbin(stOP, "x2_tile", [128, 1024], F32)
    x2w = [0] * 16
    for t in range(16):
        p.add("sync", lambda e, t=t: e.dma_start(
            out=x_tile, in_=d_xa.rearrange("(k p) d -> p k d", p=128)[:, t:t + 1, :]),
            waits=[(dve, dve.n)], incs=[(dmx, 16)])
        w_x = dmx.n
        for c in range(8):
            p.add("tensor", lambda e, c=c, t=t: e.transpose(
                PT2[:, c * 128:(c + 1) * 128], outattn[:, c, t * 128:(t + 1) * 128], id_bf),
                waits=[(act, a_op if c == 0 else 0), (dve, dve.n if c == 0 else 0)],
                incs=[(pe, 1)])
        w_tr = pe.n
        p.add("vector", lambda e: e.tensor_copy(at_tm.rearrange("p (c j) -> p c j", c=8),
                                                PT2[:, 0:1024].rearrange("p (c j) -> p c j", c=8)),
              waits=[(pe, w_tr)], incs=[(dve, 1)])
        ln_stats(at_tm, 0)
        p.add("vector", lambda e: e.tensor_scalar(at_tm, at_tm, stats[:, 0:1], stats[:, 1:2],
                                                  OP.subtract, OP.mult), incs=[(dve, 1)])
        p.add("vector", lambda e: e.tensor_tensor(x2_tile, x_tile, at_tm, OP.add),
              waits=[(dmx, w_x), (dmz, dmz.n)], incs=[(dve, 1)])
        d_x2c = dve.n
        p.add("sync", lambda e, t=t: e.dma_start(
            out=d_x2.rearrange("(k p) d -> p k d", p=128)[:, t:t + 1, :], in_=x2_tile),
            waits=[(dve, d_x2c)], incs=[(dmz, 16)])
        x2w[t] = dmz.n
    psclose()
    x2w_all = dmz.n

    stOP.close()
    stAtt.close()
    ln3r = sbin(stFFN, "ln3r", [128, 8 * 1024], BF16)
    ln3_fm = ln3r.rearrange("p (c t) -> p c t", c=8)
    hq_r = sbin(stFFN, "hq_r", [128, 32 * 1024], BF16)
    hq_sb = hq_r.rearrange("p (f t) -> p f t", f=32)
    x2s = sbin(stFFN, "x2s", [128, 1024], F32)
    ln3tmp_b = sbin(stFFN, "ln3tmp_b", [128, 1024], BF16)
    t_hi = sbin(stFFN, "t_hi", [128, 2048], I32)
    t_u = sbin(stFFN, "t_u", [128, 2048], I32)
    t_rf = sbin(stFFN, "t_rf", [128, 2048], F32)
    t_g0 = sbin(stFFN, "t_g0", [128, 2048], F32)
    t_g1 = sbin(stFFN, "t_g1", [128, 2048], F32)
    t_dg = sbin(stFFN, "t_dg", [128, 1024], F32)
    t_tdg = sbin(stFFN, "t_tdg", [128, 1024], F32)
    t_oi = sbin(stFFN, "t_oi", [128, 1024], I32)
    w1b = sbin(stFFN, "w1b", [128, 2048], BF16)
    w2b = sbin(stFFN, "w2b", [128, 1024], BF16)
    out_f = sbin(stFFN, "out_f", [128, 8 * 1024], F32)
    out_fm = out_f.rearrange("p (c t) -> p c t", c=8)
    fout_tm = sbin(stFFN, "fout_tm", [128, 1024], F32)

    hi_done = [0, 0]
    dg_done = [0, 0]
    dgd = {-1: 0, -2: 0}
    tdgd = {-1: 0, -2: 0}
    yd = {-1: 0, -2: 0}
    yid = {-1: 0}
    g1d = {-1: 0, -2: 0}
    hbfd = {-1: 0, -2: 0}
    hbf_act = {}
    oi_slot_ff = [0, 0]

    for T in range(2):
        PTl, = psopen([(f"PTl{T}", [128, 2048], BF16)])
        for j in range(8):
            t = T * 8 + j
            p.add("sync", lambda e, t=t: e.dma_start(
                out=x2s, in_=d_x2.rearrange("(k p) d -> p k d", p=128)[:, t:t + 1, :]),
                waits=[(dve, dve.n), (dmz, x2w_all)], incs=[(dmx, 16)])
            w_x = dmx.n
            ln_stats(x2s, 0, waits=[(dmx, w_x)])
            p.add("vector", lambda e: e.tensor_scalar(ln3tmp_b, x2s, stats[:, 0:1],
                                                      stats[:, 1:2], OP.subtract, OP.mult),
                  incs=[(dve, 1)])
            d_ap = dve.n
            po = (j % 2) * 1024
            for c in range(8):
                p.add("tensor", lambda e, c=c, po=po: e.transpose(
                    PTl[:, po + c * 128:po + (c + 1) * 128],
                    ln3tmp_b[:, c * 128:(c + 1) * 128], id_bf),
                    waits=[(dve, d_ap if c == 0 else 0)],
                    incs=[(pe, 1)] if c == 7 else ())
            w_tr = pe.n
            p.add("vector", lambda e, j=j, po=po: e.tensor_copy(
                ln3_fm[:, :, j * 128:(j + 1) * 128],
                PTl[:, po:po + 1024].rearrange("p (c j2) -> p c j2", c=8)),
                waits=[(pe, w_tr)], incs=[(dve, 1)])
        psclose()
        d_ln3 = dve.n

        PH, = psopen([(f"PH{T}", [128, 2048], F32)])
        w1_mm_done = [pe.n, pe.n]

        def w1_dma(f, par):
            ch = wch[par]
            p.add("gpsimd", lambda e, f=f, par=par: e.dma_start(
                out=w1b[:, par * 1024:(par + 1) * 1024].rearrange("p (k j) -> p k j", k=8),
                in_=d_w1[:, f * 128:(f + 1) * 128].rearrange("(k p) j -> p k j", p=128)),
                waits=[(pe, w1_mm_done[par])], incs=[(ch, 16)])
            return ch.n

        w1_ready = [0] * 34
        w1_ready[0] = w1_dma(0, 0)
        w1_ready[1] = w1_dma(1, 1)
        for f in range(32):
            par = f % 2
            ph = PH[:, par * 1024:(par + 1) * 1024]
            wcol = par * 1024
            hi = t_hi[:, par * 1024:(par + 1) * 1024]
            u = t_u[:, par * 1024:(par + 1) * 1024]
            rf = t_rf[:, par * 1024:(par + 1) * 1024]
            g0 = t_g0[:, par * 1024:(par + 1) * 1024]
            g1 = t_g1[:, par * 1024:(par + 1) * 1024]
            for kc in range(8):
                for h in range(2):
                    p.add("tensor", lambda e, kc=kc, h=h, ph=ph, wcol=wcol: e.matmul(
                        ph[:, h * 512:(h + 1) * 512],
                        w1b[:, wcol + kc * 128:wcol + (kc + 1) * 128],
                        ln3_fm[:, kc, h * 512:(h + 1) * 512],
                        start=(kc == 0), stop=(kc == 7)),
                        waits=[(wch[par], w1_ready[f] if (kc == 0 and h == 0) else 0),
                               (dve, max(d_ln3, hi_done[par]) if (kc == 0 and h == 0) else 0)],
                        incs=[(pe, 1)] if (kc == 7 and h == 1) else ())
            w1_mm_done[par] = pe.n
            w_mm = pe.n
            if f + 2 < 32:
                w1_ready[f + 2] = w1_dma(f + 2, par)
            p.add("vector", lambda e, ph=ph, hi=hi, f=f: e.tensor_scalar(
                hi, ph, 256.0, b1s[:, f:f + 1], OP.mult, OP.add),
                waits=[(pe, w_mm), (dve, d_bias), (act, hbfd[f - 2])], incs=[(dve, 1)])
            hi_done[par] = dve.n
            p.add("vector", lambda e, u=u, hi=hi: e.tensor_scalar(
                u, hi, 6, None, OP.arith_shift_right), incs=[(dve, 1)])
            d_u = dve.n
            p.add("vector", lambda e, rf=rf, u=u, hi=hi: e.scalar_tensor_tensor(
                rf, u, -64.0, hi, OP.mult, OP.add), incs=[(dve, 1)])
            if f >= 1:
                parp = (f - 1) % 2
                rfp = t_rf[:, parp * 1024:(parp + 1) * 1024]
                g0p = t_g0[:, parp * 1024:(parp + 1) * 1024]
                hip = t_hi[:, parp * 1024:(parp + 1) * 1024]
                p.add("vector", lambda e, rfp=rfp: e.tensor_tensor(
                    t_tdg, rfp, t_dg, OP.mult),
                    waits=[(dve, dgd[f - 1])], incs=[(dve, 1)])
                tdgd[f - 1] = dve.n
                p.add("vector", lambda e, rfp=rfp, g0p=g0p: e.scalar_tensor_tensor(
                    rfp, t_tdg, 0.015625, g0p, OP.mult, OP.add), incs=[(dve, 1)])
                yd[f - 1] = dve.n
                p.add("vector", lambda e, rfp=rfp, hip=hip: e.tensor_scalar(
                    hip, rfp, 256.0, None, OP.mult), incs=[(dve, 1)])
                yid[f - 1] = dve.n
            p.add("scalar", lambda e, g0=g0, u=u: e.activation(g0, u, AF.Gelu, scale=0.25),
                  waits=[(dve, max(d_u, dgd[f - 2], yd[f - 2]))], incs=[(act, 1)])
            p.add("scalar", lambda e, g1=g1, u=u: e.activation(g1, u, AF.Gelu,
                                                               scale=0.25, bias=cst[:, 4:5]),
                  incs=[(act, 1)])
            g1d[f] = act.n
            if f >= 1:
                parp = (f - 1) % 2
                hip = t_hi[:, parp * 1024:(parp + 1) * 1024]
                p.add("scalar", lambda e, hip=hip, fp=f - 1: e.activation(
                    hq_sb[:, fp, :], hip, AF.Identity, scale=0.00390625),
                    waits=[(dve, yid[f - 1])], incs=[(act, 1)])
                hbfd[f - 1] = act.n
                hbf_act[(T, f - 1)] = act.n
            p.add("vector", lambda e, g0=g0, g1=g1: e.tensor_tensor(t_dg, g1, g0, OP.subtract),
                  waits=[(act, g1d[f])], incs=[(dve, 1)])
            dgd[f] = dve.n
        parp = 31 % 2
        rfp = t_rf[:, parp * 1024:(parp + 1) * 1024]
        g0p = t_g0[:, parp * 1024:(parp + 1) * 1024]
        hip = t_hi[:, parp * 1024:(parp + 1) * 1024]
        p.add("vector", lambda e, rfp=rfp: e.tensor_tensor(t_tdg, rfp, t_dg, OP.mult),
              waits=[(dve, dgd[31])], incs=[(dve, 1)])
        tdgd[31] = dve.n
        p.add("vector", lambda e, rfp=rfp, g0p=g0p: e.scalar_tensor_tensor(
            rfp, t_tdg, 0.015625, g0p, OP.mult, OP.add), incs=[(dve, 1)])
        yd[31] = dve.n
        p.add("vector", lambda e, rfp=rfp, hip=hip: e.tensor_scalar(
            hip, rfp, 256.0, None, OP.mult), incs=[(dve, 1)])
        yid[31] = dve.n
        p.add("scalar", lambda e, hip=hip: e.activation(
            hq_sb[:, 31, :], hip, AF.Identity, scale=0.00390625),
            waits=[(dve, yid[31])], incs=[(act, 1)])
        hbfd[31] = act.n
        hbf_act[(T, 31)] = act.n
        psclose()
        d_chain_end = max(hi_done[0], hi_done[1], yid[31])

        PW8r, = psopen([(f"PW8{T}", [128, 4096], F32)])
        PW8 = PW8r
        oi_all = 0
        w2_mm_done = [0] * 34
        for dmh in range(2):
            w2_mm_done[0] = w2_mm_done[1] = pe.n

            def w2_dma(fc, par, dmh=dmh):
                ch = wch[par]
                p.add("gpsimd", lambda e, fc=fc, par=par, dmh=dmh: e.dma_start(
                    out=w2b[:, par * 512:(par + 1) * 512],
                    in_=d_w2.rearrange("(c p) d -> p c d", p=128)[:, fc, dmh * 512:(dmh + 1) * 512]),
                    waits=[(pe, w2_mm_done[fc])], incs=[(ch, 16)])
                return ch.n

            w2_ready = [0] * 34
            w2_ready[0] = w2_dma(0, 0)
            w2_ready[1] = w2_dma(1, 1)
            for fc in range(32):
                par = fc % 2
                for dmc in range(4):
                    for tb in range(2):
                        bank = tb * 4 + dmc
                        p.add("tensor", lambda e, fc=fc, dmc=dmc, tb=tb, bank=bank, par=par: e.matmul(
                            PW8[:, bank * 512:(bank + 1) * 512],
                            w2b[:, par * 512 + dmc * 128:par * 512 + (dmc + 1) * 128],
                            hq_sb[:, fc, tb * 512:(tb + 1) * 512],
                            start=(fc == 0), stop=(fc == 31)),
                            waits=[(wch[fc % 2], w2_ready[fc] if (dmc == 0 and tb == 0) else 0),
                                   (act, hbf_act[(T, fc)] if (dmc == 0 and tb == 0) else 0),
                                   (dve, (d_chain_end if dmh == 0 else oi_all)
                                        if (fc == 0 and dmc == 0 and tb == 0) else 0)],
                            incs=[(pe, 1)] if (dmc == 3 and tb == 1) else ())
                w2_mm_done[fc + 2] = pe.n
                if fc + 2 < 32:
                    w2_ready[fc + 2] = w2_dma(fc + 2, par)
            w_w2 = pe.n
            for k in range(8):
                tb, dmc = k // 4, k % 4
                dm = dmh * 4 + dmc
                bank = tb * 4 + dmc
                ois = t_oi[:, (k % 2) * 512:(k % 2 + 1) * 512]
                p.add("vector", lambda e, bank=bank, dm=dm, ois=ois: e.tensor_scalar(
                    ois, PW8[:, bank * 512:(bank + 1) * 512], 256.0, b2s[:, dm:dm + 1],
                    OP.mult, OP.add),
                    waits=[(pe, w_w2), (act, oi_slot_ff[k % 2])], incs=[(dve, 1)])
                d_oi = dve.n
                p.add("scalar", lambda e, dm=dm, tb=tb, ois=ois: e.activation(
                    out_fm[:, dm, tb * 512:(tb + 1) * 512], ois, AF.Identity,
                    scale=0.00390625),
                    waits=[(dve, d_oi)], incs=[(act, 1)])
                oi_slot_ff[k % 2] = act.n
            oi_all = dve.n
        a_of = act.n

        for j in range(8):
            t = T * 8 + j
            p.add("sync", lambda e, t=t: e.dma_start(
                out=x2s, in_=d_x2.rearrange("(k p) d -> p k d", p=128)[:, t:t + 1, :]),
                waits=[(dve, dve.n)], incs=[(dmx, 16)])
            w_x = dmx.n
            po = (j % 2) * 1024
            for c in range(8):
                p.add("tensor", lambda e, c=c, j=j, po=po: e.transpose(
                    PW8[:, po + c * 128:po + (c + 1) * 128],
                    out_fm[:, c, j * 128:(j + 1) * 128], id_f32),
                    waits=[(act, a_of if c == 0 else 0),
                           (dve, (oi_all if j == 0 else dve.n) if c == 0 else 0)],
                    incs=[(pe, 1)] if c == 7 else ())
            w_tr = pe.n
            p.add("vector", lambda e, po=po: e.tensor_copy(
                fout_tm.rearrange("p (c j2) -> p c j2", c=8),
                PW8[:, po:po + 1024].rearrange("p (c j2) -> p c j2", c=8)),
                waits=[(pe, w_tr), (dmo, dmo.n)], incs=[(dve, 1)])
            ln_stats(fout_tm, 0)
            p.add("vector", lambda e: e.tensor_scalar(fout_tm, fout_tm, stats[:, 0:1],
                                                      stats[:, 1:2], OP.subtract, OP.mult),
                  incs=[(dve, 1)])
            p.add("vector", lambda e: e.tensor_tensor(fout_tm, fout_tm, x2s, OP.add),
                  waits=[(dmx, w_x)], incs=[(dve, 1)])
            d_o = dve.n
            p.add("sync", lambda e, t=t: e.dma_start(
                out=d_out.rearrange("(k p) d -> p k d", p=128)[:, t:t + 1, :], in_=fout_tm),
                waits=[(dve, d_o)], incs=[(dmo, 16)])
        psclose()
    stFFN.close()

    p.add("sync", lambda e: None, waits=[(dmo, dmo.n)])

    with nc.Block() as block:
        p.emit(block)
    st.close()
    return nc


_NC_CACHE = None
_LAST_RES = None


def kernel(L0_attn_x, L0_attn_W_qkv, L0_attn_b_qkv, L0_attn_W_out, L0_attn_b_out,
           L0_ffn_w1, L0_ffn_b1, L0_ffn_w2, L0_ffn_b2):
    global _NC_CACHE
    x = np.ascontiguousarray(L0_attn_x, dtype=np.float32)
    B, SQ, D = x.shape
    if _NC_CACHE is None:
        _NC_CACHE = build_nc()
    nc = _NC_CACHE

    wqkv = np.ascontiguousarray(L0_attn_W_qkv, np.float32)
    bqkv = np.ascontiguousarray(L0_attn_b_qkv, np.float32).reshape(24, 128)
    wout = np.ascontiguousarray(L0_attn_W_out, np.float32)
    bout = np.ascontiguousarray(L0_attn_b_out, np.float32).reshape(8, 128)
    w1 = np.ascontiguousarray(L0_ffn_w1, np.float32)
    b1 = np.ascontiguousarray(L0_ffn_b1, np.float32).reshape(32, 128)
    w2 = np.ascontiguousarray(L0_ffn_w2, np.float32)
    b2 = np.ascontiguousarray(L0_ffn_b2, np.float32).reshape(8, 128)

    in_maps = []
    for core in range(8):
        b = core // 2
        half = core % 2
        own = x[b, half * 2048:(half + 1) * 2048, :]
        oth = x[b, (1 - half) * 2048:(2 - half) * 2048, :]
        po = np.empty((128, 2), np.float32)
        po[:, 0] = half * 2048.0
        po[:, 1] = (1 - half) * 2048.0
        in_maps.append({
            "xa": own, "xb": oth, "wqkv": wqkv, "bqkv": bqkv, "wout": wout,
            "bout": bout, "w1": w1, "b1": b1, "w2": w2, "b2": b2, "poff": po,
        })
    res = run_bass_kernel_spmd(nc, in_maps, core_ids=list(range(8)))
    global _LAST_RES
    _LAST_RES = res
    out = np.empty((B, SQ, D), np.float32)
    for core in range(8):
        b = core // 2
        half = core % 2
        out[b, half * 2048:(half + 1) * 2048, :] = res.results[core]["out"]
    return out
